# version 08
# baseline (speedup 1.0000x reference)
import math
import numpy as np
import ml_dtypes
import concourse.bass as bass
import concourse.mybir as mybir
from concourse.bass_utils import run_bass_kernel_spmd
from contextlib import ExitStack

F32 = mybir.dt.float32
BF16 = mybir.dt.bfloat16
FP16 = mybir.dt.float16
AF = mybir.ActivationFunctionType
ALU = mybir.AluOpType
AXX = mybir.AxisListType.X

D = 1024
T = 4096
NT_FULL = 32
NCH = 8
BF = ml_dtypes.bfloat16
EPS_LN = 1e-5
NEG = -30000.0

ENGS = ("sync", "scalar", "vector", "tensor", "gpsimd")


def own_qtiles(core):
    if core % 2 == 0:
        return list(range(0, 8)) + list(range(24, 32))
    return list(range(8, 24))


def own_row_slices(core):
    if core % 2 == 0:
        return [(0, 1024), (3072, 4096)]
    return [(1024, 3072)]


class Sem:
    __slots__ = ("h", "v", "name")

    def __init__(self, nc, name):
        self.h = nc.alloc_semaphore(name)
        self.v = 0
        self.name = name


class Lazy:
    __slots__ = ("fn",)

    def __init__(self, fn):
        self.fn = fn

    def __call__(self):
        return self.fn()


class EngQ:
    def __init__(self, auto_sem=None):
        self.ops = []
        self.auto_sem = auto_sem

    def call(self, method, *args, inc=None, incn=1, **kwargs):
        if inc is None and self.auto_sem is not None and method != "dma_start":
            inc = self.auto_sem
        if inc is not None:
            inc.v += incn
        self.ops.append((method, args, kwargs, inc, incn))
        if self.auto_sem is not None and inc is not None and method != "dma_start":
            self.ops.append(("wait_ge", (inc.h, inc.v), {}, None, 0))
        return inc.v if inc is not None else None

    def wait(self, sem, val=None):
        v = sem.v if val is None else val
        self.ops.append(("wait_ge", (sem.h, v), {}, None, 0))

    def dma(self, dst, src, sem, **kw):
        return self.call("dma_start", dst, src, inc=sem, incn=16, **kw)


class Prog:
    def __init__(self, name):
        self.nc = bass.Bass("TRN2", target_bir_lowering=False)
        self.ctx = ExitStack()
        self.name = name
        self.sbuf_bytes = 0
        self.prefix = ""
        self.q = {e: EngQ() for e in ENGS}
        self.q["vector"].auto_sem = Sem(self.nc, "VEauto")
        self.q["scalar"].auto_sem = Sem(self.nc, "SCauto")
        self.gsem = None

    def init_barrier_sems(self):
        self.gsem = {e: Sem(self.nc, f"G_{e}") for e in ENGS}

    def stage_begin(self, prefix):
        self.prefix = prefix + "_"
        self.alias_group = None
        self._stage_ctx = ExitStack()
        self._saved_bytes = self.sbuf_bytes
        self._saved_ctx, self.ctx = self.ctx, self._stage_ctx
        if self.gsem is not None:
            for e in ENGS:
                for e2 in ENGS:
                    self.q[e].wait(self.gsem[e2])

    def stage_end(self):
        if self.gsem is not None:
            for e in ENGS:
                self.q[e].call("nop", inc=self.gsem[e])
        self.ctx = self._saved_ctx
        self._stage_ctx.close()
        self.sbuf_bytes = self._saved_bytes
        self.prefix = ""

    def __getattr__(self, item):
        if item in ENGS:
            return self.q[item]
        raise AttributeError(item)

    def sem(self, name):
        if not hasattr(self, "_semcache"):
            self._semcache = {}
        if name not in self._semcache:
            self._semcache[name] = Sem(self.nc, name)
        return self._semcache[name]

    def sbuf(self, name, shape, dtype):
        self.sbuf_bytes += int(np.prod(shape[1:])) * mybir.dt.size(dtype)
        assert self.sbuf_bytes <= 206000, \
            f"{self.name}: SBUF over budget at {name}: {self.sbuf_bytes // 1024}KB/part"
        return self.ctx.enter_context(
            self.nc.sbuf_tensor(self.prefix + name, list(shape), dtype))

    def psum(self, name, shape, dtype):
        return self.ctx.enter_context(
            self.nc.psum_tensor(self.prefix + name, list(shape), dtype))

    def din(self, name, shape, dtype, no_alias=False):
        if getattr(self, "walls", None) is None:
            return self.nc.dram_tensor(self.prefix + name, list(shape), dtype,
                                       kind="ExternalInput").ap()
        group = (getattr(self, "alias_group", None) or self.prefix) \
            if not no_alias else self.prefix
        key = (group, name)
        if key in self.wall_cache:
            return self.wall_cache[key]
        wall, spec, pos = self.walls[dtype]
        n = int(np.prod(shape))
        off = pos[0]
        spec.append((group, name, tuple(shape), off))
        pos[0] = off + n
        ap = wall[0, off:off + n]
        if len(shape) == 2:
            ap = ap.rearrange("(r c) -> r c", c=shape[1])
        self.wall_cache[key] = ap
        return ap

    def dint(self, name, shape, dtype):
        if getattr(self, "walls", None) is None:
            return self.nc.dram_tensor(self.prefix + name, list(shape), dtype,
                                       kind="Internal").ap()
        wall, spec, pos = self.iwall[dtype]
        n = int(np.prod(shape))
        off = pos[0]
        pos[0] = off + n
        ap = wall[0, off:off + n]
        if len(shape) == 2:
            ap = ap.rearrange("(r c) -> r c", c=shape[1])
        return ap

    def init_walls(self, sizes):
        self.walls = {}
        self.iwall = {}
        self.wall_cache = {}
        self.alias_group = None
        for dt_, n in sizes.items():
            if isinstance(dt_, tuple):
                t = self.nc.dram_tensor(f"iwall_{mybir.dt.size(dt_[1])}",
                                        [1, n], dt_[1], kind="Internal").ap()
                self.iwall[dt_[1]] = (t, [], [0])
            else:
                t = self.nc.dram_tensor(f"wall_{mybir.dt.size(dt_)}{dt_}",
                                        [1, n], dt_, kind="ExternalInput").ap()
                self.walls[dt_] = (t, [], [0])

    def emit_and_finish(self):
        def replay(eng, ops):
            for method, args, kwargs, sem, incn in ops:
                if method == "__raw__":
                    args[0](eng)
                    continue
                args = [a() if isinstance(a, Lazy) else a for a in args]
                kwargs = {k: (v() if isinstance(v, Lazy) else v)
                          for k, v in kwargs.items()}
                ins = getattr(eng, method)(*args, **kwargs)
                if sem is not None:
                    ins.then_inc(sem.h, incn)

        with self.nc.Block() as blk:
            for e in ENGS:
                if self.q[e].ops:
                    getattr(blk, e)(lambda eng, ops=self.q[e].ops: replay(eng, ops))
        self.ctx.close()
        self.nc.finalize()
        return self.nc



def build_attention(cfg, p=None, io=None):
    H, dh = cfg["H"], cfg["dh"]
    Hm = H // 2
    P2 = 2 if dh == 64 else 1
    NP2 = Hm // P2
    VW = P2 * dh + (2 if P2 == 2 else 1)
    causal = cfg["causal"]
    rope = cfg.get("rope", False)
    eps2 = cfg.get("eps2")
    standalone = p is None
    if standalone:
        p = Prog(cfg["name"])
    nc = p.nc
    wdt = FP16 if cfg.get("w_f16") else F32
    scale = 1.0 / math.sqrt(dh)
    has_bias = cfg.get("beta_qkv") or cfg.get("bias_rows")
    DKV = D
    DQ = D // 2
    DP = D // 2

    xb = io["xb"] if io else p.din("xb", [T, D], F32)
    wkv_d = p.din("wkv", [D, DKV], wdt)
    wq_d = p.din("wq", [D, DQ], wdt)
    wp_d = p.din("wp", [DP, D], wdt)
    identf = p.din("identf", [128, 128], F32)
    gvec = p.din("gvec", [128, 8], F32) if cfg.get("gamma_fold") else None
    bvec = p.din("bvec", [128, 8], F32) if cfg.get("beta_qkv") else None
    bq_d = bkv_d = None
    if has_bias:
        bq_d = p.din("bq", [1, DQ], F32)
        bkv_d = p.din("bkv", [1, DKV], F32)
    cmask = p.din("cmask", [128, 128], F32) if causal else None
    if rope:
        rcos_d = p.din("rcos", [T, DQ // 2], F32)
        rsin_d = p.din("rsin", [T, DQ // 2], F32)
    if standalone:
        y_out = nc.dram_tensor("y", [T, D], F32, kind="ExternalOutput").ap()
        st_out = nc.dram_tensor("st_out", [128, NT_FULL * 6], F32,
                                kind="ExternalOutput").ap()
    kscr = p.dint("kscr", [128, NP2 * T], BF16)

    stats = (io.get("stats_buf") if io else None) or \
        p.sbuf("stats", [128, NT_FULL * 6], F32)
    xstage = p.sbuf("xstage", [128, 2 * D], F32)
    xnb = p.sbuf("xnb", [128, D], BF16)
    xnT = p.sbuf("xnT", [128, D], BF16)
    qkvtm = p.sbuf("qkvtm", [128, DKV + DQ], BF16)
    ktT = p.sbuf("ktT", [128, DP], BF16)
    Vb = p.sbuf("Vb", [128, NT_FULL * NP2 * VW], BF16)
    Qf = p.sbuf("Qf", [128, NP2 * T], BF16)
    kblk = p.sbuf("kblk", [128, T], BF16)
    Wres = p.sbuf("Wres", [128, NCH * (DKV + DQ)], BF16)
    Wpj = p.sbuf("Wpj", [128, (DP // 128) * D], BF16)
    Wstage = p.sbuf("Wstage", [128, 2 * 512], wdt)
    PTb = p.sbuf("PTb", [128, 4 * 512], BF16)
    ctmp = p.sbuf("ctmp", [128, 2 * 512], F32)
    ctxTM = p.sbuf("ctxTM", [128, 4 * (Hm * dh)], BF16)
    ctxFT = p.sbuf("ctxFT", [128, Hm * dh], BF16)
    ystage = p.sbuf("ystage", [128, D], F32)
    sid = p.sbuf("sid", [128, 128], F32)
    sidb = p.sbuf("sidb", [128, 128], BF16)
    onesf = p.sbuf("onesf", [1, 128], F32)
    biasq = p.sbuf("biasq", [1, DQ], F32)
    biaskv = p.sbuf("biaskv", [1, DKV], F32)
    gcol = p.sbuf("gcol", [128, 8], F32)
    bcolf = p.sbuf("bcolf", [128, 8], F32)
    bcol = p.sbuf("bcol", [128, 8], BF16)
    kn2 = p.sbuf("kn2", [128, Hm], F32)
    qn2 = p.sbuf("qn2", [128, Hm], F32)
    nrm = p.sbuf("nrm", [128, 16], F32)
    Mb = p.sbuf("Mb", [128, Hm], F32)
    mrow = p.sbuf("mrow", [1, 16], F32)
    smask = p.sbuf("smask", [128, 128], F32) if causal else None
    rc = p.sbuf("rc", [128, DQ // 2], F32) if rope else None
    rs = p.sbuf("rs", [128, DQ // 2], F32) if rope else None
    qkr = p.sbuf("qkr", [128, DKV + DQ], BF16) if rope else None

    ps_g = [p.psum(f"psg{i}", [128, 512], F32) for i in range(4)]
    ps_ctx = [p.psum(f"psc{i}", [128, 512], F32) for i in range(2)]
    ps_tr = p.psum("ps_tr", [128, 1024], BF16)
    ps_trb = p.psum("ps_trb", [128, 1024], BF16)

    LD = p.sem("LD")
    WL = p.sem("WL")
    WLb = p.sem("WLb")
    KSM = p.sem("KSM")
    GP = p.sem("GP")
    SC = p.sem("SC")
    VE = p.sem("VE")
    TT = p.sem("TT")
    STR = p.sem("STR")

    if eps2 is None:
        va, vb_ = 1.0, EPS_LN
    else:
        va, vb_ = 1.0 + eps2, EPS_LN * eps2

    p.sync.dma(sid[:], identf, LD)
    if causal:
        p.sync.dma(smask[:], cmask, LD)
    if cfg.get("gamma_fold"):
        p.sync.dma(gcol[:], gvec, LD)
    if has_bias:
        p.sync.dma(biasq[:, :], bq_d, LD)
        p.sync.dma(biaskv[:, :], bkv_d, LD)
    ld_consts = LD.v
    p.vector.wait(LD, ld_consts)
    p.vector.call("tensor_copy", sidb[:], sid[:], inc=VE)

    p.gpsimd.call("memset", onesf[:], 1.0)
    p.gpsimd.call("memset", Vb[:], 1.0)
    p.gpsimd.call("memset", kn2[:], 0.0)
    p.gpsimd.call("memset", qn2[:], 0.0, inc=GP)
    CB = p.sem("CB")
    if cfg.get("beta_qkv"):
        p.gpsimd.dma(bcolf[:], bvec, CB)
        p.vector.wait(CB, CB.v)
        p.vector.wait(LD, ld_consts)
        p.vector.call("reciprocal", nrm[:, 0:8], gcol[:], inc=VE)
        p.vector.wait(VE, VE.v)
        p.vector.call("tensor_mul", bcol[:], bcolf[:], nrm[:, 0:8], inc=VE)
    cb_consts = CB.v
    bcol_v = VE.v

    CW = DKV + DQ
    wl_cast_vals = []
    slot = [0]

    def stream_weights(wd, nchunks, dout, dst_buf, dst_cw, roff, fold):
        for c in range(nchunks):
            for s in range(dout // 512):
                sl0 = (slot[0] % 2) * 512
                wsem = WL if slot[0] % 2 == 0 else WLb
                if slot[0] >= 2:
                    p.sync.wait(VE, wl_cast_vals[slot[0] - 2])
                p.sync.dma(Wstage[:, sl0:sl0 + 512],
                           wd[c * 128:(c + 1) * 128, s * 512:(s + 1) * 512], wsem)
                p.vector.wait(wsem)
                dst = dst_buf[:, c * dst_cw + roff + s * 512:
                              c * dst_cw + roff + (s + 1) * 512]
                if fold:
                    p.vector.wait(LD, ld_consts)
                    p.vector.call("tensor_scalar", dst, Wstage[:, sl0:sl0 + 512],
                                  gcol[:, c:c + 1], None, ALU.mult, inc=VE)
                else:
                    p.vector.call("tensor_copy", dst, Wstage[:, sl0:sl0 + 512],
                                  inc=VE)
                wl_cast_vals.append(VE.v)
                slot[0] += 1

    fold_qkv = bool(cfg.get("gamma_fold"))
    stream_weights(wkv_d, NCH, DKV, Wres, CW, 0, fold_qkv)
    stream_weights(wq_d, NCH, DQ, Wres, CW, DKV, fold_qkv)
    stream_weights(wp_d, DP // 128, D, Wpj, D, 0, False)
    w_ready = VE.v

    if cfg.get("beta_qkv"):
        p.tensor.wait(VE, max(w_ready, bcol_v))
        p.tensor.wait(GP, GP.v)
        p.tensor.wait(CB, cb_consts)
        for (brow, roff, dout) in ((biaskv, 0, DKV), (biasq, DKV, DQ)):
            for s in range(dout // 512):
                p.tensor.wait(VE, VE.v)
                for c in range(NCH):
                    p.tensor.call("matmul", ps_ctx[0][0:1, 0:512], bcol[:, c:c + 1],
                                  Wres[:, c * CW + roff + s * 512:
                                       c * CW + roff + (s + 1) * 512],
                                  start=(c == 0), stop=(c == NCH - 1),
                                  inc=(TT if c == NCH - 1 else None))
                p.vector.wait(TT)
                p.vector.wait(LD, ld_consts)
                p.vector.call("tensor_add", brow[:, s * 512:(s + 1) * 512],
                              brow[:, s * 512:(s + 1) * 512],
                              ps_ctx[0][0:1, 0:512], inc=VE)
    bias_v = VE.v

    NS = (DKV + DQ) // 512
    tile_consumed = [0] * NT_FULL
    p.tensor.wait(VE, max(w_ready, bias_v))
    p.tensor.wait(GP, GP.v)
    for t in range(NT_FULL):
        gt = t
        if t >= 1:
            p.sync.wait(VE, tile_consumed[t - 1])
        x_ld = p.sync.dma(xstage[:, (t % 2) * D:(t % 2 + 1) * D],
                          xb[gt * 128:(gt + 1) * 128, :], LD)
        if rope:
            p.sync.dma(rc[:], rcos_d[gt * 128:(gt + 1) * 128, :], LD)
            p.sync.dma(rs[:], rsin_d[gt * 128:(gt + 1) * 128, :], LD)
            x_ld = LD.v
        st6 = stats[:, gt * 6:(gt + 1) * 6]
        mu, s_ap, r1 = st6[:, 0:1], st6[:, 1:2], st6[:, 2:3]
        w_ap, a_ap, b_ap = st6[:, 3:4], st6[:, 4:5], st6[:, 5:6]
        xt = xstage[:, (t % 2) * D:(t % 2 + 1) * D]
        p.scalar.wait(LD, x_ld)
        p.scalar.wait(TT, TT.v)
        p.scalar.call("activation", xnb[:], xt, AF.Square, accum_out=b_ap,
                      inc=SC)
        sq_v = SC.v
        p.vector.wait(LD, x_ld)
        p.vector.call("reduce_sum", a_ap, xt, axis=AXX, inc=VE)
        p.vector.wait(VE)
        p.vector.call("tensor_scalar", mu, a_ap, 1.0 / D, None, ALU.mult)
        p.vector.wait(SC, sq_v)
        p.vector.call("tensor_scalar", w_ap, b_ap, 1.0 / D, None, ALU.mult)
        p.vector.call("tensor_scalar", a_ap, mu, mu, None, ALU.mult)
        p.vector.call("tensor_sub", w_ap, w_ap, a_ap, inc=VE)
        p.vector.wait(VE)
        p.vector.call("tensor_scalar", b_ap, w_ap, va, vb_, ALU.mult, ALU.add,
                      inc=VE)
        p.scalar.wait(VE, VE.v)
        p.scalar.call("activation", a_ap, b_ap, AF.Ln)
        p.scalar.call("activation", s_ap, a_ap, AF.Exp, scale=-0.5, inc=SC)
        p.vector.wait(SC, SC.v)
        p.vector.call("tensor_scalar", b_ap, w_ap, 1.0, EPS_LN,
                      ALU.mult, ALU.add, inc=VE)
        p.scalar.wait(VE, VE.v)
        p.scalar.call("activation", a_ap, b_ap, AF.Ln)
        p.scalar.call("activation", r1, a_ap, AF.Exp, scale=-0.5, inc=SC)
        s_v = SC.v
        p.vector.wait(SC, s_v)
        p.vector.call("tensor_scalar", xnb[:], xt, mu, s_ap,
                      ALU.subtract, ALU.mult, inc=VE)
        xn_v = VE.v
        p.tensor.wait(VE, xn_v)
        p.tensor.wait(SC, SC.v)
        for c in range(NCH):
            p.tensor.call("transpose", ps_tr[:, c * 128:(c + 1) * 128],
                          xnb[:, c * 128:(c + 1) * 128], sidb[0:128, 0:128],
                          inc=(TT if c == NCH - 1 else None))
        p.scalar.wait(TT)
        p.scalar.call("copy", xnT[:], ps_tr[:], inc=SC)
        ev_v = SC.v
        p.tensor.wait(SC, ev_v)
        for s in range(NS):
            if has_bias:
                p.tensor.wait(LD, ld_consts)
                bsrc = biaskv[:, s * 512:(s + 1) * 512] if s < 2 else \
                    biasq[:, 0:512]
                p.tensor.call("matmul", ps_g[s][:], onesf[:, 0:128], bsrc,
                              start=True, stop=False)
            for c in range(NCH):
                p.tensor.call("matmul", ps_g[s][:],
                              xnT[:, c * 128:(c + 1) * 128],
                              Wres[:, c * CW + s * 512:c * CW + (s + 1) * 512],
                              start=(c == 0 and not has_bias),
                              stop=(c == NCH - 1),
                              inc=(TT if c == NCH - 1 else None))
        qkv_done = TT.v
        p.vector.wait(TT, qkv_done)
        p.vector.call("tensor_copy", qkvtm[:, 0:512], ps_g[0][:], inc=VE)
        p.vector.call("tensor_scalar", qkvtm[:, 1024:1536], ps_g[2][:],
                      scale, None, ALU.mult, inc=VE)
        k_ev = VE.v
        vview = Vb[:].rearrange("p (t n w) -> p t n w", t=NT_FULL, n=NP2)
        if P2 == 2:
            dst3 = Vb[:].rearrange("p (t n o w) -> p t n o w",
                                   t=NT_FULL, n=NP2, o=2)[:, gt, :, :, 0:dh]
            srcv = ps_g[1][:].rearrange("p (n o w) -> p n o w", n=NP2, o=2)
        else:
            dst3 = vview[:, gt, :, 0:dh]
            srcv = ps_g[1][:].rearrange("p (n w) -> p n w", n=NP2)
        p.scalar.wait(TT, qkv_done)
        p.scalar.call("copy", dst3, srcv, inc=SC)
        ksrc_off = 0
        qsrc_off = 1024
        src_buf = qkvtm
        if rope:
            p.vector.wait(VE, k_ev)
            for (off, dst_off) in ((0, 0), (1024, 1024)):
                k3 = qkvtm[:, off:off + 512].rearrange("p (h d) -> p h d", h=Hm)
                kd3 = qkr[:, dst_off:dst_off + 512].rearrange("p (h d) -> p h d", h=Hm)
                c3 = rc[:].rearrange("p (h d) -> p h d", h=Hm)
                s3 = rs[:].rearrange("p (h d) -> p h d", h=Hm)
                t1 = ctmp[:, 0:256].rearrange("p (h d) -> p h d", h=Hm)
                t2 = ctmp[:, 256:512].rearrange("p (h d) -> p h d", h=Hm)
                p.vector.call("tensor_mul", t1, c3, k3[:, :, 0:dh // 2])
                p.vector.call("tensor_mul", t2, s3, k3[:, :, dh // 2:dh])
                p.vector.call("tensor_sub", kd3[:, :, 0:dh // 2], t1, t2)
                p.vector.call("tensor_mul", t1, s3, k3[:, :, 0:dh // 2])
                p.vector.call("tensor_mul", t2, c3, k3[:, :, dh // 2:dh])
                p.vector.call("tensor_add", kd3[:, :, dh // 2:dh], t1, t2,
                              inc=VE)
            src_buf = qkr
        for (off, acc) in ((ksrc_off, kn2), (qsrc_off, qn2)):
            sl = src_buf[:, off:off + 512]
            p.scalar.wait(VE, VE.v)
            p.scalar.call("activation", ctmp[:, 0:512], sl, AF.Square, inc=SC)
            p.vector.wait(SC, SC.v)
            p.vector.call("reduce_sum", nrm[:, 0:Hm],
                          ctmp[:, 0:512].rearrange("p (a b) -> p a b", b=dh),
                          axis=AXX, inc=VE)
            p.vector.wait(VE)
            p.vector.call("tensor_max", acc[:, 0:Hm], acc[:, 0:Hm],
                          nrm[:, 0:Hm], inc=VE)
        p.tensor.wait(VE, VE.v)
        p.tensor.wait(KSM, KSM.v)
        for c in range(4):
            p.tensor.call("transpose", ps_trb[:, c * 128:(c + 1) * 128],
                          src_buf[:, ksrc_off + c * 128:ksrc_off + (c + 1) * 128],
                          sidb[0:128, 0:128],
                          inc=(TT if c == 3 else None))
        p.scalar.wait(TT)
        p.scalar.wait(KSM, KSM.v)
        p.scalar.call("copy", ktT[:], ps_trb[:, 0:512], inc=SC)
        kev = SC.v
        p.gpsimd.wait(SC, kev)
        p.gpsimd.dma(kscr.rearrange("p (c t) -> p c t", c=NP2)
                     [:, :, gt * 128:(gt + 1) * 128],
                     ktT[:].rearrange("p (c t) -> p c t", c=NP2), KSM)
        p.tensor.wait(SC, SC.v)
        for c in range(4):
            p.tensor.call("transpose", ps_trb[:, c * 128:(c + 1) * 128],
                          src_buf[:, qsrc_off + c * 128:qsrc_off + (c + 1) * 128],
                          sidb[0:128, 0:128],
                          inc=(TT if c == 3 else None))
        p.scalar.wait(TT)
        p.scalar.call("copy",
                      Qf[:].rearrange("p (c t) -> p c t", c=NP2)
                      [:, :, gt * 128:(gt + 1) * 128],
                      ps_trb[:, 0:512].rearrange("p (c t) -> p c t", c=NP2),
                      inc=SC)
        tile_consumed[t] = VE.v

    p.tensor.wait(VE, VE.v)
    p.tensor.call("transpose", ps_ctx[0][0:Hm, 0:128], kn2[:, 0:Hm],
                  sid[0:128, 0:128])
    p.tensor.call("transpose", ps_ctx[1][0:Hm, 0:128], qn2[:, 0:Hm],
                  sid[0:128, 0:128], inc=TT)
    p.vector.wait(TT)
    p.vector.call("reduce_max", nrm[0:Hm, 0:1], ps_ctx[0][0:Hm, 0:128],
                  axis=AXX)
    p.vector.call("reduce_max", nrm[0:Hm, 1:2], ps_ctx[1][0:Hm, 0:128],
                  axis=AXX, inc=VE)
    p.vector.wait(VE)
    p.vector.call("tensor_mul", nrm[0:Hm, 2:3], nrm[0:Hm, 0:1],
                  nrm[0:Hm, 1:2], inc=VE)
    p.scalar.wait(VE, VE.v)
    p.scalar.call("activation", nrm[0:Hm, 3:4], nrm[0:Hm, 2:3], AF.Sqrt,
                  inc=SC)
    p.vector.wait(SC, SC.v)
    p.vector.call("tensor_scalar", nrm[0:Hm, 4:5], nrm[0:Hm, 3:4], -1.0, None,
                  ALU.mult, inc=VE)
    p.tensor.wait(VE, VE.v)
    p.tensor.call("transpose", ps_ctx[0][0:1, 0:Hm], nrm[0:Hm, 4:5],
                  sid[0:Hm, 0:Hm], inc=TT)
    p.scalar.wait(TT)
    p.scalar.call("copy", mrow[:, 0:Hm], ps_ctx[0][0:1, 0:Hm], inc=SC)
    p.tensor.wait(SC, SC.v)
    p.tensor.call("matmul", ps_ctx[1][:, 0:Hm], onesf[:, 0:128], mrow[:, 0:Hm],
                  start=True, stop=True, inc=TT)
    p.scalar.wait(TT)
    p.scalar.call("copy", Mb[:, 0:Hm], ps_ctx[1][:, 0:Hm], inc=SC)
    bounds_done = SC.v
    if standalone:
        p.sync.wait(SC, SC.v)
        p.sync.wait(VE, VE.v)
        p.sync.dma(st_out, stats[:], STR)

    for qq in range(8):
        tiles = [qq * 4 + i for i in range(4)]
        ktmax = (max(tiles) + 1) if causal else NT_FULL
        for hp in range(NP2):
            p.gpsimd.wait(TT, TT.v)
            p.gpsimd.wait(KSM, KSM.v)
            p.gpsimd.dma(kblk[:], kscr[:, hp * T:(hp + 1) * T], KSM)
            kblk_v = KSM.v
            p.tensor.wait(KSM, kblk_v)
            p.tensor.wait(SC, SC.v)
            p.tensor.wait(VE, VE.v)
            for h01 in range(P2):
                h = hp * P2 + h01
                exp_done = {}
                vv = Vb[:].rearrange("p (t n w) -> p t n w", t=NT_FULL, n=NP2)

                def emit_av(kt, ppk, first, last):
                    pt_src = PTb[:, (ppk * 2 + h01) * 512:
                                 (ppk * 2 + h01 + 1) * 512]
                    p.tensor.wait(SC, exp_done[kt])
                    if P2 == 2:
                        p.tensor.call("matmul", ps_ctx[h01][0:65, :],
                                      vv[:, kt, hp, h01 * 65:(h01 + 1) * 65],
                                      pt_src, start=first, stop=last,
                                      inc=(TT if last else None))
                    else:
                        p.tensor.call("matmul", ps_ctx[0][0:64, :],
                                      vv[:, kt, hp, 0:64], pt_src,
                                      start=first, stop=last)
                        p.tensor.call("matmul", ps_ctx[1][0:65, :],
                                      vv[:, kt, hp, 64:129], pt_src,
                                      start=first, stop=last,
                                      inc=(TT if last else None))

                for kt in range(ktmax):
                    ppk = kt % 2
                    stp = ps_g[ppk * 2 + h01]
                    if P2 == 2:
                        lhsT = kblk[h01 * 64:(h01 + 1) * 64,
                                    kt * 128:(kt + 1) * 128]
                        rhs = Qf[h01 * 64:(h01 + 1) * 64,
                                 hp * T + qq * 512:hp * T + (qq + 1) * 512]
                    else:
                        lhsT = kblk[:, kt * 128:(kt + 1) * 128]
                        rhs = Qf[:, hp * T + qq * 512:hp * T + (qq + 1) * 512]
                    if kt >= 2:
                        p.tensor.wait(SC, exp_done[kt - 2])
                    p.tensor.call("matmul", stp[:], lhsT, rhs,
                                  start=True, stop=True, inc=TT)
                    st_v = TT.v
                    mk_v = None
                    if causal:
                        did = False
                        for i, g in enumerate(tiles):
                            if g == kt or g < kt:
                                if not did:
                                    p.vector.wait(TT, st_v)
                                if g == kt:
                                    p.vector.call("tensor_add",
                                                  stp[:, i * 128:(i + 1) * 128],
                                                  stp[:, i * 128:(i + 1) * 128],
                                                  smask[:], inc=VE)
                                else:
                                    p.vector.call("memset",
                                                  stp[:, i * 128:(i + 1) * 128],
                                                  NEG, inc=VE)
                                did = True
                        mk_v = VE.v if did else None
                    p.scalar.wait(TT, st_v)
                    if mk_v is not None:
                        p.scalar.wait(VE, mk_v)
                    pt_dst = PTb[:, (ppk * 2 + h01) * 512:(ppk * 2 + h01 + 1) * 512]
                    p.scalar.call("activation", pt_dst, stp[:], AF.Exp,
                                  bias=Mb[:, h:h + 1], inc=SC)
                    exp_done[kt] = SC.v
                    if kt >= 1:
                        emit_av(kt - 1, (kt - 1) % 2, kt - 1 == 0, False)
                emit_av(ktmax - 1, (ktmax - 1) % 2, ktmax == 1, True)
            av_v = TT.v
            p.scalar.wait(TT, av_v)
            if P2 == 2:
                p.scalar.call("copy", ctmp[0:65, 0:512], ps_ctx[0][0:65, :])
                p.scalar.call("copy", ctmp[0:65, 512:1024], ps_ctx[1][0:65, :],
                              inc=SC)
            else:
                p.scalar.call("copy", ctmp[0:64, 0:512], ps_ctx[0][0:64, :])
                p.scalar.call("copy", ctmp[0:65, 512:1024], ps_ctx[1][0:65, :],
                              inc=SC)
            ce_v = SC.v
            p.tensor.wait(SC, ce_v)
            for h01 in range(2):
                nr = 65 if (P2 == 2 or h01 == 1) else 64
                for i in range(4):
                    p.tensor.call(
                        "transpose",
                        ps_ctx[h01][:, 0:288].rearrange("p (a b) -> p a b", a=4)
                        [:, i, 0:nr],
                        ctmp[0:nr, h01 * 512 + i * 128:h01 * 512 + (i + 1) * 128],
                        sid[0:nr, 0:nr],
                        inc=(TT if i == 3 else None))
            p.vector.wait(TT, TT.v)
            if P2 == 2:
                for h01 in range(2):
                    cc = ps_ctx[h01][:, 0:288].rearrange("p (a b) -> p a b", a=4)
                    for i in range(4):
                        p.vector.call("reciprocal", nrm[:, 8:9], cc[:, i, 64:65],
                                      inc=VE)
                        p.vector.wait(VE, VE.v)
                        dt_ = ctxTM[:, i * (Hm * dh) + (hp * 2 + h01) * dh:
                                    i * (Hm * dh) + (hp * 2 + h01) * dh + dh]
                        p.vector.call("tensor_scalar", dt_, cc[:, i, 0:dh],
                                      nrm[:, 8:9], None, ALU.mult, inc=VE)
            else:
                c0 = ps_ctx[0][:, 0:288].rearrange("p (a b) -> p a b", a=4)
                c1 = ps_ctx[1][:, 0:288].rearrange("p (a b) -> p a b", a=4)
                for i in range(4):
                    p.vector.call("reciprocal", nrm[:, 8:9], c1[:, i, 64:65],
                                  inc=VE)
                    p.vector.wait(VE, VE.v)
                    dt_ = ctxTM[:, i * (Hm * dh) + hp * dh:
                                i * (Hm * dh) + hp * dh + dh]
                    p.vector.call("tensor_scalar", dt_[:, 0:64], c0[:, i, 0:64],
                                  nrm[:, 8:9], None, ALU.mult)
                    p.vector.call("tensor_scalar", dt_[:, 64:128], c1[:, i, 0:64],
                                  nrm[:, 8:9], None, ALU.mult, inc=VE)
        ctx_done = VE.v

        NCP = DP // 128
        for i in range(4):
            gt = tiles[i]
            p.tensor.wait(VE, ctx_done)
            p.tensor.wait(SC, SC.v)
            for c in range(NCP):
                p.tensor.call("transpose", ps_tr[:, c * 128:(c + 1) * 128],
                              ctxTM[:, i * (Hm * dh) + c * 128:
                                    i * (Hm * dh) + (c + 1) * 128],
                              sidb[0:128, 0:128],
                              inc=(TT if c == NCP - 1 else None))
            p.scalar.wait(TT)
            p.scalar.call("copy", ctxFT[:], ps_tr[:, 0:DP], inc=SC)
            ev_v = SC.v
            p.tensor.wait(SC, ev_v)
            p.tensor.wait(VE, VE.v)
            for s in range(2):
                for c in range(NCP):
                    p.tensor.call("matmul", ps_g[s][:],
                                  ctxFT[:, c * 128:(c + 1) * 128],
                                  Wpj[:, c * D + s * 512:c * D + (s + 1) * 512],
                                  start=(c == 0), stop=(c == NCP - 1),
                                  inc=(TT if c == NCP - 1 else None))
            pj_v = TT.v
            p.vector.wait(TT, pj_v)
            p.vector.wait(STR, STR.v)
            p.vector.call("tensor_copy", ystage[:, 0:512], ps_g[0][:])
            p.vector.call("tensor_copy", ystage[:, 512:1024], ps_g[1][:],
                          inc=VE)
            y_v = VE.v
            p.sync.wait(VE, y_v)
            if io:
                io["ywrite"](gt, ystage, STR)
            else:
                p.sync.dma(y_out[gt * 128:(gt + 1) * 128, :], ystage[:], STR)
    p.sync.wait(STR, STR.v)
    p.sync.wait(KSM, KSM.v)
    return p



def build_ffn(cfg, p=None, io=None):
    standalone = p is None
    if standalone:
        p = Prog(cfg["name"])
    nc = p.nc
    NT = 16
    HH = 1024
    rms_eps = cfg.get("rms_eps")
    shared = cfg.get("shared", False)
    moe_bias = cfg.get("moe_bias", False)
    combine = cfg.get("combine")
    l0mode = cfg["router"] == "topk_softmax"

    xo_d = None if io else p.din("xo", [2048, D], F32)
    pA_d = pB_d = bcmb_d = None
    if combine and not io:
        pA_d = p.din("pA", [2048, D], F32)
        pB_d = p.din("pB", [2048, D], F32)
    if cfg.get("bcmb"):
        bcmb_d = p.din("bcmb", [1, D], F32, no_alias=True)
    wrt_d = p.din("wrt", [D, 8], F32)
    brt_d = p.din("brt", [1, 8], F32) if not l0mode else None
    gated = cfg.get("gated", True)
    upw = 2 * HH if gated else HH
    wup_d = p.din("wup", [16 * D, upw], F32)
    wdn_d = p.din("wdn", [16 * HH, D], F32)
    gvec = p.din("gvec", [128, 8], F32) if cfg.get("nfold") else None
    if shared:
        wsu_d = p.din("wsu", [2 * D, HH], F32)
        wsd_d = p.din("wsd", [2 * HH, D], F32)
        bs1_d = p.din("bs1", [2, HH], F32)
        bs2_d = p.din("bs2", [1, D], F32)
    if moe_bias:
        rb1_d = p.din("rb1", [16, HH], F32)
        rb2_d = p.din("rb2", [8, D], F32)
    identf = p.din("identf", [128, 128], F32)
    iota_d = p.din("iota", [1, 8], F32)
    if standalone:
        y_out = nc.dram_tensor("y", [2048, D], F32, kind="ExternalOutput").ap()
    ybscr = p.dint("ybscr", [2048, D], F32)

    stats = p.sbuf("stats", [128, NT * 6], F32)
    xstage = p.sbuf("xstage", [128, 2 * D], F32)
    pstage = p.sbuf("pstage", [128, D], F32) if combine else None
    xnb = p.sbuf("xnb", [128, D], F32)
    xnT = p.sbuf("xnT", [128, NT * D], BF16)
    hT = p.sbuf("hT", [128, NT * HH], BF16)
    oa = [p.sbuf("oa0", [128, NT * 512], F32),
          p.sbuf("oa1", [128, NT * 512], F32)]
    Wsl = p.sbuf("Wsl", [128, NCH * 2048], BF16)
    Wstage = p.sbuf("Wstage", [128, 8], F32)
    hstage = p.sbuf("hstage", [128, HH], BF16)
    gstage = p.sbuf("gstage", [128, HH], BF16)
    sid = p.sbuf("sid", [128, 128], F32)
    sidb = p.sbuf("sidb", [128, 128], BF16)
    onesf = p.sbuf("onesf", [1, 128], F32)
    iota_b = p.sbuf("iota_b", [128, 8], F32)
    rsc = p.sbuf("rsc", [128, 24], F32)
    wsel = p.sbuf("wsel", [128, NT * 8], F32)
    has_b = cfg.get("shared") or cfg.get("moe_bias")
    brow = p.sbuf("brow", [1, (2 * D) if has_b else D], F32)
    b2row = p.sbuf("b2row", [1, D], F32) if has_b else None
    gcol = p.sbuf("gcol", [128, 8], F32)
    wrtb = p.sbuf("wrtb", [128, 8 * 8], BF16)
    ystage = p.sbuf("ystage", [128, D], F32)
    c16 = p.sbuf("c16", [128, D], FP16) if combine == "fp16" else None

    ps_g = [p.psum(f"psg{i}", [128, 512], F32) for i in range(4)]
    ps_r = p.psum("ps_r", [128, 512], F32)
    ps_tr = p.psum("ps_tr", [128, 1024], BF16)
    ps_trf = p.psum("ps_trf", [128, 1024], F32)

    LD = p.sem("LD")
    WL = p.sem("WL")
    WLb = p.sem("WLb")
    SC = p.sem("SC")
    VE = p.sem("VE")
    TT = p.sem("TT")
    STR = p.sem("STR")
    GP = p.sem("GP")
    CB = p.sem("CB")

    p.sync.dma(sid[:], identf, LD)
    p.sync.dma(iota_b[0:1, :], iota_d, LD)
    if cfg.get("nfold"):
        p.sync.dma(gcol[:], gvec, LD)
    if cfg.get("bcmb"):
        p.sync.dma(brow[:, 0:D], bcmb_d, LD)
    if not l0mode:
        p.sync.dma(rsc[0:1, 16:24], brt_d, LD)
    ld_consts = LD.v
    p.vector.wait(LD, ld_consts)
    p.vector.call("tensor_copy", sidb[:], sid[:], inc=VE)
    p.gpsimd.call("memset", onesf[:], 1.0)
    p.gpsimd.call("memset", oa[0][:], 0.0)
    p.gpsimd.call("memset", oa[1][:], 0.0, inc=GP)
    p.tensor.wait(LD, ld_consts)
    p.tensor.wait(GP, GP.v)
    p.tensor.call("matmul", ps_r[:, 0:8], onesf[:, 0:128], iota_b[0:1, 0:8],
                  start=True, stop=True, inc=TT)
    p.scalar.wait(TT)
    p.scalar.call("copy", iota_b[:, 0:8], ps_r[:, 0:8], inc=SC)
    if not l0mode:
        p.tensor.wait(SC, SC.v)
        p.tensor.call("matmul", ps_r[:, 8:16], onesf[:, 0:128], rsc[0:1, 16:24],
                      start=True, stop=True, inc=TT)
        p.scalar.wait(TT)
        p.scalar.call("copy", rsc[:, 16:24], ps_r[:, 8:16], inc=SC)
    iota_v = SC.v

    first_stream = [True]

    def stream_w(wd, r0, c0, nch, dout, dst, dcw, roff):
        if first_stream[0]:
            first_stream[0] = False
        else:
            p.gpsimd.wait(TT, TT.v)
        for c in range(nch):
            p.gpsimd.dma(dst[:, c * dcw + roff:c * dcw + roff + dout],
                         wd[r0 + c * 128:r0 + (c + 1) * 128, c0:c0 + dout],
                         CB)
        return CB.v

    for c in range(NCH):
        if c > 0:
            p.sync.wait(VE, VE.v)
        p.sync.dma(Wstage[:, 0:8], wrt_d[c * 128:(c + 1) * 128, :], WL)
        p.vector.wait(WL)
        p.vector.call("tensor_copy", wrtb[:, c * 8:(c + 1) * 8],
                      Wstage[:, 0:8], inc=VE)
    wrt_v = VE.v

    eps1 = EPS_LN
    if rms_eps is None:
        va, vb_ = 1.0, eps1
    else:
        va, vb_ = 1.0 + rms_eps, eps1 * rms_eps
    tile_done = [0] * NT
    for t in range(NT):
        if t >= 1:
            p.sync.wait(VE, tile_done[t - 1])
        if io:
            xld = io["load_xo"](xstage[:, (t % 2) * D:(t % 2 + 1) * D], t, LD)
        else:
            xld = p.sync.dma(xstage[:, (t % 2) * D:(t % 2 + 1) * D],
                             xo_d[t * 128:(t + 1) * 128, :], LD)
        xt = xstage[:, (t % 2) * D:(t % 2 + 1) * D]
        if combine:
            p.sync.wait(STR, STR.v)
            if io:
                io["load_part"](pstage[:, 0:D], 0, t, LD)
                xld = io["load_part"](ystage[:, 0:D], 1, t, LD)
            else:
                p.sync.dma(pstage[:, 0:D], pA_d[t * 128:(t + 1) * 128, :], LD)
                xld = p.sync.dma(ystage[:, 0:D],
                                 pB_d[t * 128:(t + 1) * 128, :], LD)
            p.vector.wait(LD, xld)
            p.vector.call("tensor_add", pstage[:, 0:D], pstage[:, 0:D],
                          ystage[:, 0:D], inc=VE)
            if cfg.get("bcmb"):
                p.tensor.wait(VE, VE.v)
                p.tensor.wait(SC, SC.v)
                p.tensor.call("matmul", ps_g[3][:], onesf[:, 0:128],
                              brow[:, 0:512], start=True, stop=True)
                p.tensor.call("matmul", ps_g[2][:], onesf[:, 0:128],
                              brow[:, 512:1024], start=True, stop=True, inc=TT)
                p.vector.wait(TT, TT.v)
                p.vector.call("tensor_add", pstage[:, 0:512], pstage[:, 0:512],
                              ps_g[3][:])
                p.vector.call("tensor_add", pstage[:, 512:1024],
                              pstage[:, 512:1024], ps_g[2][:], inc=VE)
            if combine == "fp16":
                p.vector.call("tensor_copy", c16[:], pstage[:, 0:D], inc=VE)
                p.vector.call("tensor_add", xt, xt, c16[:], inc=VE)
            else:
                p.vector.call("tensor_add", xt, xt, pstage[:, 0:D], inc=VE)
        st6 = stats[:, t * 6:(t + 1) * 6]
        mu, s_ap, r1 = st6[:, 0:1], st6[:, 1:2], st6[:, 2:3]
        w_ap, a_ap, b_ap = st6[:, 3:4], st6[:, 4:5], st6[:, 5:6]
        p.scalar.wait(LD, xld)
        p.scalar.wait(VE, VE.v)
        p.scalar.wait(TT, TT.v)
        p.scalar.call("activation", xnb[:], xt, AF.Square, accum_out=b_ap,
                      inc=SC)
        p.vector.wait(SC, SC.v)
        p.vector.call("reduce_sum", a_ap, xt, axis=AXX, inc=VE)
        p.vector.wait(VE)
        p.vector.call("tensor_scalar", mu, a_ap, 1.0 / D, None, ALU.mult)
        p.vector.call("tensor_scalar", w_ap, b_ap, 1.0 / D, None, ALU.mult)
        p.vector.call("tensor_scalar", a_ap, mu, mu, None, ALU.mult)
        p.vector.call("tensor_sub", w_ap, w_ap, a_ap, inc=VE)
        p.vector.call("tensor_scalar", b_ap, w_ap, va, vb_, ALU.mult, ALU.add,
                      inc=VE)
        p.scalar.wait(VE, VE.v)
        p.scalar.call("activation", a_ap, b_ap, AF.Ln)
        p.scalar.call("activation", s_ap, a_ap, AF.Exp, scale=-0.5, inc=SC)
        p.vector.wait(SC, SC.v)
        p.vector.call("tensor_scalar", b_ap, w_ap, 1.0, eps1,
                      ALU.mult, ALU.add, inc=VE)
        p.scalar.wait(VE, VE.v)
        p.scalar.call("activation", a_ap, b_ap, AF.Ln)
        p.scalar.call("activation", r1, a_ap, AF.Exp, scale=-0.5, inc=SC)
        p.vector.wait(SC, SC.v)
        p.vector.call("tensor_scalar", xnb[:], xt, mu, s_ap,
                      ALU.subtract, ALU.mult, inc=VE)
        xn_v = VE.v
        p.vector.wait(STR, STR.v)
        if cfg.get("skip_ln"):
            p.vector.call("tensor_scalar", ystage[:], xt, mu, r1,
                          ALU.subtract, ALU.mult, inc=VE)
            p.vector.call("tensor_add", ystage[:], ystage[:], xt, inc=VE)
        else:
            p.vector.call("tensor_copy", ystage[:], xt, inc=VE)
        yb_v = VE.v
        p.sync.wait(VE, yb_v)
        p.sync.dma(ybscr[t * 128:(t + 1) * 128, :], ystage[:], STR)
        p.tensor.wait(VE, xn_v)
        p.tensor.wait(SC, SC.v)
        for c in range(NCH):
            p.tensor.call("transpose", ps_trf[:, c * 128:(c + 1) * 128],
                          xnb[:, c * 128:(c + 1) * 128], sid[0:128, 0:128],
                          inc=(TT if c == NCH - 1 else None))
        p.scalar.wait(TT)
        if cfg.get("nfold"):
            for c in range(NCH):
                p.scalar.call("activation",
                              xnT[:, t * D + c * 128:t * D + (c + 1) * 128],
                              ps_trf[:, c * 128:(c + 1) * 128], AF.Copy,
                              scale=gcol[:, c:c + 1],
                              inc=(SC if c == NCH - 1 else None))
        else:
            p.scalar.call("copy", xnT[:, t * D:(t + 1) * D], ps_trf[:], inc=SC)
        ev_v = SC.v
        p.tensor.wait(SC, max(ev_v, iota_v))
        p.tensor.wait(VE, wrt_v)
        for c in range(NCH):
            p.tensor.call("matmul", ps_r[:, 0:8],
                          xnT[:, t * D + c * 128:t * D + (c + 1) * 128],
                          wrtb[:, c * 8:(c + 1) * 8],
                          start=(c == 0), stop=(c == NCH - 1),
                          inc=(TT if c == NCH - 1 else None))
        rt_v = TT.v
        p.vector.wait(TT, rt_v)
        sc0 = rsc[:, 0:8]
        p.vector.call("tensor_copy", sc0, ps_r[:, 0:8], inc=VE)
        if not l0mode:
            p.vector.call("tensor_add", sc0, sc0, rsc[:, 16:24], inc=VE)
        m1 = stats[:, t * 6 + 3:t * 6 + 4]
        m2c = stats[:, t * 6 + 4:t * 6 + 5]
        i1 = stats[:, t * 6 + 5:t * 6 + 6]
        tmp8 = wsel[:, t * 8:(t + 1) * 8]
        d_ap, e_ap, z_ap = rsc[:, 8:9], rsc[:, 9:10], rsc[:, 10:11]
        w1, w2, i2 = rsc[:, 11:12], rsc[:, 12:13], rsc[:, 13:14]
        negm1 = rsc[:, 14:15]
        p.vector.call("reduce_max", m1, sc0, axis=AXX, inc=VE)
        p.vector.wait(VE)
        if not l0mode:
            p.vector.call("tensor_scalar", negm1, m1, -1.0, None, ALU.mult,
                          inc=VE)
            p.scalar.wait(VE, VE.v)
            p.scalar.call("activation", tmp8, sc0, AF.Exp, bias=negm1,
                          accum_out=z_ap, inc=SC)
            p.vector.wait(SC, SC.v)
        p.vector.call("tensor_scalar", tmp8, sc0, m1, None, ALU.is_ge, inc=VE)
        p.vector.call("tensor_mul", tmp8, tmp8, iota_b[:, 0:8], inc=VE)
        p.vector.call("reduce_max", i1, tmp8, axis=AXX, inc=VE)
        p.vector.wait(VE)
        p.vector.call("tensor_scalar", tmp8, iota_b[:, 0:8], i1, None,
                      ALU.is_equal, inc=VE)
        p.vector.call("scalar_tensor_tensor", tmp8, tmp8, -1e9, sc0,
                      ALU.mult, ALU.add, inc=VE)
        p.vector.call("reduce_max", m2c, tmp8, axis=AXX, inc=VE)
        p.vector.wait(VE)
        p.vector.call("tensor_scalar", tmp8, tmp8, m2c, None, ALU.is_ge,
                      inc=VE)
        p.vector.call("tensor_mul", tmp8, tmp8, iota_b[:, 0:8], inc=VE)
        p.vector.call("reduce_max", i2, tmp8, axis=AXX, inc=VE)
        p.vector.wait(VE)
        if l0mode:
            p.vector.call("tensor_sub", d_ap, m2c, m1, inc=VE)
            p.scalar.wait(VE, VE.v)
            p.scalar.call("activation", e_ap, d_ap, AF.Exp, inc=SC)
            p.vector.wait(SC, SC.v)
            p.vector.call("tensor_scalar", z_ap, e_ap, 1.0, 1.0, ALU.mult,
                          ALU.add, inc=VE)
            p.vector.call("reciprocal", w1, z_ap, inc=VE)
            p.vector.wait(VE)
            p.vector.call("tensor_scalar", w2, w1, -1.0, 1.0, ALU.mult,
                          ALU.add, inc=VE)
        else:
            p.vector.call("tensor_sub", d_ap, m2c, m1, inc=VE)
            p.scalar.wait(VE, VE.v)
            p.scalar.call("activation", e_ap, d_ap, AF.Exp, inc=SC)
            p.vector.wait(SC, SC.v)
            p.vector.call("reciprocal", w1, z_ap, inc=VE)
            p.vector.wait(VE)
            p.vector.call("tensor_mul", w2, e_ap, w1, inc=VE)
        p.vector.call("tensor_scalar", tmp8, iota_b[:, 0:8], i1, None,
                      ALU.is_equal, inc=VE)
        p.vector.call("tensor_scalar", tmp8, tmp8, w1, None, ALU.mult, inc=VE)
        p.vector.call("tensor_scalar", rsc[:, 0:8], iota_b[:, 0:8], i2, None,
                      ALU.is_equal, inc=VE)
        p.vector.call("tensor_scalar", rsc[:, 0:8], rsc[:, 0:8], w2, None,
                      ALU.mult, inc=VE)
        p.vector.call("tensor_add", tmp8, tmp8, rsc[:, 0:8], inc=VE)
        tile_done[t] = VE.v

    experts = [(e, h) for e in range(8) for h in range(2)]
    if shared:
        experts += [("sh", 0), ("sh", 1)]
    for (e, hf) in experts:
        is_sh = (e == "sh")
        pe = (e * 2 + hf) if not is_sh else hf
        upd = HH if (is_sh or not gated) else 2 * HH
        if is_sh:
            up_v = stream_w(wsu_d, hf * D, 0, NCH, HH, Wsl, 2048, 0)
        else:
            up_v = stream_w(wup_d, pe * D, 0, NCH, upd, Wsl, 2048, 0)
        has_up_bias = is_sh or moe_bias
        if has_up_bias or ((is_sh or moe_bias) and hf == 0):
            p.sync.wait(TT, TT.v)
        if has_up_bias:
            bld = p.sync.dma(brow[:, D:D + HH],
                             bs1_d[hf:hf + 1, :] if is_sh else rb1_d[pe:pe + 1, :], LD)
        has_dn_bias = (is_sh or moe_bias) and hf == 0
        if has_dn_bias:
            bld = p.sync.dma(b2row[:, 0:D],
                             bs2_d if is_sh else rb2_d[e:e + 1, :], LD)
        p.tensor.wait(CB, up_v)
        if has_up_bias or has_dn_bias:
            p.tensor.wait(LD, LD.v)
        for t in range(NT):
            p.tensor.wait(SC, SC.v)
            p.tensor.wait(VE, VE.v)
            nsl = upd // 512
            for s in range(nsl):
                if has_up_bias and s < 2:
                    p.tensor.call("matmul", ps_g[s][:], onesf[:, 0:128],
                                  brow[:, D + s * 512:D + (s + 1) * 512],
                                  start=True, stop=False)
                for c in range(NCH):
                    p.tensor.call("matmul", ps_g[s][:],
                                  xnT[:, t * D + c * 128:t * D + (c + 1) * 128],
                                  Wsl[:, c * 2048 + s * 512:c * 2048 + (s + 1) * 512],
                                  start=(c == 0 and not (has_up_bias and s < 2)),
                                  stop=(c == NCH - 1),
                                  inc=(TT if c == NCH - 1 else None))
            upmm_v = TT.v
            p.scalar.wait(TT, upmm_v)
            if gated and not is_sh:
                p.scalar.call("activation", gstage[:, 0:512], ps_g[2][:],
                              AF.Silu)
                p.scalar.call("activation", gstage[:, 512:1024], ps_g[3][:],
                              AF.Silu, inc=SC)
                p.vector.wait(SC, SC.v)
                p.vector.call("tensor_mul", hstage[:, 0:512], ps_g[0][:],
                              gstage[:, 0:512])
                p.vector.call("tensor_mul", hstage[:, 512:1024], ps_g[1][:],
                              gstage[:, 512:1024], inc=VE)
            else:
                p.scalar.call("activation", hstage[:, 0:512], ps_g[0][:],
                              AF.Gelu)
                p.scalar.call("activation", hstage[:, 512:1024], ps_g[1][:],
                              AF.Gelu, inc=SC)
                p.vector.wait(SC, SC.v)
                p.vector.call("tensor_copy", hstage[:, 0:D], hstage[:, 0:D],
                              inc=VE)
            h_v = VE.v
            p.tensor.wait(VE, h_v)
            p.tensor.wait(SC, SC.v)
            for c in range(NCH):
                p.tensor.call("transpose", ps_tr[:, c * 128:(c + 1) * 128],
                              hstage[:, c * 128:(c + 1) * 128],
                              sidb[0:128, 0:128],
                              inc=(TT if c == NCH - 1 else None))
            p.scalar.wait(TT)
            p.scalar.call("copy", hT[:, t * HH:(t + 1) * HH], ps_tr[:], inc=SC)
        hT_done = SC.v
        if is_sh:
            dn_v = stream_w(wsd_d, hf * HH, 0, NCH, D, Wsl, 2048, 0)
        else:
            dn_v = stream_w(wdn_d, pe * HH, 0, NCH, D, Wsl, 2048, 0)
        p.tensor.wait(CB, dn_v)
        p.tensor.wait(SC, hT_done)
        for t in range(NT):
            p.tensor.wait(VE, VE.v)
            for s in range(2):
                if has_dn_bias:
                    p.tensor.call("matmul", ps_g[s][:], onesf[:, 0:128],
                                  b2row[:, s * 512:(s + 1) * 512],
                                  start=True, stop=False)
                for c in range(NCH):
                    p.tensor.call("matmul", ps_g[s][:],
                                  hT[:, t * HH + c * 128:t * HH + (c + 1) * 128],
                                  Wsl[:, c * 2048 + s * 512:c * 2048 + (s + 1) * 512],
                                  start=(c == 0 and not has_dn_bias),
                                  stop=(c == NCH - 1),
                                  inc=(TT if c == NCH - 1 else None))
            dnmm_v = TT.v
            p.vector.wait(TT, dnmm_v)
            if is_sh:
                p.vector.call("tensor_add", oa[0][:, t * 512:(t + 1) * 512],
                              oa[0][:, t * 512:(t + 1) * 512], ps_g[0][:])
                p.vector.call("tensor_add", oa[1][:, t * 512:(t + 1) * 512],
                              oa[1][:, t * 512:(t + 1) * 512], ps_g[1][:],
                              inc=VE)
            else:
                we = rsc[:, 15:16]
                p.vector.call("tensor_scalar", we, wsel[:, t * 8 + e:t * 8 + e + 1],
                              1.0, None, ALU.mult, inc=VE)
                p.vector.wait(VE)
                p.vector.call("scalar_tensor_tensor",
                              oa[0][:, t * 512:(t + 1) * 512],
                              ps_g[0][:], we, oa[0][:, t * 512:(t + 1) * 512],
                              ALU.mult, ALU.add)
                p.vector.call("scalar_tensor_tensor",
                              oa[1][:, t * 512:(t + 1) * 512],
                              ps_g[1][:], we, oa[1][:, t * 512:(t + 1) * 512],
                              ALU.mult, ALU.add, inc=VE)

    for t in range(NT):
        p.sync.wait(VE, VE.v)
        p.sync.wait(STR, STR.v)
        xld = p.sync.dma(xstage[:, 0:D], ybscr[t * 128:(t + 1) * 128, :], LD)
        p.vector.wait(LD, xld)
        p.vector.call("tensor_add", ystage[:, 0:512], xstage[:, 0:512],
                      oa[0][:, t * 512:(t + 1) * 512])
        p.vector.call("tensor_add", ystage[:, 512:1024], xstage[:, 512:1024],
                      oa[1][:, t * 512:(t + 1) * 512], inc=VE)
        p.sync.wait(VE, VE.v)
        if io:
            io["ywrite"](t, ystage, STR)
        else:
            p.sync.dma(y_out[t * 128:(t + 1) * 128, :], ystage[:], STR)
    p.sync.wait(STR, STR.v)
    return p



def build_scan(cfg, p=None, io=None):
    standalone = p is None
    if standalone:
        p = Prog(cfg["name"])
    nc = p.nc
    SCD, STD = 64, 16
    EPSR = 1.1920929e-07

    x_d = io["xb"] if io else p.din("xb", [T, D], F32)
    pA_d = io["pA"] if io else p.din("pA", [T, D], F32)
    pB_d = io["pB"] if io else p.din("pB", [T, D], F32)
    st_d = None if io else p.din("st_in", [128, NT_FULL * 6], F32)
    wzg_d = p.din("wzg", [D, 128], F32)
    wdt_d = p.din("wdt", [SCD, 96], F32)
    dtb_d = p.din("dtb", [1, 96], F32)
    alog_d = p.din("alog", [128, 8], F32)
    ec_d = p.din("ec", [SCD, 8 * 128], F32)
    est_d = p.din("est", [STD, 128], F32)
    g_d = p.din("gmat", [128, 8 * SCD], F32)
    wout_d = p.din("wout", [SCD, D], F32)
    gvec_d = p.din("gvec", [128, 8], F32)
    identf = p.din("identf", [128, 128], F32)
    if standalone:
        y_out = nc.dram_tensor("y", [T, D], F32, kind="ExternalOutput").ap()
    x1scr = p.dint("x1scr", [T, D], F32)

    rmsc = p.sbuf("rmsc", [128, NT_FULL * 2], F32)
    stin = (io.get("stats_buf") if io else None) or \
        p.sbuf("stin", [128, NT_FULL * 6], F32)
    xstage = p.sbuf("xstage", [128, 2 * D], F32)
    pstage = p.sbuf("pstage", [128, 2 * D], F32)
    x1b = p.sbuf("x1b", [128, D], F32)
    xnb = p.sbuf("xnb", [128, D], BF16)
    xnT = p.sbuf("xnT", [128, D], BF16)
    wzg = p.sbuf("wzg_s", [128, NCH * 128], BF16)
    wdt = p.sbuf("wdt_s", [SCD, 96], BF16)
    zF = p.sbuf("zF", [SCD, T], BF16)
    gF = p.sbuf("gF", [SCD, T], BF16)
    ztm = p.sbuf("ztm", [128, SCD], F32)
    dtm = p.sbuf("dtm", [128, 2 * SCD], F32)
    bici = p.sbuf("bici", [128, 96], F32)
    dtF = p.sbuf("dtF", [SCD, T], F32)
    dtzF = p.sbuf("dtzF", [SCD, T], F32)
    BiF = p.sbuf("BiF", [STD, T], F32)
    CiF = p.sbuf("CiF", [STD, T], F32)
    ec = p.sbuf("ec_s", [SCD, 8 * 128], F32)
    est = p.sbuf("est_s", [STD, 128], F32)
    gmat = p.sbuf("gmat_s", [128, 8 * SCD], F32)
    woutb = p.sbuf("woutb", [SCD, D], BF16)
    acol = p.sbuf("acol_s", [128, 8], F32)
    onec = p.sbuf("onec", [128, 1], F32)
    onesr = p.sbuf("onesr", [1, 128], F32)
    a_sb = p.sbuf("a_sb", [128, 512], F32)
    u0_sb = p.sbuf("u0_sb", [128, 512], F32)
    u_sb = p.sbuf("u_sb", [128, 512], F32)
    states = p.sbuf("states", [128, 512], F32)
    prod = p.sbuf("prod", [128, 512], F32)
    carry = p.sbuf("carry", [128, 8], F32)
    ygb = p.sbuf("ygb", [SCD, 512], BF16)
    ystage = p.sbuf("ystage", [128, D], F32)
    x1r = p.sbuf("x1r", [128, D], F32)
    sid = p.sbuf("sid", [128, 128], F32)
    sidb = p.sbuf("sidb", [128, 128], BF16)
    sp1 = p.sbuf("sp1", [128, SCD], F32)
    sp2 = p.sbuf("sp2", [128, SCD], F32)
    Wstage = p.sbuf("Wstage", [128, 512], F32)
    dtbrow = p.sbuf("dtbrow", [1, 96], F32)
    gcol = p.sbuf("gcol_s", [128, 8], F32)

    ps_a = p.psum("ps_a", [128, 512], F32)
    ps_b = p.psum("ps_b", [128, 512], F32)
    ps_c = p.psum("ps_c", [128, 512], F32)
    ps_y = p.psum("ps_y", [128, 512], F32)
    ps_p = [p.psum(f"ps_p{i}", [128, 512], F32) for i in range(2)]
    ps_tr = p.psum("ps_tr", [128, 1024], BF16)
    ps_trf = p.psum("ps_trf", [128, 512], F32)

    LD = p.sem("LD")
    WL = p.sem("WL")
    WLb = p.sem("WLb")
    SC = p.sem("SC")
    VE = p.sem("VE")
    TT = p.sem("TT")
    STR = p.sem("STR")
    GP = p.sem("GP")
    X1 = p.sem("X1")

    p.sync.dma(sid[:], identf, LD)
    if not io:
        p.sync.dma(stin[:], st_d, LD)
    p.sync.dma(ec[:], ec_d, LD)
    p.sync.dma(est[:], est_d, LD)
    p.sync.dma(gmat[:], g_d, LD)
    p.sync.dma(acol[:], alog_d, LD)
    p.sync.dma(dtbrow[:, :], dtb_d, LD)
    p.sync.dma(gcol[:], gvec_d, LD)
    p.sync.dma(Wstage[0:SCD, 0:96], wdt_d, LD)
    ld_consts = LD.v
    p.vector.wait(LD, ld_consts)
    p.vector.call("tensor_copy", sidb[:], sid[:], inc=VE)
    p.vector.call("tensor_copy", wdt[:], Wstage[0:SCD, 0:96], inc=VE)
    CB = p.sem("CB")
    p.gpsimd.call("memset", onec[:], 1.0)
    p.gpsimd.call("memset", onesr[:], 1.0, inc=GP)
    p.gpsimd.dma(woutb[:], wout_d, CB)
    gp_v = GP.v
    cb_v = CB.v
    p.scalar.wait(LD, ld_consts)
    p.scalar.call("activation", acol[:], acol[:], AF.Exp, inc=SC)
    p.vector.wait(SC, SC.v)
    p.vector.call("tensor_scalar", acol[:], acol[:], -1.0, None, ALU.mult,
                  inc=VE)
    acol_v = VE.v
    for c in range(NCH):
        p.sync.wait(VE, VE.v)
        p.sync.dma(Wstage[:, 0:128], wzg_d[c * 128:(c + 1) * 128, :], WL)
        p.vector.wait(WL)
        p.vector.call("tensor_scalar", wzg[:, c * 128:(c + 1) * 128],
                      Wstage[:, 0:128], gcol[:, c:c + 1], None, ALU.mult,
                      inc=VE)
    wzg_v = VE.v

    tile_done = [0] * NT_FULL
    for t in range(cfg.get("maxt", NT_FULL)):
        if t >= 1:
            p.sync.wait(VE, tile_done[t - 1])
        p.sync.dma(xstage[:, (t % 2) * D:(t % 2 + 1) * D],
                   x_d[t * 128:(t + 1) * 128, :], LD)
        p.sync.dma(pstage[:, 0:D], pA_d[t * 128:(t + 1) * 128, :], LD)
        xld = p.sync.dma(pstage[:, D:2 * D], pB_d[t * 128:(t + 1) * 128, :], LD)
        xt = xstage[:, (t % 2) * D:(t % 2 + 1) * D]
        st6 = stin[:, t * 6:(t + 1) * 6]
        p.vector.wait(LD, xld)
        p.vector.wait(X1, X1.v)
        p.vector.call("tensor_scalar", x1b[:], xt, st6[:, 0:1], st6[:, 2:3],
                      ALU.subtract, ALU.mult, inc=VE)
        p.vector.call("tensor_add", x1b[:], x1b[:], pstage[:, 0:D], inc=VE)
        p.vector.call("tensor_add", x1b[:], x1b[:], pstage[:, D:2 * D], inc=VE)
        x1_v = VE.v
        p.sync.wait(VE, x1_v)
        p.sync.dma(x1scr[t * 128:(t + 1) * 128, :], x1b[:], X1)
        if cfg.get("steps", 99) < 1:
            tile_done[t] = VE.v
            continue
        p.scalar.wait(VE, x1_v)
        p.scalar.wait(TT, TT.v)
        p.scalar.call("activation", xnb[:], x1b[:], AF.Square,
                      accum_out=rmsc[:, 2 * t:2 * t + 1], inc=SC)
        p.vector.wait(SC, SC.v)
        p.vector.call("tensor_scalar", rmsc[:, 2 * t:2 * t + 1],
                      rmsc[:, 2 * t:2 * t + 1], 1.0 / D, EPSR,
                      ALU.mult, ALU.add, inc=VE)
        p.scalar.wait(VE, VE.v)
        p.scalar.call("activation", rmsc[:, 2 * t + 1:2 * t + 2],
                      rmsc[:, 2 * t:2 * t + 1], AF.Ln)
        p.scalar.call("activation", rmsc[:, 2 * t:2 * t + 1],
                      rmsc[:, 2 * t + 1:2 * t + 2], AF.Exp, scale=-0.5, inc=SC)
        p.vector.wait(SC, SC.v)
        p.vector.call("tensor_scalar", xnb[:], x1b[:],
                      rmsc[:, 2 * t:2 * t + 1], None, ALU.mult, inc=VE)
        xn_v = VE.v
        if cfg.get("steps", 99) < 2:
            tile_done[t] = VE.v
            continue
        p.tensor.wait(VE, xn_v)
        p.tensor.wait(SC, SC.v)
        for c in range(NCH):
            p.tensor.call("transpose", ps_tr[:, c * 128:(c + 1) * 128],
                          xnb[:, c * 128:(c + 1) * 128], sidb[0:128, 0:128],
                          inc=(TT if c == NCH - 1 else None))
        p.scalar.wait(TT)
        p.scalar.call("copy", xnT[:], ps_tr[:], inc=SC)
        ev_v = SC.v
        if cfg.get("steps", 99) < 3:
            tile_done[t] = VE.v
            continue
        p.tensor.wait(SC, ev_v)
        p.tensor.wait(VE, wzg_v)
        for c in range(NCH):
            p.tensor.call("matmul", ps_a[:, 0:128],
                          xnT[:, c * 128:(c + 1) * 128],
                          wzg[:, c * 128:(c + 1) * 128],
                          start=(c == 0), stop=(c == NCH - 1),
                          inc=(TT if c == NCH - 1 else None))
        zg_v = TT.v
        if cfg.get("steps", 99) < 4:
            tile_done[t] = VE.v
            continue
        p.vector.wait(TT, zg_v)
        p.vector.call("tensor_copy", ztm[:], ps_a[:, 0:SCD], inc=VE)
        p.vector.call("tensor_copy", xnb[:, SCD:2 * SCD], ztm[:], inc=VE)
        p.scalar.wait(TT, zg_v)
        p.scalar.call("activation", xnb[:, 0:SCD], ps_a[:, SCD:128], AF.Silu,
                      inc=SC)
        p.tensor.wait(VE, VE.v)
        p.tensor.wait(SC, SC.v)
        p.tensor.call("transpose", ps_tr[:, 0:128][0:SCD, 0:128],
                      xnb[:, SCD:2 * SCD], sidb[0:128, 0:128])
        p.tensor.call("transpose", ps_tr[:, 128:256][0:SCD, 0:128],
                      xnb[:, 0:SCD], sidb[0:128, 0:128], inc=TT)
        p.scalar.wait(TT)
        p.scalar.call("copy", zF[:, t * 128:(t + 1) * 128],
                      ps_tr[:, 0:128][0:SCD, :])
        p.scalar.call("copy", gF[:, t * 128:(t + 1) * 128],
                      ps_tr[:, 128:256][0:SCD, :], inc=SC)
        zf_v = SC.v
        if cfg.get("steps", 99) < 5:
            tile_done[t] = VE.v
            continue
        ss = cfg.get("substep", 99)
        p.tensor.wait(SC, zf_v)
        p.tensor.wait(LD, ld_consts)
        p.tensor.wait(GP, gp_v)
        p.tensor.wait(CB, cb_v)
        p.tensor.wait(VE, VE.v)
        if ss >= 1:
            p.tensor.call("matmul", ps_b[:, 0:96], onesr[:, 0:128], dtbrow[:, :],
                          start=True, stop=False)
            p.tensor.call("matmul", ps_b[:, 0:96],
                          zF[:, t * 128:(t + 1) * 128], wdt[:],
                          start=False, stop=True, inc=TT)
        dt_v = TT.v
        if ss >= 2:
            p.vector.wait(TT, dt_v)
            p.vector.call("tensor_scalar", dtm[:, 0:SCD], ps_b[:, 0:SCD], 0.0,
                          None, ALU.max, inc=VE)
            p.vector.call("tensor_copy", bici[:], ps_b[:, 0:96], inc=VE)
        if ss >= 3:
            p.scalar.wait(VE, VE.v)
            p.scalar.call("activation", sp1[:], bici[:, 0:SCD], AF.Abs)
            p.scalar.call("activation", sp1[:], sp1[:], AF.Exp, scale=-1.0)
            p.scalar.call("activation", sp2[:], sp1[:], AF.Ln,
                          bias=onec[:], inc=SC)
        if ss >= 4:
            p.vector.wait(SC, SC.v)
            p.vector.call("tensor_add", dtm[:, 0:SCD], dtm[:, 0:SCD], sp2[:],
                          inc=VE)
            p.vector.call("tensor_mul", dtm[:, SCD:2 * SCD], dtm[:, 0:SCD],
                          ztm[:], inc=VE)
        dtz_v = VE.v
        if cfg.get("steps", 99) < 6:
            tile_done[t] = VE.v
            continue
        p.tensor.wait(VE, dtz_v)
        p.tensor.wait(SC, SC.v)
        p.tensor.call("transpose", ps_trf[:, 0:128][0:SCD, :],
                      dtm[:, 0:SCD], sid[0:128, 0:128])
        p.tensor.call("transpose", ps_trf[:, 128:256][0:SCD, :],
                      dtm[:, SCD:2 * SCD], sid[0:128, 0:128])
        p.tensor.call("transpose", ps_trf[:, 256:384][0:STD, :],
                      bici[:, 64:64 + STD], sid[0:128, 0:128])
        p.tensor.call("transpose", ps_trf[:, 384:512][0:STD, :],
                      bici[:, 64 + STD:96], sid[0:128, 0:128], inc=TT)
        p.scalar.wait(TT)
        p.scalar.call("copy", dtF[:, t * 128:(t + 1) * 128],
                      ps_trf[:, 0:128][0:SCD, :])
        p.scalar.call("copy", dtzF[:, t * 128:(t + 1) * 128],
                      ps_trf[:, 128:256][0:SCD, :])
        p.scalar.call("copy", BiF[:, t * 128:(t + 1) * 128],
                      ps_trf[:, 256:384][0:STD, :])
        p.scalar.call("copy", CiF[:, t * 128:(t + 1) * 128],
                      ps_trf[:, 384:512][0:STD, :], inc=SC)
        tile_done[t] = VE.v

    phaseA = (SC.v, VE.v, TT.v)
    if cfg.get("nophaseb"):
        p.sync.wait(STR, STR.v)
        p.sync.wait(X1, X1.v)
        return p
    p.tensor.wait(SC, SC.v)
    p.tensor.wait(VE, VE.v)
    p.tensor.wait(LD, ld_consts)
    for ch in range(8):
        cs = slice(ch * 512, (ch + 1) * 512)
        for j in range(8):
            p.tensor.wait(VE, VE.v)
            p.tensor.call("matmul", ps_a[:], ec[:, j * 128:(j + 1) * 128],
                          dtF[:, cs], start=True, stop=True)
            p.tensor.call("matmul", ps_c[:], ec[:, j * 128:(j + 1) * 128],
                          dtzF[:, cs], start=True, stop=True)
            p.tensor.call("matmul", ps_b[:], est[:], BiF[:, cs],
                          start=True, stop=True, inc=TT)
            rep_v = TT.v
            p.vector.wait(TT, rep_v)
            p.vector.call("tensor_scalar", a_sb[:], ps_a[:],
                          acol[:, j:j + 1], 1.0, ALU.mult, ALU.add, inc=VE)
            p.vector.call("tensor_copy", u0_sb[:], ps_c[:], inc=VE)
            p.vector.call("tensor_mul", u_sb[:], u0_sb[:], ps_b[:], inc=VE)
            if cfg.get("noscan"):
                p.vector.call("tensor_copy", states[:], u_sb[:], inc=VE)
            elif ch == 0:
                p.vector.call("tensor_tensor_scan", states[:], a_sb[:],
                              u_sb[:], 0.0, ALU.mult, ALU.add, inc=VE)
            else:
                p.vector.call("tensor_tensor_scan", states[:], a_sb[:],
                              u_sb[:], carry[:, j:j + 1], ALU.mult, ALU.add,
                              inc=VE)
            p.vector.call("tensor_copy", carry[:, j:j + 1],
                          states[:, 511:512], inc=VE)
            p.tensor.wait(VE, VE.v)
            p.tensor.call("matmul", ps_b[:], est[:], CiF[:, cs],
                          start=True, stop=True, inc=TT)
            p.vector.wait(TT, TT.v)
            p.vector.call("tensor_mul", prod[:], states[:], ps_b[:], inc=VE)
            prod_v = VE.v
            p.tensor.wait(VE, prod_v)
            p.tensor.call("matmul", ps_y[0:SCD, :],
                          gmat[:, j * SCD:(j + 1) * SCD], prod[:],
                          start=(j == 0), stop=(j == 7),
                          inc=(TT if j == 7 else None))
        y_v = TT.v
        p.vector.wait(TT, y_v)
        p.vector.call("tensor_mul", ygb[:], ps_y[0:SCD, :], gF[0:SCD, cs],
                      inc=VE)
        yg_v = VE.v
        for i in range(4):
            gt = ch * 4 + i
            p.tensor.wait(VE, yg_v if i == 0 else VE.v)
            for s in range(2):
                p.tensor.call("matmul", ps_p[s][:],
                              ygb[:, i * 128:(i + 1) * 128],
                              woutb[:, s * 512:(s + 1) * 512],
                              start=True, stop=True,
                              inc=(TT if s == 1 else None))
            pj_v = TT.v
            p.sync.wait(VE, VE.v)
            p.sync.wait(STR, STR.v)
            p.sync.dma(xstage[:, 0:D], x_d[gt * 128:(gt + 1) * 128, :], LD)
            p.sync.wait(X1, X1.v)
            xld = p.sync.dma(x1r[:], x1scr[gt * 128:(gt + 1) * 128, :], LD)
            p.vector.wait(TT, pj_v)
            p.vector.wait(LD, xld)
            p.vector.call("tensor_add", ystage[:], xstage[:, 0:D], x1r[:],
                          inc=VE)
            p.vector.call("tensor_add", ystage[:, 0:512], ystage[:, 0:512],
                          ps_p[0][:])
            p.vector.call("tensor_add", ystage[:, 512:1024],
                          ystage[:, 512:1024], ps_p[1][:], inc=VE)
            fy_v = VE.v
            p.sync.wait(VE, fy_v)
            if io:
                io["ywrite"](gt, ystage, STR)
            else:
                p.sync.dma(y_out[gt * 128:(gt + 1) * 128, :], ystage[:], STR)
    p.sync.wait(STR, STR.v)
    p.sync.wait(X1, X1.v)
    return p



_PROG_CACHE = {}

CFG_L0A = dict(name="l0a", H=16, dh=64, causal=False, eps2=EPS_LN,
               gamma_fold=True, beta_qkv=True)
CFG_L0B = dict(name="l0b", H=16, dh=64, causal=True, rope=True, eps2=None,
               bias_rows=True, w_f16=True)
CFG_L1A = dict(name="l1a", H=16, dh=64, causal=True, eps2=1.1920929e-07,
               gamma_fold=True)
CFG_L1B = dict(name="l1b", H=8, dh=128, causal=True, eps2=None)
CFG_FFN0A = dict(name="ffn0a", router="topk_softmax", rms_eps=1e-6, nfold=True,
                 skip_ln=True, combine="plain", bcmb=True, gated=True)
CFG_FFN0B = dict(name="ffn0b", router="topk_softmax", rms_eps=1e-6, nfold=True,
                 skip_ln=True, combine="fp16", bcmb=True, gated=True)
CFG_FFN1A = dict(name="ffn1a", router="softmax_topk", rms_eps=None, nfold=False,
                 skip_ln=False, combine=None, shared=True, moe_bias=True,
                 gated=False)
CFG_FFN1B = dict(name="ffn1b", router="softmax_topk", rms_eps=None, nfold=False,
                 skip_ln=False, combine="plain", shared=True, moe_bias=True,
                 gated=False)


def _prog(key, builder, cfg):
    if key not in _PROG_CACHE:
        _PROG_CACHE[key] = builder(cfg).emit_and_finish()
    return _PROG_CACHE[key]


def _colvec(v):
    return np.ascontiguousarray(np.asarray(v, np.float32).reshape(8, 128).T)


def _identf():
    return np.eye(128, dtype=np.float32)


def _cmask():
    k = np.arange(128)
    return np.ascontiguousarray(
        np.where(k[:, None] > k[None, :], np.float32(NEG), np.float32(0.0)))


def _iota8():
    return np.arange(8, dtype=np.float32)[None]


def _rope_tables():
    dh = 64
    pos = np.arange(T, dtype=np.float32)
    inv = np.exp(np.arange(0, dh, 2, dtype=np.float32) * (-np.log(10000.0) / dh))
    ang = pos[:, None] * inv[None, :]
    sin = np.sin(ang).astype(np.float16).astype(np.float32)
    cos = np.cos(ang).astype(np.float16).astype(np.float32)
    return (np.ascontiguousarray(np.tile(cos, (1, 8))),
            np.ascontiguousarray(np.tile(sin, (1, 8))))


def _scan_consts():
    ec = np.zeros((64, 8 * 128), np.float32)
    for j in range(8):
        for pp in range(128):
            ec[8 * j + pp // 16, j * 128 + pp] = 1.0
    est = np.zeros((16, 128), np.float32)
    for pp in range(128):
        est[pp % 16, pp] = 1.0
    g = np.zeros((128, 8 * 64), np.float32)
    for j in range(8):
        for pp in range(128):
            g[pp, j * 64 + 8 * j + pp // 16] = 1.0
    return ec, est, g


def _run(nc, in_maps):
    r = run_bass_kernel_spmd(nc, in_maps, core_ids=list(range(8)))
    return r.results


def _attn_launch(state, cfg, wmaps):
    nc = _prog(cfg["name"], build_attention, cfg)
    in_maps = []
    for c in range(8):
        m = dict(wmaps[c % 2])
        m["xb"] = np.ascontiguousarray(state[c // 2])
        in_maps.append(m)
    res = _run(nc, in_maps)
    return ([res[c]["y"] for c in range(8)],
            [res[c]["st_out"] for c in range(8)])


def _ffn_launch(state, cfg, wshared, partials=None, key=None):
    nc = _prog(key or cfg["name"], build_ffn, cfg)
    in_maps = []
    for c in range(8):
        b = c // 2
        half = c % 2
        rows = slice(half * 2048, (half + 1) * 2048)
        m = dict(wshared)
        m["xo"] = np.ascontiguousarray(state[b][rows])
        if partials is not None:
            m["pA"] = np.ascontiguousarray(partials[2 * b][rows])
            m["pB"] = np.ascontiguousarray(partials[2 * b + 1][rows])
        in_maps.append(m)
    res = _run(nc, in_maps)
    new_state = state.copy()
    for c in range(8):
        b, half = c // 2, c % 2
        new_state[b][half * 2048:(half + 1) * 2048] = res[c]["y"]
    return new_state


def kernel_unfused(**inputs):
    g = lambda k: np.asarray(inputs[k])
    x = g("L0_attn_a_x").astype(np.float32)
    state = np.ascontiguousarray(x.reshape(4, T, D))
    identf = _identf()
    cmask = _cmask()
    iota = _iota8()
    rcos, rsin = _rope_tables()

    Wqkv = g("L0_attn_a_W_qkv")
    bqkv = g("L0_attn_a_b_qkv")
    wmaps = []
    for pi in range(2):
        sl = slice(pi * 512, (pi + 1) * 512)
        wmaps.append(dict(
            wq=np.ascontiguousarray(Wqkv[:, 0:1024][:, sl]),
            wkv=np.ascontiguousarray(np.hstack([Wqkv[:, 1024:2048][:, sl],
                                                Wqkv[:, 2048:3072][:, sl]])),
            wp=np.ascontiguousarray(g("L0_attn_a_W_proj")[sl, :]),
            bq=np.ascontiguousarray(bqkv[0:1024][sl])[None],
            bkv=np.ascontiguousarray(np.hstack([bqkv[1024:2048][sl],
                                                bqkv[2048:3072][sl]]))[None],
            gvec=_colvec(g("L0_attn_a_ln_gamma")),
            bvec=_colvec(g("L0_attn_a_ln_beta")),
            identf=identf))
    parts, _ = _attn_launch(state, CFG_L0A, wmaps)

    wupT = g("L0_ffn_w_up").transpose(0, 2, 1)
    wdnT = g("L0_ffn_w_down").transpose(0, 2, 1)
    wu = np.ascontiguousarray(np.concatenate(
        [np.concatenate([wupT[e][:, h * 1024:(h + 1) * 1024],
                         wupT[e][:, 2048 + h * 1024:2048 + (h + 1) * 1024]],
                        axis=1)
         for e in range(8) for h in range(2)], axis=0))
    wd = np.ascontiguousarray(np.concatenate(
        [wdnT[e][h * 1024:(h + 1) * 1024, :]
         for e in range(8) for h in range(2)], axis=0))
    ffn0_w = dict(wrt=np.ascontiguousarray(g("L0_ffn_w_router").T),
                  wup=wu, wdn=wd,
                  gvec=_colvec(g("L0_ffn_norm_scale")),
                  identf=identf, iota=iota)
    w0a = dict(ffn0_w)
    w0a["bcmb"] = np.ascontiguousarray(g("L0_attn_a_b_proj"))[None]
    state = _ffn_launch(state, CFG_FFN0A, w0a, parts)

    wmaps = []
    for pi in range(2):
        sl = slice(pi * 512, (pi + 1) * 512)
        wmaps.append(dict(
            wq=np.ascontiguousarray(g("L0_attn_b_Wq")[:, sl]),
            wkv=np.ascontiguousarray(np.hstack([g("L0_attn_b_Wk")[:, sl],
                                                g("L0_attn_b_Wv")[:, sl]])),
            wp=np.ascontiguousarray(g("L0_attn_b_Wo")[sl, :]),
            bq=np.ascontiguousarray(
                g("L0_attn_b_bq").astype(np.float32)[sl])[None],
            bkv=np.ascontiguousarray(np.hstack(
                [g("L0_attn_b_bk").astype(np.float32)[sl],
                 g("L0_attn_b_bv").astype(np.float32)[sl]]))[None],
            rcos=rcos, rsin=rsin, cmask=cmask, identf=identf))
    parts, _ = _attn_launch(state, CFG_L0B, wmaps)

    w0b = dict(ffn0_w)
    w0b["bcmb"] = np.ascontiguousarray(
        g("L0_attn_b_bo").astype(np.float32))[None]
    state = _ffn_launch(state, CFG_FFN0B, w0b, parts)

    qkvT = np.ascontiguousarray(g("L1_attn_a_qkv_w").T)
    wmaps = []
    for pi in range(2):
        sl = slice(pi * 512, (pi + 1) * 512)
        wmaps.append(dict(
            wq=np.ascontiguousarray(qkvT[:, 0:1024][:, sl]),
            wkv=np.ascontiguousarray(np.hstack([qkvT[:, 1024:2048][:, sl],
                                                qkvT[:, 2048:3072][:, sl]])),
            wp=np.ascontiguousarray(g("L1_attn_a_o_w").T[sl, :]),
            gvec=_colvec(g("L1_attn_a_norm1_w")),
            cmask=cmask, identf=identf))
    parts, stats = _attn_launch(state, CFG_L1A, wmaps)

    ec, est, gmat = _scan_consts()
    nc = _prog("scan", build_scan, dict(name="scan"))
    in_maps = []
    for c in range(8):
        b = c // 2
        m = dict(
            xb=np.ascontiguousarray(state[b]),
            pA=np.ascontiguousarray(parts[2 * b]),
            pB=np.ascontiguousarray(parts[2 * b + 1]),
            st_in=stats[2 * b],
            wzg=np.ascontiguousarray(np.hstack([g("L1_attn_a_in_w").T,
                                                g("L1_attn_a_gate_w").T])),
            wdt=np.ascontiguousarray(np.hstack([g("L1_attn_a_dt_w").T,
                                                g("L1_attn_a_Bp_w").T,
                                                g("L1_attn_a_Cp_w").T])),
            dtb=np.ascontiguousarray(np.concatenate(
                [g("L1_attn_a_dt_b"), np.zeros(32, np.float32)]))[None],
            alog=np.ascontiguousarray(
                g("L1_attn_a_A_log").reshape(-1).reshape(8, 128).T),
            ec=ec, est=est, gmat=gmat,
            wout=np.ascontiguousarray(g("L1_attn_a_out_w").T),
            gvec=_colvec(g("L1_attn_a_norm2_w")),
            identf=identf)
        in_maps.append(m)
    res = _run(nc, in_maps)
    for b in range(4):
        state[b] = res[2 * b]["y"]

    rw1 = g("L1_ffn_rw1")
    rw2 = g("L1_ffn_rw2")
    wu = np.ascontiguousarray(np.concatenate(
        [rw1[e][:, h * 1024:(h + 1) * 1024]
         for e in range(8) for h in range(2)], axis=0))
    wd = np.ascontiguousarray(np.concatenate(
        [rw2[e][h * 1024:(h + 1) * 1024, :]
         for e in range(8) for h in range(2)], axis=0))
    sw1 = g("L1_ffn_sw1")
    ffn1_w = dict(
        wrt=np.ascontiguousarray(g("L1_ffn_router_w")),
        brt=np.ascontiguousarray(g("L1_ffn_router_b"))[None],
        wup=wu, wdn=wd,
        wsu=np.ascontiguousarray(np.concatenate(
            [sw1[:, 0:1024], sw1[:, 1024:2048]], axis=0)),
        wsd=np.ascontiguousarray(g("L1_ffn_sw2")),
        bs1=np.ascontiguousarray(g("L1_ffn_sb1").reshape(2, 1024)),
        bs2=np.ascontiguousarray(g("L1_ffn_sb2"))[None],
        rb1=np.ascontiguousarray(g("L1_ffn_rb1").reshape(16, 1024)),
        rb2=np.ascontiguousarray(g("L1_ffn_rb2")),
        identf=identf, iota=iota)
    state = _ffn_launch(state, CFG_FFN1A, ffn1_w)

    wmaps = []
    for pi in range(2):
        sl = slice(pi * 512, (pi + 1) * 512)
        wmaps.append(dict(
            wq=np.ascontiguousarray(g("L1_attn_b_Wq").T[:, sl]),
            wkv=np.ascontiguousarray(np.hstack([g("L1_attn_b_Wk").T[:, sl],
                                                g("L1_attn_b_Wv").T[:, sl]])),
            wp=np.ascontiguousarray(g("L1_attn_b_Wo").T[sl, :]),
            cmask=cmask, identf=identf))
    parts, _ = _attn_launch(state, CFG_L1B, wmaps)

    state = _ffn_launch(state, CFG_FFN1B, ffn1_w, parts)
    return np.ascontiguousarray(state.reshape(4, T, D))


I32 = mybir.dt.int32


def build_fused():
    p = Prog("fused")
    nc = p.nc
    p.init_barrier_sems()
    p.init_walls({F32: 98_000_000, FP16: 2_200_000,
                  ("internal", BF16): 12_000_000,
                  ("internal", F32): 24_000_000})
    state = nc.dram_tensor("state", [T, D], F32, kind="Internal",
                           addr_space="Shared").ap()
    parts = nc.dram_tensor("parts", [2 * T, D], F32, kind="Internal",
                           addr_space="Shared").ap()
    flags = nc.dram_tensor("flags", [1, 32], I32, kind="Internal",
                           addr_space="Shared").ap()
    xb_in = nc.dram_tensor("xb", [T, D], F32, kind="ExternalInput").ap()
    nonce_in = nc.dram_tensor("nonce", [1, 2], I32, kind="ExternalInput").ap()
    y_final = nc.dram_tensor("y", [2048, D], F32, kind="ExternalOutput").ap()
    partA = parts[0:T, :]
    partB = parts[T:2 * T, :]

    stats_l1a = p.sbuf("statsP", [128, NT_FULL * 6], F32)
    snon = p.sbuf("snon", [1, 2], I32)
    fpoll = p.sbuf("fpoll", [1, 1], I32)
    FL = p.sem("FLG")
    NLD = p.sem("NLD")

    H = {}

    def init_vals(eng):
        pid = eng.partition_id()
        par = eng.compute_val(pid % 2)
        rp = eng.alloc_register("rpar")
        eng.reg_mov(rp, par)
        H["rp"] = rp

    p.sync.dma(snon[:], nonce_in, NLD)
    p.sync.wait(NLD, NLD.v)
    p.q["sync"].ops.append(("__raw__", [init_vals], {}, None, 0))

    def pair_dma(dstA, dstB, srcA, srcB, S):
        def go(eng, dA=dstA, dB=dstB, sA=srcA, sB=srcB, sh=S.h):
            with eng.If_eq(H["rp"], 0):
                eng.dma_start(dA, sA).then_inc(sh, 16)
            with eng.Else():
                eng.dma_start(dB, sB).then_inc(sh, 16)
        p.q["sync"].ops.append(("__raw__", [go], {}, None, 0))
        S.v += 16
        return S.v

    def publish(stage):
        def go(eng, se=2 * stage, so=2 * stage + 1, sh=FL.h):
            with eng.If_eq(H["rp"], 0):
                eng.dma_start(flags[0:1, se:se + 1], snon[:, 0:1]).then_inc(sh, 16)
            with eng.Else():
                eng.dma_start(flags[0:1, so:so + 1], snon[:, 0:1]).then_inc(sh, 16)
        p.q["sync"].ops.append(("__raw__", [go], {}, None, 0))
        FL.v += 16
        p.sync.wait(FL, FL.v)

    def poll(stage):
        PL = p.sem(f"PL{stage}")
        slot_e = 2 * stage + 1
        slot_o = 2 * stage

        def go(eng):
            with eng.register(f"pv{stage}") as pv, \
                 eng.register(f"ex{stage}") as ex, \
                 eng.register(f"rn{stage}") as rn:
                nv = eng.value_load(snon[:, 0:1])
                eng.reg_mov(rn, nv)
                eng.reg_mov(pv, 0)
                eng.reg_mov(ex, 16)

                def cond():
                    return (eng.snap(pv, min_val=-(1 << 31),
                                     max_val=(1 << 31) - 1)
                            != eng.snap(rn, min_val=-(1 << 31),
                                        max_val=(1 << 31) - 1))
                with eng.While(cond):
                    with eng.If_eq(H["rp"], 0):
                        eng.dma_start(fpoll[:, 0:1],
                                      flags[0:1, slot_e:slot_e + 1]).then_inc(PL.h, 16)
                    with eng.Else():
                        eng.dma_start(fpoll[:, 0:1],
                                      flags[0:1, slot_o:slot_o + 1]).then_inc(PL.h, 16)
                    eng.wait_ge(PL.h, ex)
                    eng.reg_add(ex, ex, 16)
                    v = eng.value_load(fpoll[:, 0:1])
                    eng.reg_mov(pv, v)
        p.q["sync"].ops.append(("__raw__", [go], {}, None, 0))

    def attn_io(xb_ap, stats_buf=None):
        def ywrite(gt, ystage, STR):
            pair_dma(partA[gt * 128:(gt + 1) * 128, :],
                     partB[gt * 128:(gt + 1) * 128, :],
                     ystage[:], ystage[:], STR)
        io = {"xb": xb_ap, "ywrite": ywrite}
        if stats_buf is not None:
            io["stats_buf"] = stats_buf
        return io

    def ffn_io(x_src, final=False):
        def load_xo(dst, t, LD):
            return pair_dma(dst, dst,
                            x_src[t * 128:(t + 1) * 128, :],
                            x_src[2048 + t * 128:2048 + (t + 1) * 128, :], LD)

        def load_part(dst, which, t, LD):
            src = partA if which == 0 else partB
            return pair_dma(dst, dst,
                            src[t * 128:(t + 1) * 128, :],
                            src[2048 + t * 128:2048 + (t + 1) * 128, :], LD)

        def ywrite(t, ystage, STR):
            if final:
                p.sync.dma(y_final[t * 128:(t + 1) * 128, :], ystage[:], STR)
            else:
                pair_dma(state[t * 128:(t + 1) * 128, :],
                         state[2048 + t * 128:2048 + (t + 1) * 128, :],
                         ystage[:], ystage[:], STR)
        return {"load_xo": load_xo, "load_part": load_part, "ywrite": ywrite}

    def scan_io():
        def ywrite(gt, ystage, STR):
            def go(eng, gt=gt, ap=ystage[:], sh=STR.h):
                with eng.If_eq(H["rp"], 0 if gt < 16 else 1):
                    eng.dma_start(state[gt * 128:(gt + 1) * 128, :],
                                  ap).then_inc(sh, 16)
                with eng.Else():
                    eng.dma_start(fpoll[:, 0:1], flags[0:1, 30:31]).then_inc(sh, 16)
            p.q["sync"].ops.append(("__raw__", [go], {}, None, 0))
            STR.v += 16
        return {"xb": state, "pA": partA, "pB": partB,
                "stats_buf": stats_l1a, "ywrite": ywrite}

    p.stage_begin("l0a")
    build_attention(CFG_L0A, p=p, io=attn_io(xb_in))
    publish(1)
    p.stage_end()

    p.stage_begin("f0a")
    p.alias_group = "ffn0"
    poll(1)
    build_ffn(CFG_FFN0A, p=p, io=ffn_io(xb_in))
    publish(2)
    p.stage_end()

    p.stage_begin("l0b")
    poll(2)
    build_attention(CFG_L0B, p=p, io=attn_io(state))
    publish(3)
    p.stage_end()

    p.stage_begin("f0b")
    p.alias_group = "ffn0"
    poll(3)
    build_ffn(CFG_FFN0B, p=p, io=ffn_io(state))
    publish(4)
    p.stage_end()

    p.stage_begin("l1a")
    poll(4)
    build_attention(CFG_L1A, p=p, io=attn_io(state, stats_buf=stats_l1a))
    publish(5)
    p.stage_end()

    p.stage_begin("scn")
    poll(5)
    build_scan(dict(name="scan"), p=p, io=scan_io())
    p.stage_end()

    p.stage_begin("f1a")
    p.alias_group = "ffn1"
    build_ffn(CFG_FFN1A, p=p, io=ffn_io(state))
    publish(7)
    p.stage_end()

    p.stage_begin("l1b")
    poll(7)
    build_attention(CFG_L1B, p=p, io=attn_io(state))
    publish(8)
    p.stage_end()

    p.stage_begin("f1b")
    p.alias_group = "ffn1"
    poll(8)
    build_ffn(CFG_FFN1B, p=p, io=ffn_io(state, final=True))
    p.stage_end()
    return p


def _fused_values(inputs):
    g = lambda k: np.asarray(inputs[k])
    identf = _identf()
    cmask = _cmask()
    iota = _iota8()
    rcos, rsin = _rope_tables()
    V = {}

    def add(group, name, v0, v1=None):
        V[(group, name)] = [v0, v0 if v1 is None else v1]

    Wqkv = g("L0_attn_a_W_qkv")
    bqkv = g("L0_attn_a_b_qkv")
    sls = [slice(0, 512), slice(512, 1024)]

    def two(fn):
        return [fn(sls[0]), fn(sls[1])]

    a, b = two(lambda sl: np.ascontiguousarray(Wqkv[:, 0:1024][:, sl]))
    add("l0a_", "wq", a, b)
    a, b = two(lambda sl: np.ascontiguousarray(
        np.hstack([Wqkv[:, 1024:2048][:, sl], Wqkv[:, 2048:3072][:, sl]])))
    add("l0a_", "wkv", a, b)
    a, b = two(lambda sl: np.ascontiguousarray(g("L0_attn_a_W_proj")[sl, :]))
    add("l0a_", "wp", a, b)
    a, b = two(lambda sl: np.ascontiguousarray(bqkv[0:1024][sl])[None])
    add("l0a_", "bq", a, b)
    a, b = two(lambda sl: np.ascontiguousarray(
        np.hstack([bqkv[1024:2048][sl], bqkv[2048:3072][sl]]))[None])
    add("l0a_", "bkv", a, b)
    add("l0a_", "gvec", _colvec(g("L0_attn_a_ln_gamma")))
    add("l0a_", "bvec", _colvec(g("L0_attn_a_ln_beta")))
    add("l0a_", "identf", identf)

    a, b = two(lambda sl: np.ascontiguousarray(g("L0_attn_b_Wq")[:, sl]))
    add("l0b_", "wq", a, b)
    a, b = two(lambda sl: np.ascontiguousarray(
        np.hstack([g("L0_attn_b_Wk")[:, sl], g("L0_attn_b_Wv")[:, sl]])))
    add("l0b_", "wkv", a, b)
    a, b = two(lambda sl: np.ascontiguousarray(g("L0_attn_b_Wo")[sl, :]))
    add("l0b_", "wp", a, b)
    a, b = two(lambda sl: np.ascontiguousarray(
        g("L0_attn_b_bq").astype(np.float32)[sl])[None])
    add("l0b_", "bq", a, b)
    a, b = two(lambda sl: np.ascontiguousarray(np.hstack(
        [g("L0_attn_b_bk").astype(np.float32)[sl],
         g("L0_attn_b_bv").astype(np.float32)[sl]]))[None])
    add("l0b_", "bkv", a, b)
    add("l0b_", "rcos", rcos)
    add("l0b_", "rsin", rsin)
    add("l0b_", "cmask", cmask)
    add("l0b_", "identf", identf)

    qkvT = np.ascontiguousarray(g("L1_attn_a_qkv_w").T)
    a, b = two(lambda sl: np.ascontiguousarray(qkvT[:, 0:1024][:, sl]))
    add("l1a_", "wq", a, b)
    a, b = two(lambda sl: np.ascontiguousarray(
        np.hstack([qkvT[:, 1024:2048][:, sl], qkvT[:, 2048:3072][:, sl]])))
    add("l1a_", "wkv", a, b)
    a, b = two(lambda sl: np.ascontiguousarray(g("L1_attn_a_o_w").T[sl, :]))
    add("l1a_", "wp", a, b)
    add("l1a_", "gvec", _colvec(g("L1_attn_a_norm1_w")))
    add("l1a_", "cmask", cmask)
    add("l1a_", "identf", identf)

    a, b = two(lambda sl: np.ascontiguousarray(g("L1_attn_b_Wq").T[:, sl]))
    add("l1b_", "wq", a, b)
    a, b = two(lambda sl: np.ascontiguousarray(
        np.hstack([g("L1_attn_b_Wk").T[:, sl], g("L1_attn_b_Wv").T[:, sl]])))
    add("l1b_", "wkv", a, b)
    a, b = two(lambda sl: np.ascontiguousarray(g("L1_attn_b_Wo").T[sl, :]))
    add("l1b_", "wp", a, b)
    add("l1b_", "cmask", cmask)
    add("l1b_", "identf", identf)

    wupT = g("L0_ffn_w_up").transpose(0, 2, 1)
    wdnT = g("L0_ffn_w_down").transpose(0, 2, 1)
    wu = np.ascontiguousarray(np.concatenate(
        [np.concatenate([wupT[e][:, h * 1024:(h + 1) * 1024],
                         wupT[e][:, 2048 + h * 1024:2048 + (h + 1) * 1024]],
                        axis=1)
         for e in range(8) for h in range(2)], axis=0))
    wd = np.ascontiguousarray(np.concatenate(
        [wdnT[e][h * 1024:(h + 1) * 1024, :]
         for e in range(8) for h in range(2)], axis=0))
    add("ffn0", "wrt", np.ascontiguousarray(g("L0_ffn_w_router").T))
    add("ffn0", "wup", wu)
    add("ffn0", "wdn", wd)
    add("ffn0", "gvec", _colvec(g("L0_ffn_norm_scale")))
    add("ffn0", "identf", identf)
    add("ffn0", "iota", iota)
    add("f0a_", "bcmb", np.ascontiguousarray(g("L0_attn_a_b_proj"))[None])
    add("f0b_", "bcmb", np.ascontiguousarray(
        g("L0_attn_b_bo").astype(np.float32))[None])

    ec, est, gmat = _scan_consts()
    add("scn_", "wzg", np.ascontiguousarray(
        np.hstack([g("L1_attn_a_in_w").T, g("L1_attn_a_gate_w").T])))
    add("scn_", "wdt", np.ascontiguousarray(
        np.hstack([g("L1_attn_a_dt_w").T, g("L1_attn_a_Bp_w").T,
                   g("L1_attn_a_Cp_w").T])))
    add("scn_", "dtb", np.ascontiguousarray(np.concatenate(
        [g("L1_attn_a_dt_b"), np.zeros(32, np.float32)]))[None])
    add("scn_", "alog", np.ascontiguousarray(
        g("L1_attn_a_A_log").reshape(-1).reshape(8, 128).T))
    add("scn_", "ec", ec)
    add("scn_", "est", est)
    add("scn_", "gmat", gmat)
    add("scn_", "wout", np.ascontiguousarray(g("L1_attn_a_out_w").T))
    add("scn_", "gvec", _colvec(g("L1_attn_a_norm2_w")))
    add("scn_", "identf", identf)

    rw1 = g("L1_ffn_rw1")
    rw2 = g("L1_ffn_rw2")
    wu = np.ascontiguousarray(np.concatenate(
        [rw1[e][:, h * 1024:(h + 1) * 1024]
         for e in range(8) for h in range(2)], axis=0))
    wd = np.ascontiguousarray(np.concatenate(
        [rw2[e][h * 1024:(h + 1) * 1024, :]
         for e in range(8) for h in range(2)], axis=0))
    sw1 = g("L1_ffn_sw1")
    add("ffn1", "wrt", np.ascontiguousarray(g("L1_ffn_router_w")))
    add("ffn1", "brt", np.ascontiguousarray(g("L1_ffn_router_b"))[None])
    add("ffn1", "wup", wu)
    add("ffn1", "wdn", wd)
    add("ffn1", "wsu", np.ascontiguousarray(np.concatenate(
        [sw1[:, 0:1024], sw1[:, 1024:2048]], axis=0)))
    add("ffn1", "wsd", np.ascontiguousarray(g("L1_ffn_sw2")))
    add("ffn1", "bs1", np.ascontiguousarray(g("L1_ffn_sb1").reshape(2, 1024)))
    add("ffn1", "bs2", np.ascontiguousarray(g("L1_ffn_sb2"))[None])
    add("ffn1", "rb1", np.ascontiguousarray(g("L1_ffn_rb1").reshape(16, 1024)))
    add("ffn1", "rb2", np.ascontiguousarray(g("L1_ffn_rb2")))
    add("ffn1", "identf", identf)
    add("ffn1", "iota", iota)
    return V


_FUSED_CACHE = {}


def kernel_fused(**inputs):
    import secrets
    x = np.asarray(inputs["L0_attn_a_x"]).astype(np.float32)
    state0 = np.ascontiguousarray(x.reshape(4, T, D))
    if "fused" not in _FUSED_CACHE:
        pb = build_fused()
        nc = pb.emit_and_finish()
        _FUSED_CACHE["fused"] = (nc, {dt: (w.tensor.name, spec,
                                           int(w.tensor.shape[1]))
                                      for dt, (w, spec, pos) in
                                      pb.walls.items()})
    nc, wallinfo = _FUSED_CACHE["fused"]
    V = _fused_values(inputs)
    np_dt = {F32: np.float32, FP16: np.float16}
    walls = {}
    for dt, (wname, spec, ndecl) in wallinfo.items():
        arrs = [np.zeros((1, ndecl), dtype=np_dt[dt]) for _ in range(2)]
        for (group, name, shape, off) in spec:
            v = V[(group, name)]
            n = int(np.prod(shape))
            for pi in range(2):
                arrs[pi][0, off:off + n] = np.asarray(v[pi], np_dt[dt]).ravel()
        walls[wname] = arrs
    nonce = np.array([[secrets.randbits(31) | 1, 0]], dtype=np.int32)
    in_maps = []
    for c in range(8):
        m = {wname: arrs[c % 2] for wname, arrs in walls.items()}
        m["xb"] = np.ascontiguousarray(state0[c // 2])
        m["nonce"] = nonce
        in_maps.append(m)
    res = run_bass_kernel_spmd(nc, in_maps, core_ids=list(range(8)))
    out = np.empty((4, T, D), dtype=np.float32)
    for c in range(8):
        b, half = c // 2, c % 2
        out[b][half * 2048:(half + 1) * 2048] = res.results[c]["y"]
    return out


FUSED = True


def kernel(**inputs):
    if FUSED:
        return kernel_fused(**inputs)
    return kernel_unfused(**inputs)


# version 10
# speedup vs baseline: 2.0313x; 2.0313x over previous
import math
import numpy as np
import ml_dtypes
import concourse.bass as bass
import concourse.mybir as mybir
from concourse.bass_utils import run_bass_kernel_spmd
from contextlib import ExitStack

F32 = mybir.dt.float32
BF16 = mybir.dt.bfloat16
FP16 = mybir.dt.float16
AF = mybir.ActivationFunctionType
ALU = mybir.AluOpType
AXX = mybir.AxisListType.X

D = 1024
T = 4096
NT_FULL = 32
NCH = 8
BF = ml_dtypes.bfloat16
EPS_LN = 1e-5
NEG = -30000.0

ENGS = ("sync", "scalar", "vector", "tensor", "gpsimd")


def own_qtiles(core):
    if core % 2 == 0:
        return list(range(0, 8)) + list(range(24, 32))
    return list(range(8, 24))


def own_row_slices(core):
    if core % 2 == 0:
        return [(0, 1024), (3072, 4096)]
    return [(1024, 3072)]


class Sem:
    __slots__ = ("h", "v", "name")

    def __init__(self, nc, name):
        self.h = nc.alloc_semaphore(name)
        self.v = 0
        self.name = name


class Lazy:
    __slots__ = ("fn",)

    def __init__(self, fn):
        self.fn = fn

    def __call__(self):
        return self.fn()


class EngQ:
    def __init__(self, auto_sem=None):
        self.ops = []
        self.auto_sem = auto_sem

    def call(self, method, *args, inc=None, incn=1, **kwargs):
        if inc is None and self.auto_sem is not None and method != "dma_start":
            inc = self.auto_sem
        if inc is not None:
            inc.v += incn
        self.ops.append((method, args, kwargs, inc, incn))
        if self.auto_sem is not None and inc is not None and method != "dma_start":
            self.ops.append(("wait_ge", (inc.h, inc.v), {}, None, 0))
        return inc.v if inc is not None else None

    def wait(self, sem, val=None):
        v = sem.v if val is None else val
        self.ops.append(("wait_ge", (sem.h, v), {}, None, 0))

    def dma(self, dst, src, sem, **kw):
        return self.call("dma_start", dst, src, inc=sem, incn=16, **kw)


class Prog:
    def __init__(self, name):
        self.nc = bass.Bass("TRN2", target_bir_lowering=False)
        self.ctx = ExitStack()
        self.name = name
        self.sbuf_bytes = 0
        self.prefix = ""
        self.q = {e: EngQ() for e in ENGS}
        self.q["vector"].auto_sem = Sem(self.nc, "VEauto")
        self.q["scalar"].auto_sem = Sem(self.nc, "SCauto")
        self.gsem = None

    def init_barrier_sems(self):
        self.gsem = {e: Sem(self.nc, f"G_{e}") for e in ENGS}

    def stage_begin(self, prefix):
        self.prefix = prefix + "_"
        self.alias_group = None
        self._stage_ctx = ExitStack()
        self._saved_bytes = self.sbuf_bytes
        self._saved_ctx, self.ctx = self.ctx, self._stage_ctx
        if self.gsem is not None:
            for e in ENGS:
                for e2 in ENGS:
                    self.q[e].wait(self.gsem[e2])

    def stage_end(self):
        if self.gsem is not None:
            for e in ENGS:
                self.q[e].call("nop", inc=self.gsem[e])
        self.ctx = self._saved_ctx
        self._stage_ctx.close()
        self.sbuf_bytes = self._saved_bytes
        self.prefix = ""

    def __getattr__(self, item):
        if item in ENGS:
            return self.q[item]
        raise AttributeError(item)

    def sem(self, name):
        if not hasattr(self, "_semcache"):
            self._semcache = {}
        if name not in self._semcache:
            self._semcache[name] = Sem(self.nc, name)
        return self._semcache[name]

    def sbuf(self, name, shape, dtype):
        self.sbuf_bytes += int(np.prod(shape[1:])) * mybir.dt.size(dtype)
        assert self.sbuf_bytes <= 206000, \
            f"{self.name}: SBUF over budget at {name}: {self.sbuf_bytes // 1024}KB/part"
        return self.ctx.enter_context(
            self.nc.sbuf_tensor(self.prefix + name, list(shape), dtype))

    def psum(self, name, shape, dtype):
        return self.ctx.enter_context(
            self.nc.psum_tensor(self.prefix + name, list(shape), dtype))

    def din(self, name, shape, dtype, no_alias=False):
        if getattr(self, "walls", None) is None:
            return self.nc.dram_tensor(self.prefix + name, list(shape), dtype,
                                       kind="ExternalInput").ap()
        group = (getattr(self, "alias_group", None) or self.prefix) \
            if not no_alias else self.prefix
        key = (group, name)
        if key in self.wall_cache:
            return self.wall_cache[key]
        wall, spec, pos = self.walls[dtype]
        n = int(np.prod(shape))
        off = pos[0]
        spec.append((group, name, tuple(shape), off))
        pos[0] = off + n
        ap = wall[0, off:off + n]
        if len(shape) == 2:
            ap = ap.rearrange("(r c) -> r c", c=shape[1])
        self.wall_cache[key] = ap
        return ap

    def dint(self, name, shape, dtype):
        if getattr(self, "walls", None) is None:
            return self.nc.dram_tensor(self.prefix + name, list(shape), dtype,
                                       kind="Internal").ap()
        wall, spec, pos = self.iwall[dtype]
        n = int(np.prod(shape))
        off = pos[0]
        pos[0] = off + n
        ap = wall[0, off:off + n]
        if len(shape) == 2:
            ap = ap.rearrange("(r c) -> r c", c=shape[1])
        return ap

    def init_walls(self, sizes):
        self.walls = {}
        self.iwall = {}
        self.wall_cache = {}
        self.alias_group = None
        for dt_, n in sizes.items():
            if isinstance(dt_, tuple):
                t = self.nc.dram_tensor(f"iwall_{mybir.dt.size(dt_[1])}",
                                        [1, n], dt_[1], kind="Internal").ap()
                self.iwall[dt_[1]] = (t, [], [0])
            else:
                t = self.nc.dram_tensor(f"wall_{mybir.dt.size(dt_)}{dt_}",
                                        [1, n], dt_, kind="ExternalInput").ap()
                self.walls[dt_] = (t, [], [0])

    def emit_and_finish(self):
        def replay(eng, ops):
            for method, args, kwargs, sem, incn in ops:
                if method == "__raw__":
                    args[0](eng)
                    continue
                args = [a() if isinstance(a, Lazy) else a for a in args]
                kwargs = {k: (v() if isinstance(v, Lazy) else v)
                          for k, v in kwargs.items()}
                ins = getattr(eng, method)(*args, **kwargs)
                if sem is not None:
                    ins.then_inc(sem.h, incn)

        with self.nc.Block() as blk:
            for e in ENGS:
                if self.q[e].ops:
                    getattr(blk, e)(lambda eng, ops=self.q[e].ops: replay(eng, ops))
        self.ctx.close()
        self.nc.finalize()
        return self.nc



def build_attention(cfg, p=None, io=None):
    H, dh = cfg["H"], cfg["dh"]
    Hm = H // 2
    P2 = 2 if dh == 64 else 1
    NP2 = Hm // P2
    VW = P2 * dh + (2 if P2 == 2 else 1)
    causal = cfg["causal"]
    rope = cfg.get("rope", False)
    eps2 = cfg.get("eps2")
    standalone = p is None
    if standalone:
        p = Prog(cfg["name"])
    nc = p.nc
    wdt = FP16 if cfg.get("w_f16") else F32
    scale = 1.0 / math.sqrt(dh)
    has_bias = cfg.get("beta_qkv") or cfg.get("bias_rows")
    DKV = D
    DQ = D // 2
    DP = D // 2

    xb = io["xb"] if io else p.din("xb", [T, D], F32)
    wkv_d = p.din("wkv", [D, DKV], wdt)
    wq_d = p.din("wq", [D, DQ], wdt)
    wp_d = p.din("wp", [DP, D], wdt)
    identf = p.din("identf", [128, 128], F32)
    gvec = p.din("gvec", [128, 8], F32) if cfg.get("gamma_fold") else None
    bvec = p.din("bvec", [128, 8], F32) if cfg.get("beta_qkv") else None
    bq_d = bkv_d = None
    if has_bias:
        bq_d = p.din("bq", [1, DQ], F32)
        bkv_d = p.din("bkv", [1, DKV], F32)
    cmask = p.din("cmask", [128, 128], F32) if causal else None
    if rope:
        rcos_d = p.din("rcos", [T, DQ // 2], F32)
        rsin_d = p.din("rsin", [T, DQ // 2], F32)
    if standalone:
        y_out = nc.dram_tensor("y", [T, D], F32, kind="ExternalOutput").ap()
        st_out = nc.dram_tensor("st_out", [128, NT_FULL * 6], F32,
                                kind="ExternalOutput").ap()
    kscr = p.dint("kscr", [128, NP2 * T], BF16)

    stats = (io.get("stats_buf") if io else None) or \
        p.sbuf("stats", [128, NT_FULL * 6], F32)
    xstage = p.sbuf("xstage", [128, 2 * D], F32)
    xnb = p.sbuf("xnb", [128, D], BF16)
    xnT = p.sbuf("xnT", [128, D], BF16)
    qkvtm = p.sbuf("qkvtm", [128, DKV + DQ], BF16)
    ktT = p.sbuf("ktT", [128, DP], BF16)
    Vb = p.sbuf("Vb", [128, NT_FULL * NP2 * VW], BF16)
    Qf = p.sbuf("Qf", [128, NP2 * T], BF16)
    kblk = p.sbuf("kblk", [128, T], BF16)
    Wres = p.sbuf("Wres", [128, NCH * (DKV + DQ)], BF16)
    Wpj = p.sbuf("Wpj", [128, (DP // 128) * D], BF16)
    Wstage = p.sbuf("Wstage", [128, 2 * 512], wdt)
    PTb = p.sbuf("PTb", [128, 4 * 512], BF16)
    ctmp = p.sbuf("ctmp", [128, 2 * 512], F32)
    ctxTM = p.sbuf("ctxTM", [128, 4 * (Hm * dh)], BF16)
    ctxFT = p.sbuf("ctxFT", [128, Hm * dh], BF16)
    ystage = p.sbuf("ystage", [128, D], F32)
    sid = p.sbuf("sid", [128, 128], F32)
    sidb = p.sbuf("sidb", [128, 128], BF16)
    onesf = p.sbuf("onesf", [1, 128], F32)
    biasq = p.sbuf("biasq", [1, DQ], F32)
    biaskv = p.sbuf("biaskv", [1, DKV], F32)
    gcol = p.sbuf("gcol", [128, 8], F32)
    bcolf = p.sbuf("bcolf", [128, 8], F32)
    bcol = p.sbuf("bcol", [128, 8], BF16)
    kn2 = p.sbuf("kn2", [128, Hm], F32)
    qn2 = p.sbuf("qn2", [128, Hm], F32)
    nrm = p.sbuf("nrm", [128, 16], F32)
    Mb = p.sbuf("Mb", [128, Hm], F32)
    mrow = p.sbuf("mrow", [1, 16], F32)
    smask = p.sbuf("smask", [128, 128], F32) if causal else None
    rc = p.sbuf("rc", [128, DQ // 2], F32) if rope else None
    rs = p.sbuf("rs", [128, DQ // 2], F32) if rope else None
    qkr = p.sbuf("qkr", [128, DKV + DQ], BF16) if rope else None

    ps_g = [p.psum(f"psg{i}", [128, 512], F32) for i in range(4)]
    ps_ctx = [p.psum(f"psc{i}", [128, 512], F32) for i in range(2)]
    ps_tr = p.psum("ps_tr", [128, 1024], BF16)
    ps_trb = p.psum("ps_trb", [128, 1024], BF16)

    LD = p.sem("LD")
    WL = p.sem("WL")
    WLb = p.sem("WLb")
    KSM = p.sem("KSM")
    GP = p.sem("GP")
    SC = p.sem("SC")
    VE = p.sem("VE")
    TT = p.sem("TT")
    STR = p.sem("STR")

    if eps2 is None:
        va, vb_ = 1.0, EPS_LN
    else:
        va, vb_ = 1.0 + eps2, EPS_LN * eps2

    p.sync.dma(sid[:], identf, LD)
    if causal:
        p.sync.dma(smask[:], cmask, LD)
    if cfg.get("gamma_fold"):
        p.sync.dma(gcol[:], gvec, LD)
    if has_bias:
        p.sync.dma(biasq[:, :], bq_d, LD)
        p.sync.dma(biaskv[:, :], bkv_d, LD)
    ld_consts = LD.v
    p.vector.wait(LD, ld_consts)
    p.vector.call("tensor_copy", sidb[:], sid[:], inc=VE)

    p.gpsimd.call("memset", onesf[:], 1.0)
    p.gpsimd.call("memset", Vb[:], 1.0)
    p.gpsimd.call("memset", kn2[:], 0.0)
    p.gpsimd.call("memset", qn2[:], 0.0, inc=GP)
    CB = p.sem("CB")
    if cfg.get("beta_qkv"):
        p.gpsimd.dma(bcolf[:], bvec, CB)
        p.vector.wait(CB, CB.v)
        p.vector.wait(LD, ld_consts)
        p.vector.call("reciprocal", nrm[:, 0:8], gcol[:], inc=VE)
        p.vector.wait(VE, VE.v)
        p.vector.call("tensor_mul", bcol[:], bcolf[:], nrm[:, 0:8], inc=VE)
    cb_consts = CB.v
    bcol_v = VE.v

    CW = DKV + DQ
    wl_cast_vals = []
    slot = [0]

    def stream_weights(wd, nchunks, dout, dst_buf, dst_cw, roff, fold):
        for c in range(nchunks):
            for s in range(dout // 512):
                sl0 = (slot[0] % 2) * 512
                wsem = WL if slot[0] % 2 == 0 else WLb
                if slot[0] >= 2:
                    p.sync.wait(VE, wl_cast_vals[slot[0] - 2])
                p.sync.dma(Wstage[:, sl0:sl0 + 512],
                           wd[c * 128:(c + 1) * 128, s * 512:(s + 1) * 512], wsem)
                p.vector.wait(wsem)
                dst = dst_buf[:, c * dst_cw + roff + s * 512:
                              c * dst_cw + roff + (s + 1) * 512]
                if fold:
                    p.vector.wait(LD, ld_consts)
                    p.vector.call("tensor_scalar", dst, Wstage[:, sl0:sl0 + 512],
                                  gcol[:, c:c + 1], None, ALU.mult, inc=VE)
                else:
                    p.vector.call("tensor_copy", dst, Wstage[:, sl0:sl0 + 512],
                                  inc=VE)
                wl_cast_vals.append(VE.v)
                slot[0] += 1

    fold_qkv = bool(cfg.get("gamma_fold"))
    stream_weights(wkv_d, NCH, DKV, Wres, CW, 0, fold_qkv)
    stream_weights(wq_d, NCH, DQ, Wres, CW, DKV, fold_qkv)
    stream_weights(wp_d, DP // 128, D, Wpj, D, 0, False)
    w_ready = VE.v

    if cfg.get("beta_qkv"):
        p.tensor.wait(VE, max(w_ready, bcol_v))
        p.tensor.wait(GP, GP.v)
        p.tensor.wait(CB, cb_consts)
        for (brow, roff, dout) in ((biaskv, 0, DKV), (biasq, DKV, DQ)):
            for s in range(dout // 512):
                p.tensor.wait(VE, VE.v)
                for c in range(NCH):
                    p.tensor.call("matmul", ps_ctx[0][0:1, 0:512], bcol[:, c:c + 1],
                                  Wres[:, c * CW + roff + s * 512:
                                       c * CW + roff + (s + 1) * 512],
                                  start=(c == 0), stop=(c == NCH - 1),
                                  inc=(TT if c == NCH - 1 else None))
                p.vector.wait(TT)
                p.vector.wait(LD, ld_consts)
                p.vector.call("tensor_add", brow[:, s * 512:(s + 1) * 512],
                              brow[:, s * 512:(s + 1) * 512],
                              ps_ctx[0][0:1, 0:512], inc=VE)
    bias_v = VE.v

    NS = (DKV + DQ) // 512
    tile_consumed = [0] * NT_FULL
    p.tensor.wait(VE, max(w_ready, bias_v))
    p.tensor.wait(GP, GP.v)
    for t in range(NT_FULL):
        gt = t
        if t >= 1:
            p.sync.wait(VE, tile_consumed[t - 1])
        x_ld = p.sync.dma(xstage[:, (t % 2) * D:(t % 2 + 1) * D],
                          xb[gt * 128:(gt + 1) * 128, :], LD)
        if rope:
            p.sync.dma(rc[:], rcos_d[gt * 128:(gt + 1) * 128, :], LD)
            p.sync.dma(rs[:], rsin_d[gt * 128:(gt + 1) * 128, :], LD)
            x_ld = LD.v
        st6 = stats[:, gt * 6:(gt + 1) * 6]
        mu, s_ap, r1 = st6[:, 0:1], st6[:, 1:2], st6[:, 2:3]
        w_ap, a_ap, b_ap = st6[:, 3:4], st6[:, 4:5], st6[:, 5:6]
        xt = xstage[:, (t % 2) * D:(t % 2 + 1) * D]
        p.scalar.wait(LD, x_ld)
        p.scalar.wait(TT, TT.v)
        p.scalar.call("activation", xnb[:], xt, AF.Square, accum_out=b_ap,
                      inc=SC)
        sq_v = SC.v
        p.vector.wait(LD, x_ld)
        p.vector.call("reduce_sum", a_ap, xt, axis=AXX, inc=VE)
        p.vector.wait(VE)
        p.vector.call("tensor_scalar", mu, a_ap, 1.0 / D, None, ALU.mult)
        p.vector.wait(SC, sq_v)
        p.vector.call("tensor_scalar", w_ap, b_ap, 1.0 / D, None, ALU.mult)
        p.vector.call("tensor_scalar", a_ap, mu, mu, None, ALU.mult)
        p.vector.call("tensor_sub", w_ap, w_ap, a_ap, inc=VE)
        p.vector.wait(VE)
        p.vector.call("tensor_scalar", b_ap, w_ap, va, vb_, ALU.mult, ALU.add,
                      inc=VE)
        p.scalar.wait(VE, VE.v)
        p.scalar.call("activation", a_ap, b_ap, AF.Ln)
        p.scalar.call("activation", s_ap, a_ap, AF.Exp, scale=-0.5, inc=SC)
        p.vector.wait(SC, SC.v)
        p.vector.call("tensor_scalar", b_ap, w_ap, 1.0, EPS_LN,
                      ALU.mult, ALU.add, inc=VE)
        p.scalar.wait(VE, VE.v)
        p.scalar.call("activation", a_ap, b_ap, AF.Ln)
        p.scalar.call("activation", r1, a_ap, AF.Exp, scale=-0.5, inc=SC)
        s_v = SC.v
        p.vector.wait(SC, s_v)
        p.vector.call("tensor_scalar", xnb[:], xt, mu, s_ap,
                      ALU.subtract, ALU.mult, inc=VE)
        xn_v = VE.v
        p.tensor.wait(VE, xn_v)
        p.tensor.wait(SC, SC.v)
        for c in range(NCH):
            p.tensor.call("transpose", ps_tr[:, c * 128:(c + 1) * 128],
                          xnb[:, c * 128:(c + 1) * 128], sidb[0:128, 0:128],
                          inc=(TT if c == NCH - 1 else None))
        p.scalar.wait(TT)
        p.scalar.call("copy", xnT[:], ps_tr[:], inc=SC)
        ev_v = SC.v
        p.tensor.wait(SC, ev_v)
        for s in range(NS):
            if has_bias:
                p.tensor.wait(LD, ld_consts)
                bsrc = biaskv[:, s * 512:(s + 1) * 512] if s < 2 else \
                    biasq[:, 0:512]
                p.tensor.call("matmul", ps_g[s][:], onesf[:, 0:128], bsrc,
                              start=True, stop=False)
            for c in range(NCH):
                p.tensor.call("matmul", ps_g[s][:],
                              xnT[:, c * 128:(c + 1) * 128],
                              Wres[:, c * CW + s * 512:c * CW + (s + 1) * 512],
                              start=(c == 0 and not has_bias),
                              stop=(c == NCH - 1),
                              inc=(TT if c == NCH - 1 else None))
        qkv_done = TT.v
        p.vector.wait(TT, qkv_done)
        p.vector.call("tensor_copy", qkvtm[:, 0:512], ps_g[0][:], inc=VE)
        p.vector.call("tensor_scalar", qkvtm[:, 1024:1536], ps_g[2][:],
                      scale, None, ALU.mult, inc=VE)
        k_ev = VE.v
        vview = Vb[:].rearrange("p (t n w) -> p t n w", t=NT_FULL, n=NP2)
        if P2 == 2:
            dst3 = Vb[:].rearrange("p (t n o w) -> p t n o w",
                                   t=NT_FULL, n=NP2, o=2)[:, gt, :, :, 0:dh]
            srcv = ps_g[1][:].rearrange("p (n o w) -> p n o w", n=NP2, o=2)
        else:
            dst3 = vview[:, gt, :, 0:dh]
            srcv = ps_g[1][:].rearrange("p (n w) -> p n w", n=NP2)
        p.scalar.wait(TT, qkv_done)
        p.scalar.call("copy", dst3, srcv, inc=SC)
        ksrc_off = 0
        qsrc_off = 1024
        src_buf = qkvtm
        if rope:
            p.vector.wait(VE, k_ev)
            for (off, dst_off) in ((0, 0), (1024, 1024)):
                k3 = qkvtm[:, off:off + 512].rearrange("p (h d) -> p h d", h=Hm)
                kd3 = qkr[:, dst_off:dst_off + 512].rearrange("p (h d) -> p h d", h=Hm)
                c3 = rc[:].rearrange("p (h d) -> p h d", h=Hm)
                s3 = rs[:].rearrange("p (h d) -> p h d", h=Hm)
                t1 = ctmp[:, 0:256].rearrange("p (h d) -> p h d", h=Hm)
                t2 = ctmp[:, 256:512].rearrange("p (h d) -> p h d", h=Hm)
                p.vector.call("tensor_mul", t1, c3, k3[:, :, 0:dh // 2])
                p.vector.call("tensor_mul", t2, s3, k3[:, :, dh // 2:dh])
                p.vector.call("tensor_sub", kd3[:, :, 0:dh // 2], t1, t2)
                p.vector.call("tensor_mul", t1, s3, k3[:, :, 0:dh // 2])
                p.vector.call("tensor_mul", t2, c3, k3[:, :, dh // 2:dh])
                p.vector.call("tensor_add", kd3[:, :, dh // 2:dh], t1, t2,
                              inc=VE)
            src_buf = qkr
        for (off, acc) in ((ksrc_off, kn2), (qsrc_off, qn2)):
            sl = src_buf[:, off:off + 512]
            p.scalar.wait(VE, VE.v)
            p.scalar.call("activation", ctmp[:, 0:512], sl, AF.Square, inc=SC)
            p.vector.wait(SC, SC.v)
            p.vector.call("reduce_sum", nrm[:, 0:Hm],
                          ctmp[:, 0:512].rearrange("p (a b) -> p a b", b=dh),
                          axis=AXX, inc=VE)
            p.vector.wait(VE)
            p.vector.call("tensor_max", acc[:, 0:Hm], acc[:, 0:Hm],
                          nrm[:, 0:Hm], inc=VE)
        p.tensor.wait(VE, VE.v)
        p.tensor.wait(KSM, KSM.v)
        for c in range(4):
            p.tensor.call("transpose", ps_trb[:, c * 128:(c + 1) * 128],
                          src_buf[:, ksrc_off + c * 128:ksrc_off + (c + 1) * 128],
                          sidb[0:128, 0:128],
                          inc=(TT if c == 3 else None))
        p.scalar.wait(TT)
        p.scalar.wait(KSM, KSM.v)
        p.scalar.call("copy", ktT[:], ps_trb[:, 0:512], inc=SC)
        kev = SC.v
        p.gpsimd.wait(SC, kev)
        p.gpsimd.dma(kscr.rearrange("p (c t) -> p c t", c=NP2)
                     [:, :, gt * 128:(gt + 1) * 128],
                     ktT[:].rearrange("p (c t) -> p c t", c=NP2), KSM)
        p.tensor.wait(SC, SC.v)
        for c in range(4):
            p.tensor.call("transpose", ps_trb[:, c * 128:(c + 1) * 128],
                          src_buf[:, qsrc_off + c * 128:qsrc_off + (c + 1) * 128],
                          sidb[0:128, 0:128],
                          inc=(TT if c == 3 else None))
        p.scalar.wait(TT)
        p.scalar.call("copy",
                      Qf[:].rearrange("p (c t) -> p c t", c=NP2)
                      [:, :, gt * 128:(gt + 1) * 128],
                      ps_trb[:, 0:512].rearrange("p (c t) -> p c t", c=NP2),
                      inc=SC)
        tile_consumed[t] = VE.v

    p.tensor.wait(VE, VE.v)
    p.tensor.call("transpose", ps_ctx[0][0:Hm, 0:128], kn2[:, 0:Hm],
                  sid[0:128, 0:128])
    p.tensor.call("transpose", ps_ctx[1][0:Hm, 0:128], qn2[:, 0:Hm],
                  sid[0:128, 0:128], inc=TT)
    p.vector.wait(TT)
    p.vector.call("reduce_max", nrm[0:Hm, 0:1], ps_ctx[0][0:Hm, 0:128],
                  axis=AXX)
    p.vector.call("reduce_max", nrm[0:Hm, 1:2], ps_ctx[1][0:Hm, 0:128],
                  axis=AXX, inc=VE)
    p.vector.wait(VE)
    p.vector.call("tensor_mul", nrm[0:Hm, 2:3], nrm[0:Hm, 0:1],
                  nrm[0:Hm, 1:2], inc=VE)
    p.scalar.wait(VE, VE.v)
    p.scalar.call("activation", nrm[0:Hm, 3:4], nrm[0:Hm, 2:3], AF.Sqrt,
                  inc=SC)
    p.vector.wait(SC, SC.v)
    p.vector.call("tensor_scalar", nrm[0:Hm, 4:5], nrm[0:Hm, 3:4], -1.0, None,
                  ALU.mult, inc=VE)
    p.tensor.wait(VE, VE.v)
    p.tensor.call("transpose", ps_ctx[0][0:1, 0:Hm], nrm[0:Hm, 4:5],
                  sid[0:Hm, 0:Hm], inc=TT)
    p.scalar.wait(TT)
    p.scalar.call("copy", mrow[:, 0:Hm], ps_ctx[0][0:1, 0:Hm], inc=SC)
    p.tensor.wait(SC, SC.v)
    p.tensor.call("matmul", ps_ctx[1][:, 0:Hm], onesf[:, 0:128], mrow[:, 0:Hm],
                  start=True, stop=True, inc=TT)
    p.scalar.wait(TT)
    p.scalar.call("copy", Mb[:, 0:Hm], ps_ctx[1][:, 0:Hm], inc=SC)
    bounds_done = SC.v
    if standalone:
        p.sync.wait(SC, SC.v)
        p.sync.wait(VE, VE.v)
        p.sync.dma(st_out, stats[:], STR)

    for qq in range(8):
        tiles = [qq * 4 + i for i in range(4)]
        ktmax = (max(tiles) + 1) if causal else NT_FULL
        for hp in range(NP2):
            p.gpsimd.wait(TT, TT.v)
            p.gpsimd.wait(KSM, KSM.v)
            p.gpsimd.dma(kblk[:], kscr[:, hp * T:(hp + 1) * T], KSM)
            kblk_v = KSM.v
            p.tensor.wait(KSM, kblk_v)
            p.tensor.wait(SC, SC.v)
            p.tensor.wait(VE, VE.v)
            exp_done = {}
            vv = Vb[:].rearrange("p (t n w) -> p t n w", t=NT_FULL, n=NP2)

            def emit_av(kt, h01, first, last):
                ppk = kt % 2
                pt_src = PTb[:, (ppk * 2 + h01) * 512:
                             (ppk * 2 + h01 + 1) * 512]
                p.tensor.wait(SC, exp_done[(kt, h01)])
                if P2 == 2:
                    p.tensor.call("matmul", ps_ctx[h01][0:65, :],
                                  vv[:, kt, hp, h01 * 65:(h01 + 1) * 65],
                                  pt_src, start=first, stop=last,
                                  inc=(TT if last and h01 == P2 - 1 else None))
                else:
                    p.tensor.call("matmul", ps_ctx[0][0:64, :],
                                  vv[:, kt, hp, 0:64], pt_src,
                                  start=first, stop=last)
                    p.tensor.call("matmul", ps_ctx[1][0:65, :],
                                  vv[:, kt, hp, 64:129], pt_src,
                                  start=first, stop=last,
                                  inc=(TT if last else None))

            for kt in range(ktmax):
                ppk = kt % 2
                st_vs = {}
                for h01 in range(P2):
                    stp = ps_g[ppk * 2 + h01]
                    if P2 == 2:
                        lhsT = kblk[h01 * 64:(h01 + 1) * 64,
                                    kt * 128:(kt + 1) * 128]
                        rhs = Qf[h01 * 64:(h01 + 1) * 64,
                                 hp * T + qq * 512:hp * T + (qq + 1) * 512]
                    else:
                        lhsT = kblk[:, kt * 128:(kt + 1) * 128]
                        rhs = Qf[:, hp * T + qq * 512:hp * T + (qq + 1) * 512]
                    if kt >= 2:
                        p.tensor.wait(SC, exp_done[(kt - 2, h01)])
                    p.tensor.call("matmul", stp[:], lhsT, rhs,
                                  start=True, stop=True, inc=TT)
                    st_vs[h01] = TT.v
                for h01 in range(P2):
                    h = hp * P2 + h01
                    stp = ps_g[ppk * 2 + h01]
                    st_v = st_vs[h01]
                    mk_v = None
                    if causal:
                        did = False
                        for i, g in enumerate(tiles):
                            if g == kt or g < kt:
                                if not did:
                                    p.vector.wait(TT, st_v)
                                if g == kt:
                                    p.vector.call("tensor_add",
                                                  stp[:, i * 128:(i + 1) * 128],
                                                  stp[:, i * 128:(i + 1) * 128],
                                                  smask[:], inc=VE)
                                else:
                                    p.vector.call("memset",
                                                  stp[:, i * 128:(i + 1) * 128],
                                                  NEG, inc=VE)
                                did = True
                        mk_v = VE.v if did else None
                    p.scalar.wait(TT, st_v)
                    if mk_v is not None:
                        p.scalar.wait(VE, mk_v)
                    pt_dst = PTb[:, (ppk * 2 + h01) * 512:(ppk * 2 + h01 + 1) * 512]
                    p.scalar.call("activation", pt_dst, stp[:], AF.Exp,
                                  bias=Mb[:, h:h + 1], inc=SC)
                    exp_done[(kt, h01)] = SC.v
                if kt >= 1:
                    for h01 in range(P2):
                        emit_av(kt - 1, h01, kt - 1 == 0, False)
            for h01 in range(P2):
                emit_av(ktmax - 1, h01, ktmax == 1, True)
            av_v = TT.v
            p.scalar.wait(TT, av_v)
            if P2 == 2:
                p.scalar.call("copy", ctmp[0:65, 0:512], ps_ctx[0][0:65, :])
                p.scalar.call("copy", ctmp[0:65, 512:1024], ps_ctx[1][0:65, :],
                              inc=SC)
            else:
                p.scalar.call("copy", ctmp[0:64, 0:512], ps_ctx[0][0:64, :])
                p.scalar.call("copy", ctmp[0:65, 512:1024], ps_ctx[1][0:65, :],
                              inc=SC)
            ce_v = SC.v
            p.tensor.wait(SC, ce_v)
            for h01 in range(2):
                nr = 65 if (P2 == 2 or h01 == 1) else 64
                for i in range(4):
                    p.tensor.call(
                        "transpose",
                        ps_ctx[h01][:, 0:288].rearrange("p (a b) -> p a b", a=4)
                        [:, i, 0:nr],
                        ctmp[0:nr, h01 * 512 + i * 128:h01 * 512 + (i + 1) * 128],
                        sid[0:nr, 0:nr],
                        inc=(TT if i == 3 else None))
            p.vector.wait(TT, TT.v)
            if P2 == 2:
                for h01 in range(2):
                    cc = ps_ctx[h01][:, 0:288].rearrange("p (a b) -> p a b", a=4)
                    for i in range(4):
                        p.vector.call("reciprocal", nrm[:, 8:9], cc[:, i, 64:65],
                                      inc=VE)
                        p.vector.wait(VE, VE.v)
                        dt_ = ctxTM[:, i * (Hm * dh) + (hp * 2 + h01) * dh:
                                    i * (Hm * dh) + (hp * 2 + h01) * dh + dh]
                        p.vector.call("tensor_scalar", dt_, cc[:, i, 0:dh],
                                      nrm[:, 8:9], None, ALU.mult, inc=VE)
            else:
                c0 = ps_ctx[0][:, 0:288].rearrange("p (a b) -> p a b", a=4)
                c1 = ps_ctx[1][:, 0:288].rearrange("p (a b) -> p a b", a=4)
                for i in range(4):
                    p.vector.call("reciprocal", nrm[:, 8:9], c1[:, i, 64:65],
                                  inc=VE)
                    p.vector.wait(VE, VE.v)
                    dt_ = ctxTM[:, i * (Hm * dh) + hp * dh:
                                i * (Hm * dh) + hp * dh + dh]
                    p.vector.call("tensor_scalar", dt_[:, 0:64], c0[:, i, 0:64],
                                  nrm[:, 8:9], None, ALU.mult)
                    p.vector.call("tensor_scalar", dt_[:, 64:128], c1[:, i, 0:64],
                                  nrm[:, 8:9], None, ALU.mult, inc=VE)
        ctx_done = VE.v

        NCP = DP // 128
        for i in range(4):
            gt = tiles[i]
            p.tensor.wait(VE, ctx_done)
            p.tensor.wait(SC, SC.v)
            for c in range(NCP):
                p.tensor.call("transpose", ps_tr[:, c * 128:(c + 1) * 128],
                              ctxTM[:, i * (Hm * dh) + c * 128:
                                    i * (Hm * dh) + (c + 1) * 128],
                              sidb[0:128, 0:128],
                              inc=(TT if c == NCP - 1 else None))
            p.scalar.wait(TT)
            p.scalar.call("copy", ctxFT[:], ps_tr[:, 0:DP], inc=SC)
            ev_v = SC.v
            p.tensor.wait(SC, ev_v)
            p.tensor.wait(VE, VE.v)
            for s in range(2):
                for c in range(NCP):
                    p.tensor.call("matmul", ps_g[s][:],
                                  ctxFT[:, c * 128:(c + 1) * 128],
                                  Wpj[:, c * D + s * 512:c * D + (s + 1) * 512],
                                  start=(c == 0), stop=(c == NCP - 1),
                                  inc=(TT if c == NCP - 1 else None))
            pj_v = TT.v
            p.vector.wait(TT, pj_v)
            p.vector.wait(STR, STR.v)
            p.vector.call("tensor_copy", ystage[:, 0:512], ps_g[0][:])
            p.vector.call("tensor_copy", ystage[:, 512:1024], ps_g[1][:],
                          inc=VE)
            y_v = VE.v
            p.sync.wait(VE, y_v)
            if io:
                io["ywrite"](gt, ystage, STR)
            else:
                p.sync.dma(y_out[gt * 128:(gt + 1) * 128, :], ystage[:], STR)
    p.sync.wait(STR, STR.v)
    p.sync.wait(KSM, KSM.v)
    return p



def build_ffn(cfg, p=None, io=None):
    standalone = p is None
    if standalone:
        p = Prog(cfg["name"])
    nc = p.nc
    NT = 16
    HH = 1024
    rms_eps = cfg.get("rms_eps")
    shared = cfg.get("shared", False)
    moe_bias = cfg.get("moe_bias", False)
    combine = cfg.get("combine")
    l0mode = cfg["router"] == "topk_softmax"

    xo_d = None if io else p.din("xo", [2048, D], F32)
    pA_d = pB_d = bcmb_d = None
    if combine and not io:
        pA_d = p.din("pA", [2048, D], F32)
        pB_d = p.din("pB", [2048, D], F32)
    if cfg.get("bcmb"):
        bcmb_d = p.din("bcmb", [1, D], F32, no_alias=True)
    wrt_d = p.din("wrt", [D, 8], F32)
    brt_d = p.din("brt", [1, 8], F32) if not l0mode else None
    gated = cfg.get("gated", True)
    upw = 2 * HH if gated else HH
    wup_d = p.din("wup", [16 * D, upw], F32)
    wdn_d = p.din("wdn", [16 * HH, D], F32)
    gvec = p.din("gvec", [128, 8], F32) if cfg.get("nfold") else None
    if shared:
        wsu_d = p.din("wsu", [2 * D, HH], F32)
        wsd_d = p.din("wsd", [2 * HH, D], F32)
        bs1_d = p.din("bs1", [2, HH], F32)
        bs2_d = p.din("bs2", [1, D], F32)
    if moe_bias:
        rb1_d = p.din("rb1", [16, HH], F32)
        rb2_d = p.din("rb2", [8, D], F32)
    identf = p.din("identf", [128, 128], F32)
    iota_d = p.din("iota", [1, 8], F32)
    if standalone:
        y_out = nc.dram_tensor("y", [2048, D], F32, kind="ExternalOutput").ap()
    ybscr = p.dint("ybscr", [2048, D], F32)

    stats = p.sbuf("stats", [128, NT * 6], F32)
    xstage = p.sbuf("xstage", [128, 2 * D], F32)
    pstage = p.sbuf("pstage", [128, D], F32) if combine else None
    xnb = p.sbuf("xnb", [128, D], F32)
    xnT = p.sbuf("xnT", [128, NT * D], BF16)
    hT = p.sbuf("hT", [128, NT * HH], BF16)
    oa = [p.sbuf("oa0", [128, NT * 512], F32),
          p.sbuf("oa1", [128, NT * 512], F32)]
    Wsl = p.sbuf("Wsl", [128, NCH * 2048], BF16)
    Wstage = p.sbuf("Wstage", [128, 8], F32)
    hstage = p.sbuf("hstage", [128, HH], BF16)
    gstage = p.sbuf("gstage", [128, HH], BF16)
    gsc = p.sbuf("gsc", [128, 1024], F32) if gated else None
    sid = p.sbuf("sid", [128, 128], F32)
    sidb = p.sbuf("sidb", [128, 128], BF16)
    onesf = p.sbuf("onesf", [1, 128], F32)
    iota_b = p.sbuf("iota_b", [128, 8], F32)
    rsc = p.sbuf("rsc", [128, 24], F32)
    wsel = p.sbuf("wsel", [128, NT * 8], F32)
    has_b = cfg.get("shared") or cfg.get("moe_bias")
    brow = p.sbuf("brow", [1, (2 * D) if has_b else D], F32)
    b2row = p.sbuf("b2row", [1, D], F32) if has_b else None
    gcol = p.sbuf("gcol", [128, 8], F32)
    wrtb = p.sbuf("wrtb", [128, 8 * 8], BF16)
    ystage = p.sbuf("ystage", [128, D], F32)
    c16 = p.sbuf("c16", [128, D], FP16) if combine == "fp16" else None

    ps_g = [p.psum(f"psg{i}", [128, 512], F32) for i in range(4)]
    ps_r = p.psum("ps_r", [128, 512], F32)
    ps_tr = p.psum("ps_tr", [128, 1024], BF16)
    ps_trf = p.psum("ps_trf", [128, 1024], F32)

    LD = p.sem("LD")
    WL = p.sem("WL")
    WLb = p.sem("WLb")
    SC = p.sem("SC")
    VE = p.sem("VE")
    TT = p.sem("TT")
    STR = p.sem("STR")
    GP = p.sem("GP")
    CB = p.sem("CB")

    p.sync.dma(sid[:], identf, LD)
    p.sync.dma(iota_b[0:1, :], iota_d, LD)
    if cfg.get("nfold"):
        p.sync.dma(gcol[:], gvec, LD)
    if cfg.get("bcmb"):
        p.sync.dma(brow[:, 0:D], bcmb_d, LD)
    if not l0mode:
        p.sync.dma(rsc[0:1, 16:24], brt_d, LD)
    ld_consts = LD.v
    p.vector.wait(LD, ld_consts)
    p.vector.call("tensor_copy", sidb[:], sid[:], inc=VE)
    p.gpsimd.call("memset", onesf[:], 1.0)
    p.gpsimd.call("memset", oa[0][:], 0.0)
    p.gpsimd.call("memset", oa[1][:], 0.0, inc=GP)
    p.tensor.wait(LD, ld_consts)
    p.tensor.wait(GP, GP.v)
    p.tensor.call("matmul", ps_r[:, 0:8], onesf[:, 0:128], iota_b[0:1, 0:8],
                  start=True, stop=True, inc=TT)
    p.scalar.wait(TT)
    p.scalar.call("copy", iota_b[:, 0:8], ps_r[:, 0:8], inc=SC)
    if not l0mode:
        p.tensor.wait(SC, SC.v)
        p.tensor.call("matmul", ps_r[:, 8:16], onesf[:, 0:128], rsc[0:1, 16:24],
                      start=True, stop=True, inc=TT)
        p.scalar.wait(TT)
        p.scalar.call("copy", rsc[:, 16:24], ps_r[:, 8:16], inc=SC)
    iota_v = SC.v

    first_stream = [True]

    def stream_w(wd, r0, c0, nch, dout, dst, dcw, roff):
        if first_stream[0]:
            first_stream[0] = False
        else:
            p.gpsimd.wait(TT, TT.v)
        for c in range(nch):
            p.gpsimd.dma(dst[:, c * dcw + roff:c * dcw + roff + dout],
                         wd[r0 + c * 128:r0 + (c + 1) * 128, c0:c0 + dout],
                         CB)
        return CB.v

    for c in range(NCH):
        if c > 0:
            p.sync.wait(VE, VE.v)
        p.sync.dma(Wstage[:, 0:8], wrt_d[c * 128:(c + 1) * 128, :], WL)
        p.vector.wait(WL)
        p.vector.call("tensor_copy", wrtb[:, c * 8:(c + 1) * 8],
                      Wstage[:, 0:8], inc=VE)
    wrt_v = VE.v

    eps1 = EPS_LN
    if rms_eps is None:
        va, vb_ = 1.0, eps1
    else:
        va, vb_ = 1.0 + rms_eps, eps1 * rms_eps
    tile_done = [0] * NT
    for t in range(NT):
        if t >= 1:
            p.sync.wait(VE, tile_done[t - 1])
        if io:
            xld = io["load_xo"](xstage[:, (t % 2) * D:(t % 2 + 1) * D], t, LD)
        else:
            xld = p.sync.dma(xstage[:, (t % 2) * D:(t % 2 + 1) * D],
                             xo_d[t * 128:(t + 1) * 128, :], LD)
        xt = xstage[:, (t % 2) * D:(t % 2 + 1) * D]
        if combine:
            p.sync.wait(STR, STR.v)
            if io:
                io["load_part"](pstage[:, 0:D], 0, t, LD)
                xld = io["load_part"](ystage[:, 0:D], 1, t, LD)
            else:
                p.sync.dma(pstage[:, 0:D], pA_d[t * 128:(t + 1) * 128, :], LD)
                xld = p.sync.dma(ystage[:, 0:D],
                                 pB_d[t * 128:(t + 1) * 128, :], LD)
            p.vector.wait(LD, xld)
            p.vector.call("tensor_add", pstage[:, 0:D], pstage[:, 0:D],
                          ystage[:, 0:D], inc=VE)
            if cfg.get("bcmb"):
                p.tensor.wait(VE, VE.v)
                p.tensor.wait(SC, SC.v)
                p.tensor.call("matmul", ps_g[3][:], onesf[:, 0:128],
                              brow[:, 0:512], start=True, stop=True)
                p.tensor.call("matmul", ps_g[2][:], onesf[:, 0:128],
                              brow[:, 512:1024], start=True, stop=True, inc=TT)
                p.vector.wait(TT, TT.v)
                p.vector.call("tensor_add", pstage[:, 0:512], pstage[:, 0:512],
                              ps_g[3][:])
                p.vector.call("tensor_add", pstage[:, 512:1024],
                              pstage[:, 512:1024], ps_g[2][:], inc=VE)
            if combine == "fp16":
                p.vector.call("tensor_copy", c16[:], pstage[:, 0:D], inc=VE)
                p.vector.call("tensor_add", xt, xt, c16[:], inc=VE)
            else:
                p.vector.call("tensor_add", xt, xt, pstage[:, 0:D], inc=VE)
        st6 = stats[:, t * 6:(t + 1) * 6]
        mu, s_ap, r1 = st6[:, 0:1], st6[:, 1:2], st6[:, 2:3]
        w_ap, a_ap, b_ap = st6[:, 3:4], st6[:, 4:5], st6[:, 5:6]
        p.scalar.wait(LD, xld)
        p.scalar.wait(VE, VE.v)
        p.scalar.wait(TT, TT.v)
        p.scalar.call("activation", xnb[:], xt, AF.Square, accum_out=b_ap,
                      inc=SC)
        p.vector.wait(SC, SC.v)
        p.vector.call("reduce_sum", a_ap, xt, axis=AXX, inc=VE)
        p.vector.wait(VE)
        p.vector.call("tensor_scalar", mu, a_ap, 1.0 / D, None, ALU.mult)
        p.vector.call("tensor_scalar", w_ap, b_ap, 1.0 / D, None, ALU.mult)
        p.vector.call("tensor_scalar", a_ap, mu, mu, None, ALU.mult)
        p.vector.call("tensor_sub", w_ap, w_ap, a_ap, inc=VE)
        p.vector.call("tensor_scalar", b_ap, w_ap, va, vb_, ALU.mult, ALU.add,
                      inc=VE)
        p.scalar.wait(VE, VE.v)
        p.scalar.call("activation", a_ap, b_ap, AF.Ln)
        p.scalar.call("activation", s_ap, a_ap, AF.Exp, scale=-0.5, inc=SC)
        p.vector.wait(SC, SC.v)
        p.vector.call("tensor_scalar", b_ap, w_ap, 1.0, eps1,
                      ALU.mult, ALU.add, inc=VE)
        p.scalar.wait(VE, VE.v)
        p.scalar.call("activation", a_ap, b_ap, AF.Ln)
        p.scalar.call("activation", r1, a_ap, AF.Exp, scale=-0.5, inc=SC)
        p.vector.wait(SC, SC.v)
        p.vector.call("tensor_scalar", xnb[:], xt, mu, s_ap,
                      ALU.subtract, ALU.mult, inc=VE)
        xn_v = VE.v
        p.vector.wait(STR, STR.v)
        if cfg.get("skip_ln"):
            p.vector.call("tensor_scalar", ystage[:], xt, mu, r1,
                          ALU.subtract, ALU.mult, inc=VE)
            p.vector.call("tensor_add", ystage[:], ystage[:], xt, inc=VE)
        else:
            p.vector.call("tensor_copy", ystage[:], xt, inc=VE)
        yb_v = VE.v
        p.sync.wait(VE, yb_v)
        p.sync.dma(ybscr[t * 128:(t + 1) * 128, :], ystage[:], STR)
        p.tensor.wait(VE, xn_v)
        p.tensor.wait(SC, SC.v)
        for c in range(NCH):
            p.tensor.call("transpose", ps_trf[:, c * 128:(c + 1) * 128],
                          xnb[:, c * 128:(c + 1) * 128], sid[0:128, 0:128],
                          inc=(TT if c == NCH - 1 else None))
        p.scalar.wait(TT)
        if cfg.get("nfold"):
            for c in range(NCH):
                p.scalar.call("activation",
                              xnT[:, t * D + c * 128:t * D + (c + 1) * 128],
                              ps_trf[:, c * 128:(c + 1) * 128], AF.Copy,
                              scale=gcol[:, c:c + 1],
                              inc=(SC if c == NCH - 1 else None))
        else:
            p.scalar.call("copy", xnT[:, t * D:(t + 1) * D], ps_trf[:], inc=SC)
        ev_v = SC.v
        p.tensor.wait(SC, max(ev_v, iota_v))
        p.tensor.wait(VE, wrt_v)
        for c in range(NCH):
            p.tensor.call("matmul", ps_r[:, 0:8],
                          xnT[:, t * D + c * 128:t * D + (c + 1) * 128],
                          wrtb[:, c * 8:(c + 1) * 8],
                          start=(c == 0), stop=(c == NCH - 1),
                          inc=(TT if c == NCH - 1 else None))
        rt_v = TT.v
        p.vector.wait(TT, rt_v)
        sc0 = rsc[:, 0:8]
        p.vector.call("tensor_copy", sc0, ps_r[:, 0:8], inc=VE)
        if not l0mode:
            p.vector.call("tensor_add", sc0, sc0, rsc[:, 16:24], inc=VE)
        m1 = stats[:, t * 6 + 3:t * 6 + 4]
        m2c = stats[:, t * 6 + 4:t * 6 + 5]
        i1 = stats[:, t * 6 + 5:t * 6 + 6]
        tmp8 = wsel[:, t * 8:(t + 1) * 8]
        d_ap, e_ap, z_ap = rsc[:, 8:9], rsc[:, 9:10], rsc[:, 10:11]
        w1, w2, i2 = rsc[:, 11:12], rsc[:, 12:13], rsc[:, 13:14]
        negm1 = rsc[:, 14:15]
        p.vector.call("reduce_max", m1, sc0, axis=AXX, inc=VE)
        p.vector.wait(VE)
        if not l0mode:
            p.vector.call("tensor_scalar", negm1, m1, -1.0, None, ALU.mult,
                          inc=VE)
            p.scalar.wait(VE, VE.v)
            p.scalar.call("activation", tmp8, sc0, AF.Exp, bias=negm1,
                          accum_out=z_ap, inc=SC)
            p.vector.wait(SC, SC.v)
        p.vector.call("tensor_scalar", tmp8, sc0, m1, None, ALU.is_ge, inc=VE)
        p.vector.call("tensor_mul", tmp8, tmp8, iota_b[:, 0:8], inc=VE)
        p.vector.call("reduce_max", i1, tmp8, axis=AXX, inc=VE)
        p.vector.wait(VE)
        p.vector.call("tensor_scalar", tmp8, iota_b[:, 0:8], i1, None,
                      ALU.is_equal, inc=VE)
        p.vector.call("scalar_tensor_tensor", tmp8, tmp8, -1e9, sc0,
                      ALU.mult, ALU.add, inc=VE)
        p.vector.call("reduce_max", m2c, tmp8, axis=AXX, inc=VE)
        p.vector.wait(VE)
        p.vector.call("tensor_scalar", tmp8, tmp8, m2c, None, ALU.is_ge,
                      inc=VE)
        p.vector.call("tensor_mul", tmp8, tmp8, iota_b[:, 0:8], inc=VE)
        p.vector.call("reduce_max", i2, tmp8, axis=AXX, inc=VE)
        p.vector.wait(VE)
        if l0mode:
            p.vector.call("tensor_sub", d_ap, m2c, m1, inc=VE)
            p.scalar.wait(VE, VE.v)
            p.scalar.call("activation", e_ap, d_ap, AF.Exp, inc=SC)
            p.vector.wait(SC, SC.v)
            p.vector.call("tensor_scalar", z_ap, e_ap, 1.0, 1.0, ALU.mult,
                          ALU.add, inc=VE)
            p.vector.call("reciprocal", w1, z_ap, inc=VE)
            p.vector.wait(VE)
            p.vector.call("tensor_scalar", w2, w1, -1.0, 1.0, ALU.mult,
                          ALU.add, inc=VE)
        else:
            p.vector.call("tensor_sub", d_ap, m2c, m1, inc=VE)
            p.scalar.wait(VE, VE.v)
            p.scalar.call("activation", e_ap, d_ap, AF.Exp, inc=SC)
            p.vector.wait(SC, SC.v)
            p.vector.call("reciprocal", w1, z_ap, inc=VE)
            p.vector.wait(VE)
            p.vector.call("tensor_mul", w2, e_ap, w1, inc=VE)
        p.vector.call("tensor_scalar", tmp8, iota_b[:, 0:8], i1, None,
                      ALU.is_equal, inc=VE)
        p.vector.call("tensor_scalar", tmp8, tmp8, w1, None, ALU.mult, inc=VE)
        p.vector.call("tensor_scalar", rsc[:, 0:8], iota_b[:, 0:8], i2, None,
                      ALU.is_equal, inc=VE)
        p.vector.call("tensor_scalar", rsc[:, 0:8], rsc[:, 0:8], w2, None,
                      ALU.mult, inc=VE)
        p.vector.call("tensor_add", tmp8, tmp8, rsc[:, 0:8], inc=VE)
        tile_done[t] = VE.v

    experts = [(e, h) for e in range(8) for h in range(2)]
    if shared:
        experts += [("sh", 0), ("sh", 1)]
    for (e, hf) in experts:
        is_sh = (e == "sh")
        pe = (e * 2 + hf) if not is_sh else hf
        upd = HH if (is_sh or not gated) else 2 * HH
        if is_sh:
            up_v = stream_w(wsu_d, hf * D, 0, NCH, HH, Wsl, 2048, 0)
        else:
            up_v = stream_w(wup_d, pe * D, 0, NCH, upd, Wsl, 2048, 0)
        has_up_bias = is_sh or moe_bias
        if has_up_bias or ((is_sh or moe_bias) and hf == 0):
            p.sync.wait(TT, TT.v)
        if has_up_bias:
            bld = p.sync.dma(brow[:, D:D + HH],
                             bs1_d[hf:hf + 1, :] if is_sh else rb1_d[pe:pe + 1, :], LD)
        has_dn_bias = (is_sh or moe_bias) and hf == 0
        if has_dn_bias:
            bld = p.sync.dma(b2row[:, 0:D],
                             bs2_d if is_sh else rb2_d[e:e + 1, :], LD)
        p.tensor.wait(CB, up_v)
        if has_up_bias or has_dn_bias:
            p.tensor.wait(LD, LD.v)
        p.tensor.wait(VE, VE.v)
        hst = [hstage, gstage]
        g_sub = gated and not is_sh
        evac_sc = {}
        evac_ve = {}
        tr_cp = {}
        h_rdy = {}

        def emit_tr(tt):
            sem, val = h_rdy[tt]
            p.tensor.wait(sem, val)
            if tt >= 1:
                p.tensor.wait(SC, tr_cp[tt - 1])
            for c in range(NCH):
                p.tensor.call("transpose", ps_tr[:, c * 128:(c + 1) * 128],
                              hst[tt % 2][:, c * 128:(c + 1) * 128],
                              sidb[0:128, 0:128],
                              inc=(TT if c == NCH - 1 else None))
            p.scalar.wait(TT)
            p.scalar.call("copy", hT[:, tt * HH:(tt + 1) * HH], ps_tr[:],
                          inc=SC)
            tr_cp[tt] = SC.v

        for t in range(NT):
            if g_sub:
                for s in range(2):
                    k = t * 2 + s
                    if k >= 2:
                        p.tensor.wait(SC, evac_sc[k - 2])
                        p.tensor.wait(VE, evac_ve[k - 2])
                    for c in range(NCH):
                        p.tensor.call("matmul", ps_g[s][:],
                                      xnT[:, t * D + c * 128:t * D + (c + 1) * 128],
                                      Wsl[:, c * 2048 + s * 512:c * 2048 + (s + 1) * 512],
                                      start=(c == 0), stop=(c == NCH - 1))
                    for c in range(NCH):
                        p.tensor.call("matmul", ps_g[2 + s][:],
                                      xnT[:, t * D + c * 128:t * D + (c + 1) * 128],
                                      Wsl[:, c * 2048 + (2 + s) * 512:
                                          c * 2048 + (3 + s) * 512],
                                      start=(c == 0), stop=(c == NCH - 1),
                                      inc=(TT if c == NCH - 1 else None))
                    p.scalar.wait(TT)
                    gh = gsc[:, (k % 2) * 512:(k % 2 + 1) * 512]
                    p.scalar.call("activation", gh, ps_g[2 + s][:], AF.Silu,
                                  inc=SC)
                    evac_sc[k] = SC.v
                    p.vector.wait(SC, evac_sc[k])
                    p.vector.call("tensor_mul",
                                  hst[t % 2][:, s * 512:(s + 1) * 512],
                                  ps_g[s][:], gh, inc=VE)
                    evac_ve[k] = VE.v
                h_rdy[t] = (VE, evac_ve[t * 2 + 1])
            else:
                pg = (ps_g[0], ps_g[1]) if t % 2 == 0 else (ps_g[2], ps_g[3])
                if t >= 2:
                    p.tensor.wait(SC, evac_sc[t - 2])
                for s in range(2):
                    if has_up_bias:
                        p.tensor.call("matmul", pg[s][:], onesf[:, 0:128],
                                      brow[:, D + s * 512:D + (s + 1) * 512],
                                      start=True, stop=False)
                    for c in range(NCH):
                        p.tensor.call("matmul", pg[s][:],
                                      xnT[:, t * D + c * 128:t * D + (c + 1) * 128],
                                      Wsl[:, c * 2048 + s * 512:c * 2048 + (s + 1) * 512],
                                      start=(c == 0 and not has_up_bias),
                                      stop=(c == NCH - 1),
                                      inc=(TT if c == NCH - 1 else None))
                p.scalar.wait(TT)
                p.scalar.call("activation", hst[t % 2][:, 0:512], pg[0][:],
                              AF.Gelu)
                p.scalar.call("activation", hst[t % 2][:, 512:1024], pg[1][:],
                              AF.Gelu, inc=SC)
                evac_sc[t] = SC.v
                h_rdy[t] = (SC, evac_sc[t])
            if t >= 1:
                emit_tr(t - 1)
        emit_tr(NT - 1)
        hT_done = SC.v
        if is_sh:
            dn_v = stream_w(wsd_d, hf * HH, 0, NCH, D, Wsl, 2048, 0)
        else:
            dn_v = stream_w(wdn_d, pe * HH, 0, NCH, D, Wsl, 2048, 0)
        p.tensor.wait(CB, dn_v)
        p.tensor.wait(SC, hT_done)
        p.tensor.wait(VE, VE.v)
        oa_ve = {}
        for t in range(NT):
            pg = (ps_g[0], ps_g[1]) if t % 2 == 0 else (ps_g[2], ps_g[3])
            if t >= 2:
                p.tensor.wait(VE, oa_ve[t - 2])
            for s in range(2):
                if has_dn_bias:
                    p.tensor.call("matmul", pg[s][:], onesf[:, 0:128],
                                  b2row[:, s * 512:(s + 1) * 512],
                                  start=True, stop=False)
                for c in range(NCH):
                    p.tensor.call("matmul", pg[s][:],
                                  hT[:, t * HH + c * 128:t * HH + (c + 1) * 128],
                                  Wsl[:, c * 2048 + s * 512:c * 2048 + (s + 1) * 512],
                                  start=(c == 0 and not has_dn_bias),
                                  stop=(c == NCH - 1),
                                  inc=(TT if c == NCH - 1 else None))
            dnmm_v = TT.v
            p.vector.wait(TT, dnmm_v)
            if is_sh:
                p.vector.call("tensor_add", oa[0][:, t * 512:(t + 1) * 512],
                              oa[0][:, t * 512:(t + 1) * 512], pg[0][:])
                p.vector.call("tensor_add", oa[1][:, t * 512:(t + 1) * 512],
                              oa[1][:, t * 512:(t + 1) * 512], pg[1][:],
                              inc=VE)
            else:
                we = rsc[:, 15:16] if t % 2 == 0 else rsc[:, 16:17]
                p.vector.call("tensor_scalar", we, wsel[:, t * 8 + e:t * 8 + e + 1],
                              1.0, None, ALU.mult, inc=VE)
                p.vector.wait(VE)
                p.vector.call("scalar_tensor_tensor",
                              oa[0][:, t * 512:(t + 1) * 512],
                              pg[0][:], we, oa[0][:, t * 512:(t + 1) * 512],
                              ALU.mult, ALU.add)
                p.vector.call("scalar_tensor_tensor",
                              oa[1][:, t * 512:(t + 1) * 512],
                              pg[1][:], we, oa[1][:, t * 512:(t + 1) * 512],
                              ALU.mult, ALU.add, inc=VE)
            oa_ve[t] = VE.v

    for t in range(NT):
        p.sync.wait(VE, VE.v)
        p.sync.wait(STR, STR.v)
        xld = p.sync.dma(xstage[:, 0:D], ybscr[t * 128:(t + 1) * 128, :], LD)
        p.vector.wait(LD, xld)
        p.vector.call("tensor_add", ystage[:, 0:512], xstage[:, 0:512],
                      oa[0][:, t * 512:(t + 1) * 512])
        p.vector.call("tensor_add", ystage[:, 512:1024], xstage[:, 512:1024],
                      oa[1][:, t * 512:(t + 1) * 512], inc=VE)
        p.sync.wait(VE, VE.v)
        if io:
            io["ywrite"](t, ystage, STR)
        else:
            p.sync.dma(y_out[t * 128:(t + 1) * 128, :], ystage[:], STR)
    p.sync.wait(STR, STR.v)
    return p



def build_scan(cfg, p=None, io=None):
    standalone = p is None
    if standalone:
        p = Prog(cfg["name"])
    nc = p.nc
    SCD, STD = 64, 16
    EPSR = 1.1920929e-07

    x_d = io["xb"] if io else p.din("xb", [T, D], F32)
    pA_d = io["pA"] if io else p.din("pA", [T, D], F32)
    pB_d = io["pB"] if io else p.din("pB", [T, D], F32)
    st_d = None if io else p.din("st_in", [128, NT_FULL * 6], F32)
    wzg_d = p.din("wzg", [D, 128], F32)
    wdt_d = p.din("wdt", [SCD, 96], F32)
    dtb_d = p.din("dtb", [1, 96], F32)
    alog_d = p.din("alog", [128, 8], F32)
    ec_d = p.din("ec", [SCD, 8 * 128], F32)
    est_d = p.din("est", [STD, 128], F32)
    g_d = p.din("gmat", [128, 8 * SCD], F32)
    wout_d = p.din("wout", [SCD, D], F32)
    gvec_d = p.din("gvec", [128, 8], F32)
    identf = p.din("identf", [128, 128], F32)
    if standalone:
        y_out = nc.dram_tensor("y", [T, D], F32, kind="ExternalOutput").ap()
    x1scr = p.dint("x1scr", [T, D], F32)

    rmsc = p.sbuf("rmsc", [128, NT_FULL * 2], F32)
    stin = (io.get("stats_buf") if io else None) or \
        p.sbuf("stin", [128, NT_FULL * 6], F32)
    xstage = p.sbuf("xstage", [128, 2 * D], F32)
    pstage = p.sbuf("pstage", [128, 2 * D], F32)
    x1b = p.sbuf("x1b", [128, D], F32)
    xnb = p.sbuf("xnb", [128, D], BF16)
    xnT = p.sbuf("xnT", [128, D], BF16)
    wzg = p.sbuf("wzg_s", [128, NCH * 128], BF16)
    wdt = p.sbuf("wdt_s", [SCD, 96], BF16)
    zF = p.sbuf("zF", [SCD, T], BF16)
    gF = p.sbuf("gF", [SCD, T], BF16)
    ztm = p.sbuf("ztm", [128, SCD], F32)
    dtm = p.sbuf("dtm", [128, 2 * SCD], F32)
    bici = p.sbuf("bici", [128, 96], F32)
    dtF = p.sbuf("dtF", [SCD, T], F32)
    dtzF = p.sbuf("dtzF", [SCD, T], F32)
    BiF = p.sbuf("BiF", [STD, T], F32)
    CiF = p.sbuf("CiF", [STD, T], F32)
    ec = p.sbuf("ec_s", [SCD, 8 * 128], F32)
    est = p.sbuf("est_s", [STD, 128], F32)
    gmat = p.sbuf("gmat_s", [128, 8 * SCD], F32)
    woutb = p.sbuf("woutb", [SCD, D], BF16)
    acol = p.sbuf("acol_s", [128, 8], F32)
    onec = p.sbuf("onec", [128, 1], F32)
    onesr = p.sbuf("onesr", [1, 128], F32)
    a_sb = p.sbuf("a_sb", [128, 512], F32)
    u0_sb = p.sbuf("u0_sb", [128, 512], F32)
    u_sb = p.sbuf("u_sb", [128, 512], F32)
    states = p.sbuf("states", [128, 512], F32)
    prod = p.sbuf("prod", [128, 512], F32)
    carry = p.sbuf("carry", [128, 8], F32)
    ygb = p.sbuf("ygb", [SCD, 512], BF16)
    ystage = p.sbuf("ystage", [128, D], F32)
    x1r = p.sbuf("x1r", [128, D], F32)
    sid = p.sbuf("sid", [128, 128], F32)
    sidb = p.sbuf("sidb", [128, 128], BF16)
    sp1 = p.sbuf("sp1", [128, SCD], F32)
    sp2 = p.sbuf("sp2", [128, SCD], F32)
    Wstage = p.sbuf("Wstage", [128, 512], F32)
    dtbrow = p.sbuf("dtbrow", [1, 96], F32)
    gcol = p.sbuf("gcol_s", [128, 8], F32)

    ps_a = p.psum("ps_a", [128, 512], F32)
    ps_b = p.psum("ps_b", [128, 512], F32)
    ps_c = p.psum("ps_c", [128, 512], F32)
    ps_y = p.psum("ps_y", [128, 512], F32)
    ps_p = [p.psum(f"ps_p{i}", [128, 512], F32) for i in range(2)]
    ps_tr = p.psum("ps_tr", [128, 1024], BF16)
    ps_trf = p.psum("ps_trf", [128, 512], F32)

    LD = p.sem("LD")
    WL = p.sem("WL")
    WLb = p.sem("WLb")
    SC = p.sem("SC")
    VE = p.sem("VE")
    TT = p.sem("TT")
    STR = p.sem("STR")
    GP = p.sem("GP")
    X1 = p.sem("X1")

    p.sync.dma(sid[:], identf, LD)
    if not io:
        p.sync.dma(stin[:], st_d, LD)
    p.sync.dma(ec[:], ec_d, LD)
    p.sync.dma(est[:], est_d, LD)
    p.sync.dma(gmat[:], g_d, LD)
    p.sync.dma(acol[:], alog_d, LD)
    p.sync.dma(dtbrow[:, :], dtb_d, LD)
    p.sync.dma(gcol[:], gvec_d, LD)
    p.sync.dma(Wstage[0:SCD, 0:96], wdt_d, LD)
    ld_consts = LD.v
    p.vector.wait(LD, ld_consts)
    p.vector.call("tensor_copy", sidb[:], sid[:], inc=VE)
    p.vector.call("tensor_copy", wdt[:], Wstage[0:SCD, 0:96], inc=VE)
    CB = p.sem("CB")
    p.gpsimd.call("memset", onec[:], 1.0)
    p.gpsimd.call("memset", onesr[:], 1.0, inc=GP)
    p.gpsimd.dma(woutb[:], wout_d, CB)
    gp_v = GP.v
    cb_v = CB.v
    p.scalar.wait(LD, ld_consts)
    p.scalar.call("activation", acol[:], acol[:], AF.Exp, inc=SC)
    p.vector.wait(SC, SC.v)
    p.vector.call("tensor_scalar", acol[:], acol[:], -1.0, None, ALU.mult,
                  inc=VE)
    acol_v = VE.v
    for c in range(NCH):
        p.sync.wait(VE, VE.v)
        p.sync.dma(Wstage[:, 0:128], wzg_d[c * 128:(c + 1) * 128, :], WL)
        p.vector.wait(WL)
        p.vector.call("tensor_scalar", wzg[:, c * 128:(c + 1) * 128],
                      Wstage[:, 0:128], gcol[:, c:c + 1], None, ALU.mult,
                      inc=VE)
    wzg_v = VE.v

    tile_done = [0] * NT_FULL
    for t in range(cfg.get("maxt", NT_FULL)):
        if t >= 1:
            p.sync.wait(VE, tile_done[t - 1])
        p.sync.dma(xstage[:, (t % 2) * D:(t % 2 + 1) * D],
                   x_d[t * 128:(t + 1) * 128, :], LD)
        p.sync.dma(pstage[:, 0:D], pA_d[t * 128:(t + 1) * 128, :], LD)
        xld = p.sync.dma(pstage[:, D:2 * D], pB_d[t * 128:(t + 1) * 128, :], LD)
        xt = xstage[:, (t % 2) * D:(t % 2 + 1) * D]
        st6 = stin[:, t * 6:(t + 1) * 6]
        p.vector.wait(LD, xld)
        p.vector.wait(X1, X1.v)
        p.vector.call("tensor_scalar", x1b[:], xt, st6[:, 0:1], st6[:, 2:3],
                      ALU.subtract, ALU.mult, inc=VE)
        p.vector.call("tensor_add", x1b[:], x1b[:], pstage[:, 0:D], inc=VE)
        p.vector.call("tensor_add", x1b[:], x1b[:], pstage[:, D:2 * D], inc=VE)
        x1_v = VE.v
        p.sync.wait(VE, x1_v)
        p.sync.dma(x1scr[t * 128:(t + 1) * 128, :], x1b[:], X1)
        if cfg.get("steps", 99) < 1:
            tile_done[t] = VE.v
            continue
        p.scalar.wait(VE, x1_v)
        p.scalar.wait(TT, TT.v)
        p.scalar.call("activation", xnb[:], x1b[:], AF.Square,
                      accum_out=rmsc[:, 2 * t:2 * t + 1], inc=SC)
        p.vector.wait(SC, SC.v)
        p.vector.call("tensor_scalar", rmsc[:, 2 * t:2 * t + 1],
                      rmsc[:, 2 * t:2 * t + 1], 1.0 / D, EPSR,
                      ALU.mult, ALU.add, inc=VE)
        p.scalar.wait(VE, VE.v)
        p.scalar.call("activation", rmsc[:, 2 * t + 1:2 * t + 2],
                      rmsc[:, 2 * t:2 * t + 1], AF.Ln)
        p.scalar.call("activation", rmsc[:, 2 * t:2 * t + 1],
                      rmsc[:, 2 * t + 1:2 * t + 2], AF.Exp, scale=-0.5, inc=SC)
        p.vector.wait(SC, SC.v)
        p.vector.call("tensor_scalar", xnb[:], x1b[:],
                      rmsc[:, 2 * t:2 * t + 1], None, ALU.mult, inc=VE)
        xn_v = VE.v
        if cfg.get("steps", 99) < 2:
            tile_done[t] = VE.v
            continue
        p.tensor.wait(VE, xn_v)
        p.tensor.wait(SC, SC.v)
        for c in range(NCH):
            p.tensor.call("transpose", ps_tr[:, c * 128:(c + 1) * 128],
                          xnb[:, c * 128:(c + 1) * 128], sidb[0:128, 0:128],
                          inc=(TT if c == NCH - 1 else None))
        p.scalar.wait(TT)
        p.scalar.call("copy", xnT[:], ps_tr[:], inc=SC)
        ev_v = SC.v
        if cfg.get("steps", 99) < 3:
            tile_done[t] = VE.v
            continue
        p.tensor.wait(SC, ev_v)
        p.tensor.wait(VE, wzg_v)
        for c in range(NCH):
            p.tensor.call("matmul", ps_a[:, 0:128],
                          xnT[:, c * 128:(c + 1) * 128],
                          wzg[:, c * 128:(c + 1) * 128],
                          start=(c == 0), stop=(c == NCH - 1),
                          inc=(TT if c == NCH - 1 else None))
        zg_v = TT.v
        if cfg.get("steps", 99) < 4:
            tile_done[t] = VE.v
            continue
        p.vector.wait(TT, zg_v)
        p.vector.call("tensor_copy", ztm[:], ps_a[:, 0:SCD], inc=VE)
        p.vector.call("tensor_copy", xnb[:, SCD:2 * SCD], ztm[:], inc=VE)
        p.scalar.wait(TT, zg_v)
        p.scalar.call("activation", xnb[:, 0:SCD], ps_a[:, SCD:128], AF.Silu,
                      inc=SC)
        p.tensor.wait(VE, VE.v)
        p.tensor.wait(SC, SC.v)
        p.tensor.call("transpose", ps_tr[:, 0:128][0:SCD, 0:128],
                      xnb[:, SCD:2 * SCD], sidb[0:128, 0:128])
        p.tensor.call("transpose", ps_tr[:, 128:256][0:SCD, 0:128],
                      xnb[:, 0:SCD], sidb[0:128, 0:128], inc=TT)
        p.scalar.wait(TT)
        p.scalar.call("copy", zF[:, t * 128:(t + 1) * 128],
                      ps_tr[:, 0:128][0:SCD, :])
        p.scalar.call("copy", gF[:, t * 128:(t + 1) * 128],
                      ps_tr[:, 128:256][0:SCD, :], inc=SC)
        zf_v = SC.v
        if cfg.get("steps", 99) < 5:
            tile_done[t] = VE.v
            continue
        ss = cfg.get("substep", 99)
        p.tensor.wait(SC, zf_v)
        p.tensor.wait(LD, ld_consts)
        p.tensor.wait(GP, gp_v)
        p.tensor.wait(CB, cb_v)
        p.tensor.wait(VE, VE.v)
        if ss >= 1:
            p.tensor.call("matmul", ps_b[:, 0:96], onesr[:, 0:128], dtbrow[:, :],
                          start=True, stop=False)
            p.tensor.call("matmul", ps_b[:, 0:96],
                          zF[:, t * 128:(t + 1) * 128], wdt[:],
                          start=False, stop=True, inc=TT)
        dt_v = TT.v
        if ss >= 2:
            p.vector.wait(TT, dt_v)
            p.vector.call("tensor_scalar", dtm[:, 0:SCD], ps_b[:, 0:SCD], 0.0,
                          None, ALU.max, inc=VE)
            p.vector.call("tensor_copy", bici[:], ps_b[:, 0:96], inc=VE)
        if ss >= 3:
            p.scalar.wait(VE, VE.v)
            p.scalar.call("activation", sp1[:], bici[:, 0:SCD], AF.Abs)
            p.scalar.call("activation", sp1[:], sp1[:], AF.Exp, scale=-1.0)
            p.scalar.call("activation", sp2[:], sp1[:], AF.Ln,
                          bias=onec[:], inc=SC)
        if ss >= 4:
            p.vector.wait(SC, SC.v)
            p.vector.call("tensor_add", dtm[:, 0:SCD], dtm[:, 0:SCD], sp2[:],
                          inc=VE)
            p.vector.call("tensor_mul", dtm[:, SCD:2 * SCD], dtm[:, 0:SCD],
                          ztm[:], inc=VE)
        dtz_v = VE.v
        if cfg.get("steps", 99) < 6:
            tile_done[t] = VE.v
            continue
        p.tensor.wait(VE, dtz_v)
        p.tensor.wait(SC, SC.v)
        p.tensor.call("transpose", ps_trf[:, 0:128][0:SCD, :],
                      dtm[:, 0:SCD], sid[0:128, 0:128])
        p.tensor.call("transpose", ps_trf[:, 128:256][0:SCD, :],
                      dtm[:, SCD:2 * SCD], sid[0:128, 0:128])
        p.tensor.call("transpose", ps_trf[:, 256:384][0:STD, :],
                      bici[:, 64:64 + STD], sid[0:128, 0:128])
        p.tensor.call("transpose", ps_trf[:, 384:512][0:STD, :],
                      bici[:, 64 + STD:96], sid[0:128, 0:128], inc=TT)
        p.scalar.wait(TT)
        p.scalar.call("copy", dtF[:, t * 128:(t + 1) * 128],
                      ps_trf[:, 0:128][0:SCD, :])
        p.scalar.call("copy", dtzF[:, t * 128:(t + 1) * 128],
                      ps_trf[:, 128:256][0:SCD, :])
        p.scalar.call("copy", BiF[:, t * 128:(t + 1) * 128],
                      ps_trf[:, 256:384][0:STD, :])
        p.scalar.call("copy", CiF[:, t * 128:(t + 1) * 128],
                      ps_trf[:, 384:512][0:STD, :], inc=SC)
        tile_done[t] = VE.v

    phaseA = (SC.v, VE.v, TT.v)
    if cfg.get("nophaseb"):
        p.sync.wait(STR, STR.v)
        p.sync.wait(X1, X1.v)
        return p
    p.tensor.wait(SC, SC.v)
    p.tensor.wait(VE, VE.v)
    p.tensor.wait(LD, ld_consts)
    for ch in range(8):
        cs = slice(ch * 512, (ch + 1) * 512)
        for j in range(8):
            p.tensor.wait(VE, VE.v)
            p.tensor.call("matmul", ps_a[:], ec[:, j * 128:(j + 1) * 128],
                          dtF[:, cs], start=True, stop=True)
            p.tensor.call("matmul", ps_c[:], ec[:, j * 128:(j + 1) * 128],
                          dtzF[:, cs], start=True, stop=True)
            p.tensor.call("matmul", ps_b[:], est[:], BiF[:, cs],
                          start=True, stop=True, inc=TT)
            rep_v = TT.v
            p.vector.wait(TT, rep_v)
            p.vector.call("tensor_scalar", a_sb[:], ps_a[:],
                          acol[:, j:j + 1], 1.0, ALU.mult, ALU.add, inc=VE)
            p.vector.call("tensor_copy", u0_sb[:], ps_c[:], inc=VE)
            p.vector.call("tensor_mul", u_sb[:], u0_sb[:], ps_b[:], inc=VE)
            if cfg.get("noscan"):
                p.vector.call("tensor_copy", states[:], u_sb[:], inc=VE)
            elif ch == 0:
                p.vector.call("tensor_tensor_scan", states[:], a_sb[:],
                              u_sb[:], 0.0, ALU.mult, ALU.add, inc=VE)
            else:
                p.vector.call("tensor_tensor_scan", states[:], a_sb[:],
                              u_sb[:], carry[:, j:j + 1], ALU.mult, ALU.add,
                              inc=VE)
            p.vector.call("tensor_copy", carry[:, j:j + 1],
                          states[:, 511:512], inc=VE)
            p.tensor.wait(VE, VE.v)
            p.tensor.call("matmul", ps_b[:], est[:], CiF[:, cs],
                          start=True, stop=True, inc=TT)
            p.vector.wait(TT, TT.v)
            p.vector.call("tensor_mul", prod[:], states[:], ps_b[:], inc=VE)
            prod_v = VE.v
            p.tensor.wait(VE, prod_v)
            p.tensor.call("matmul", ps_y[0:SCD, :],
                          gmat[:, j * SCD:(j + 1) * SCD], prod[:],
                          start=(j == 0), stop=(j == 7),
                          inc=(TT if j == 7 else None))
        y_v = TT.v
        p.vector.wait(TT, y_v)
        p.vector.call("tensor_mul", ygb[:], ps_y[0:SCD, :], gF[0:SCD, cs],
                      inc=VE)
        yg_v = VE.v
        for i in range(4):
            gt = ch * 4 + i
            p.tensor.wait(VE, yg_v if i == 0 else VE.v)
            for s in range(2):
                p.tensor.call("matmul", ps_p[s][:],
                              ygb[:, i * 128:(i + 1) * 128],
                              woutb[:, s * 512:(s + 1) * 512],
                              start=True, stop=True,
                              inc=(TT if s == 1 else None))
            pj_v = TT.v
            p.sync.wait(VE, VE.v)
            p.sync.wait(STR, STR.v)
            p.sync.dma(xstage[:, 0:D], x_d[gt * 128:(gt + 1) * 128, :], LD)
            p.sync.wait(X1, X1.v)
            xld = p.sync.dma(x1r[:], x1scr[gt * 128:(gt + 1) * 128, :], LD)
            p.vector.wait(TT, pj_v)
            p.vector.wait(LD, xld)
            p.vector.call("tensor_add", ystage[:], xstage[:, 0:D], x1r[:],
                          inc=VE)
            p.vector.call("tensor_add", ystage[:, 0:512], ystage[:, 0:512],
                          ps_p[0][:])
            p.vector.call("tensor_add", ystage[:, 512:1024],
                          ystage[:, 512:1024], ps_p[1][:], inc=VE)
            fy_v = VE.v
            p.sync.wait(VE, fy_v)
            if io:
                io["ywrite"](gt, ystage, STR)
            else:
                p.sync.dma(y_out[gt * 128:(gt + 1) * 128, :], ystage[:], STR)
    p.sync.wait(STR, STR.v)
    p.sync.wait(X1, X1.v)
    return p



_PROG_CACHE = {}

CFG_L0A = dict(name="l0a", H=16, dh=64, causal=False, eps2=EPS_LN,
               gamma_fold=True, beta_qkv=True)
CFG_L0B = dict(name="l0b", H=16, dh=64, causal=True, rope=True, eps2=None,
               bias_rows=True, w_f16=True)
CFG_L1A = dict(name="l1a", H=16, dh=64, causal=True, eps2=1.1920929e-07,
               gamma_fold=True)
CFG_L1B = dict(name="l1b", H=8, dh=128, causal=True, eps2=None)
CFG_FFN0A = dict(name="ffn0a", router="topk_softmax", rms_eps=1e-6, nfold=True,
                 skip_ln=True, combine="plain", bcmb=True, gated=True)
CFG_FFN0B = dict(name="ffn0b", router="topk_softmax", rms_eps=1e-6, nfold=True,
                 skip_ln=True, combine="fp16", bcmb=True, gated=True)
CFG_FFN1A = dict(name="ffn1a", router="softmax_topk", rms_eps=None, nfold=False,
                 skip_ln=False, combine=None, shared=True, moe_bias=True,
                 gated=False)
CFG_FFN1B = dict(name="ffn1b", router="softmax_topk", rms_eps=None, nfold=False,
                 skip_ln=False, combine="plain", shared=True, moe_bias=True,
                 gated=False)


def _prog(key, builder, cfg):
    if key not in _PROG_CACHE:
        _PROG_CACHE[key] = builder(cfg).emit_and_finish()
    return _PROG_CACHE[key]


def _colvec(v):
    return np.ascontiguousarray(np.asarray(v, np.float32).reshape(8, 128).T)


def _identf():
    return np.eye(128, dtype=np.float32)


def _cmask():
    k = np.arange(128)
    return np.ascontiguousarray(
        np.where(k[:, None] > k[None, :], np.float32(NEG), np.float32(0.0)))


def _iota8():
    return np.arange(8, dtype=np.float32)[None]


def _rope_tables():
    dh = 64
    pos = np.arange(T, dtype=np.float32)
    inv = np.exp(np.arange(0, dh, 2, dtype=np.float32) * (-np.log(10000.0) / dh))
    ang = pos[:, None] * inv[None, :]
    sin = np.sin(ang).astype(np.float16).astype(np.float32)
    cos = np.cos(ang).astype(np.float16).astype(np.float32)
    return (np.ascontiguousarray(np.tile(cos, (1, 8))),
            np.ascontiguousarray(np.tile(sin, (1, 8))))


def _scan_consts():
    ec = np.zeros((64, 8 * 128), np.float32)
    for j in range(8):
        for pp in range(128):
            ec[8 * j + pp // 16, j * 128 + pp] = 1.0
    est = np.zeros((16, 128), np.float32)
    for pp in range(128):
        est[pp % 16, pp] = 1.0
    g = np.zeros((128, 8 * 64), np.float32)
    for j in range(8):
        for pp in range(128):
            g[pp, j * 64 + 8 * j + pp // 16] = 1.0
    return ec, est, g


def _run(nc, in_maps):
    r = run_bass_kernel_spmd(nc, in_maps, core_ids=list(range(8)))
    return r.results


def _attn_launch(state, cfg, wmaps):
    nc = _prog(cfg["name"], build_attention, cfg)
    in_maps = []
    for c in range(8):
        m = dict(wmaps[c % 2])
        m["xb"] = np.ascontiguousarray(state[c // 2])
        in_maps.append(m)
    res = _run(nc, in_maps)
    return ([res[c]["y"] for c in range(8)],
            [res[c]["st_out"] for c in range(8)])


def _ffn_launch(state, cfg, wshared, partials=None, key=None):
    nc = _prog(key or cfg["name"], build_ffn, cfg)
    in_maps = []
    for c in range(8):
        b = c // 2
        half = c % 2
        rows = slice(half * 2048, (half + 1) * 2048)
        m = dict(wshared)
        m["xo"] = np.ascontiguousarray(state[b][rows])
        if partials is not None:
            m["pA"] = np.ascontiguousarray(partials[2 * b][rows])
            m["pB"] = np.ascontiguousarray(partials[2 * b + 1][rows])
        in_maps.append(m)
    res = _run(nc, in_maps)
    new_state = state.copy()
    for c in range(8):
        b, half = c // 2, c % 2
        new_state[b][half * 2048:(half + 1) * 2048] = res[c]["y"]
    return new_state


def kernel_unfused(**inputs):
    g = lambda k: np.asarray(inputs[k])
    x = g("L0_attn_a_x").astype(np.float32)
    state = np.ascontiguousarray(x.reshape(4, T, D))
    identf = _identf()
    cmask = _cmask()
    iota = _iota8()
    rcos, rsin = _rope_tables()

    Wqkv = g("L0_attn_a_W_qkv")
    bqkv = g("L0_attn_a_b_qkv")
    wmaps = []
    for pi in range(2):
        sl = slice(pi * 512, (pi + 1) * 512)
        wmaps.append(dict(
            wq=np.ascontiguousarray(Wqkv[:, 0:1024][:, sl]),
            wkv=np.ascontiguousarray(np.hstack([Wqkv[:, 1024:2048][:, sl],
                                                Wqkv[:, 2048:3072][:, sl]])),
            wp=np.ascontiguousarray(g("L0_attn_a_W_proj")[sl, :]),
            bq=np.ascontiguousarray(bqkv[0:1024][sl])[None],
            bkv=np.ascontiguousarray(np.hstack([bqkv[1024:2048][sl],
                                                bqkv[2048:3072][sl]]))[None],
            gvec=_colvec(g("L0_attn_a_ln_gamma")),
            bvec=_colvec(g("L0_attn_a_ln_beta")),
            identf=identf))
    parts, _ = _attn_launch(state, CFG_L0A, wmaps)

    wupT = g("L0_ffn_w_up").transpose(0, 2, 1)
    wdnT = g("L0_ffn_w_down").transpose(0, 2, 1)
    wu = np.ascontiguousarray(np.concatenate(
        [np.concatenate([wupT[e][:, h * 1024:(h + 1) * 1024],
                         wupT[e][:, 2048 + h * 1024:2048 + (h + 1) * 1024]],
                        axis=1)
         for e in range(8) for h in range(2)], axis=0))
    wd = np.ascontiguousarray(np.concatenate(
        [wdnT[e][h * 1024:(h + 1) * 1024, :]
         for e in range(8) for h in range(2)], axis=0))
    ffn0_w = dict(wrt=np.ascontiguousarray(g("L0_ffn_w_router").T),
                  wup=wu, wdn=wd,
                  gvec=_colvec(g("L0_ffn_norm_scale")),
                  identf=identf, iota=iota)
    w0a = dict(ffn0_w)
    w0a["bcmb"] = np.ascontiguousarray(g("L0_attn_a_b_proj"))[None]
    state = _ffn_launch(state, CFG_FFN0A, w0a, parts)

    wmaps = []
    for pi in range(2):
        sl = slice(pi * 512, (pi + 1) * 512)
        wmaps.append(dict(
            wq=np.ascontiguousarray(g("L0_attn_b_Wq")[:, sl]),
            wkv=np.ascontiguousarray(np.hstack([g("L0_attn_b_Wk")[:, sl],
                                                g("L0_attn_b_Wv")[:, sl]])),
            wp=np.ascontiguousarray(g("L0_attn_b_Wo")[sl, :]),
            bq=np.ascontiguousarray(
                g("L0_attn_b_bq").astype(np.float32)[sl])[None],
            bkv=np.ascontiguousarray(np.hstack(
                [g("L0_attn_b_bk").astype(np.float32)[sl],
                 g("L0_attn_b_bv").astype(np.float32)[sl]]))[None],
            rcos=rcos, rsin=rsin, cmask=cmask, identf=identf))
    parts, _ = _attn_launch(state, CFG_L0B, wmaps)

    w0b = dict(ffn0_w)
    w0b["bcmb"] = np.ascontiguousarray(
        g("L0_attn_b_bo").astype(np.float32))[None]
    state = _ffn_launch(state, CFG_FFN0B, w0b, parts)

    qkvT = np.ascontiguousarray(g("L1_attn_a_qkv_w").T)
    wmaps = []
    for pi in range(2):
        sl = slice(pi * 512, (pi + 1) * 512)
        wmaps.append(dict(
            wq=np.ascontiguousarray(qkvT[:, 0:1024][:, sl]),
            wkv=np.ascontiguousarray(np.hstack([qkvT[:, 1024:2048][:, sl],
                                                qkvT[:, 2048:3072][:, sl]])),
            wp=np.ascontiguousarray(g("L1_attn_a_o_w").T[sl, :]),
            gvec=_colvec(g("L1_attn_a_norm1_w")),
            cmask=cmask, identf=identf))
    parts, stats = _attn_launch(state, CFG_L1A, wmaps)

    ec, est, gmat = _scan_consts()
    nc = _prog("scan", build_scan, dict(name="scan"))
    in_maps = []
    for c in range(8):
        b = c // 2
        m = dict(
            xb=np.ascontiguousarray(state[b]),
            pA=np.ascontiguousarray(parts[2 * b]),
            pB=np.ascontiguousarray(parts[2 * b + 1]),
            st_in=stats[2 * b],
            wzg=np.ascontiguousarray(np.hstack([g("L1_attn_a_in_w").T,
                                                g("L1_attn_a_gate_w").T])),
            wdt=np.ascontiguousarray(np.hstack([g("L1_attn_a_dt_w").T,
                                                g("L1_attn_a_Bp_w").T,
                                                g("L1_attn_a_Cp_w").T])),
            dtb=np.ascontiguousarray(np.concatenate(
                [g("L1_attn_a_dt_b"), np.zeros(32, np.float32)]))[None],
            alog=np.ascontiguousarray(
                g("L1_attn_a_A_log").reshape(-1).reshape(8, 128).T),
            ec=ec, est=est, gmat=gmat,
            wout=np.ascontiguousarray(g("L1_attn_a_out_w").T),
            gvec=_colvec(g("L1_attn_a_norm2_w")),
            identf=identf)
        in_maps.append(m)
    res = _run(nc, in_maps)
    for b in range(4):
        state[b] = res[2 * b]["y"]

    rw1 = g("L1_ffn_rw1")
    rw2 = g("L1_ffn_rw2")
    wu = np.ascontiguousarray(np.concatenate(
        [rw1[e][:, h * 1024:(h + 1) * 1024]
         for e in range(8) for h in range(2)], axis=0))
    wd = np.ascontiguousarray(np.concatenate(
        [rw2[e][h * 1024:(h + 1) * 1024, :]
         for e in range(8) for h in range(2)], axis=0))
    sw1 = g("L1_ffn_sw1")
    ffn1_w = dict(
        wrt=np.ascontiguousarray(g("L1_ffn_router_w")),
        brt=np.ascontiguousarray(g("L1_ffn_router_b"))[None],
        wup=wu, wdn=wd,
        wsu=np.ascontiguousarray(np.concatenate(
            [sw1[:, 0:1024], sw1[:, 1024:2048]], axis=0)),
        wsd=np.ascontiguousarray(g("L1_ffn_sw2")),
        bs1=np.ascontiguousarray(g("L1_ffn_sb1").reshape(2, 1024)),
        bs2=np.ascontiguousarray(g("L1_ffn_sb2"))[None],
        rb1=np.ascontiguousarray(g("L1_ffn_rb1").reshape(16, 1024)),
        rb2=np.ascontiguousarray(g("L1_ffn_rb2")),
        identf=identf, iota=iota)
    state = _ffn_launch(state, CFG_FFN1A, ffn1_w)

    wmaps = []
    for pi in range(2):
        sl = slice(pi * 512, (pi + 1) * 512)
        wmaps.append(dict(
            wq=np.ascontiguousarray(g("L1_attn_b_Wq").T[:, sl]),
            wkv=np.ascontiguousarray(np.hstack([g("L1_attn_b_Wk").T[:, sl],
                                                g("L1_attn_b_Wv").T[:, sl]])),
            wp=np.ascontiguousarray(g("L1_attn_b_Wo").T[sl, :]),
            cmask=cmask, identf=identf))
    parts, _ = _attn_launch(state, CFG_L1B, wmaps)

    state = _ffn_launch(state, CFG_FFN1B, ffn1_w, parts)
    return np.ascontiguousarray(state.reshape(4, T, D))


I32 = mybir.dt.int32


def build_fused():
    p = Prog("fused")
    nc = p.nc
    p.init_barrier_sems()
    p.init_walls({F32: 98_000_000, FP16: 2_200_000,
                  ("internal", BF16): 12_000_000,
                  ("internal", F32): 24_000_000})
    state = nc.dram_tensor("state", [T, D], F32, kind="Internal",
                           addr_space="Shared").ap()
    parts = nc.dram_tensor("parts", [2 * T, D], F32, kind="Internal",
                           addr_space="Shared").ap()
    flags = nc.dram_tensor("flags", [1, 32], I32, kind="Internal",
                           addr_space="Shared").ap()
    xb_in = nc.dram_tensor("xb", [T, D], F32, kind="ExternalInput").ap()
    nonce_in = nc.dram_tensor("nonce", [1, 2], I32, kind="ExternalInput").ap()
    y_final = nc.dram_tensor("y", [2048, D], F32, kind="ExternalOutput").ap()
    partA = parts[0:T, :]
    partB = parts[T:2 * T, :]

    stats_l1a = p.sbuf("statsP", [128, NT_FULL * 6], F32)
    snon = p.sbuf("snon", [1, 2], I32)
    fpoll = p.sbuf("fpoll", [1, 1], I32)
    FL = p.sem("FLG")
    NLD = p.sem("NLD")

    H = {}

    def init_vals(eng):
        pid = eng.partition_id()
        par = eng.compute_val(pid % 2)
        rp = eng.alloc_register("rpar")
        eng.reg_mov(rp, par)
        H["rp"] = rp

    p.sync.dma(snon[:], nonce_in, NLD)
    p.sync.wait(NLD, NLD.v)
    p.q["sync"].ops.append(("__raw__", [init_vals], {}, None, 0))

    def pair_dma(dstA, dstB, srcA, srcB, S):
        def go(eng, dA=dstA, dB=dstB, sA=srcA, sB=srcB, sh=S.h):
            with eng.If_eq(H["rp"], 0):
                eng.dma_start(dA, sA).then_inc(sh, 16)
            with eng.Else():
                eng.dma_start(dB, sB).then_inc(sh, 16)
        p.q["sync"].ops.append(("__raw__", [go], {}, None, 0))
        S.v += 16
        return S.v

    def publish(stage):
        def go(eng, se=2 * stage, so=2 * stage + 1, sh=FL.h):
            with eng.If_eq(H["rp"], 0):
                eng.dma_start(flags[0:1, se:se + 1], snon[:, 0:1]).then_inc(sh, 16)
            with eng.Else():
                eng.dma_start(flags[0:1, so:so + 1], snon[:, 0:1]).then_inc(sh, 16)
        p.q["sync"].ops.append(("__raw__", [go], {}, None, 0))
        FL.v += 16
        p.sync.wait(FL, FL.v)

    def poll(stage):
        PL = p.sem(f"PL{stage}")
        slot_e = 2 * stage + 1
        slot_o = 2 * stage

        def go(eng):
            with eng.register(f"pv{stage}") as pv, \
                 eng.register(f"ex{stage}") as ex, \
                 eng.register(f"rn{stage}") as rn:
                nv = eng.value_load(snon[:, 0:1])
                eng.reg_mov(rn, nv)
                eng.reg_mov(pv, 0)
                eng.reg_mov(ex, 16)

                def cond():
                    return (eng.snap(pv, min_val=-(1 << 31),
                                     max_val=(1 << 31) - 1)
                            != eng.snap(rn, min_val=-(1 << 31),
                                        max_val=(1 << 31) - 1))
                with eng.While(cond):
                    with eng.If_eq(H["rp"], 0):
                        eng.dma_start(fpoll[:, 0:1],
                                      flags[0:1, slot_e:slot_e + 1]).then_inc(PL.h, 16)
                    with eng.Else():
                        eng.dma_start(fpoll[:, 0:1],
                                      flags[0:1, slot_o:slot_o + 1]).then_inc(PL.h, 16)
                    eng.wait_ge(PL.h, ex)
                    eng.reg_add(ex, ex, 16)
                    v = eng.value_load(fpoll[:, 0:1])
                    eng.reg_mov(pv, v)
        p.q["sync"].ops.append(("__raw__", [go], {}, None, 0))

    def attn_io(xb_ap, stats_buf=None):
        def ywrite(gt, ystage, STR):
            pair_dma(partA[gt * 128:(gt + 1) * 128, :],
                     partB[gt * 128:(gt + 1) * 128, :],
                     ystage[:], ystage[:], STR)
        io = {"xb": xb_ap, "ywrite": ywrite}
        if stats_buf is not None:
            io["stats_buf"] = stats_buf
        return io

    def ffn_io(x_src, final=False):
        def load_xo(dst, t, LD):
            return pair_dma(dst, dst,
                            x_src[t * 128:(t + 1) * 128, :],
                            x_src[2048 + t * 128:2048 + (t + 1) * 128, :], LD)

        def load_part(dst, which, t, LD):
            src = partA if which == 0 else partB
            return pair_dma(dst, dst,
                            src[t * 128:(t + 1) * 128, :],
                            src[2048 + t * 128:2048 + (t + 1) * 128, :], LD)

        def ywrite(t, ystage, STR):
            if final:
                p.sync.dma(y_final[t * 128:(t + 1) * 128, :], ystage[:], STR)
            else:
                pair_dma(state[t * 128:(t + 1) * 128, :],
                         state[2048 + t * 128:2048 + (t + 1) * 128, :],
                         ystage[:], ystage[:], STR)
        return {"load_xo": load_xo, "load_part": load_part, "ywrite": ywrite}

    def scan_io():
        def ywrite(gt, ystage, STR):
            def go(eng, gt=gt, ap=ystage[:], sh=STR.h):
                with eng.If_eq(H["rp"], 0 if gt < 16 else 1):
                    eng.dma_start(state[gt * 128:(gt + 1) * 128, :],
                                  ap).then_inc(sh, 16)
                with eng.Else():
                    eng.dma_start(fpoll[:, 0:1], flags[0:1, 30:31]).then_inc(sh, 16)
            p.q["sync"].ops.append(("__raw__", [go], {}, None, 0))
            STR.v += 16
        return {"xb": state, "pA": partA, "pB": partB,
                "stats_buf": stats_l1a, "ywrite": ywrite}

    p.stage_begin("l0a")
    build_attention(CFG_L0A, p=p, io=attn_io(xb_in))
    publish(1)
    p.stage_end()

    p.stage_begin("f0a")
    p.alias_group = "ffn0"
    poll(1)
    build_ffn(CFG_FFN0A, p=p, io=ffn_io(xb_in))
    publish(2)
    p.stage_end()

    p.stage_begin("l0b")
    poll(2)
    build_attention(CFG_L0B, p=p, io=attn_io(state))
    publish(3)
    p.stage_end()

    p.stage_begin("f0b")
    p.alias_group = "ffn0"
    poll(3)
    build_ffn(CFG_FFN0B, p=p, io=ffn_io(state))
    publish(4)
    p.stage_end()

    p.stage_begin("l1a")
    poll(4)
    build_attention(CFG_L1A, p=p, io=attn_io(state, stats_buf=stats_l1a))
    publish(5)
    p.stage_end()

    p.stage_begin("scn")
    poll(5)
    build_scan(dict(name="scan"), p=p, io=scan_io())
    p.stage_end()

    p.stage_begin("f1a")
    p.alias_group = "ffn1"
    build_ffn(CFG_FFN1A, p=p, io=ffn_io(state))
    publish(7)
    p.stage_end()

    p.stage_begin("l1b")
    poll(7)
    build_attention(CFG_L1B, p=p, io=attn_io(state))
    publish(8)
    p.stage_end()

    p.stage_begin("f1b")
    p.alias_group = "ffn1"
    poll(8)
    build_ffn(CFG_FFN1B, p=p, io=ffn_io(state, final=True))
    p.stage_end()
    return p


def _fused_values(inputs):
    g = lambda k: np.asarray(inputs[k])
    identf = _identf()
    cmask = _cmask()
    iota = _iota8()
    rcos, rsin = _rope_tables()
    V = {}

    def add(group, name, v0, v1=None):
        V[(group, name)] = [v0, v0 if v1 is None else v1]

    Wqkv = g("L0_attn_a_W_qkv")
    bqkv = g("L0_attn_a_b_qkv")
    sls = [slice(0, 512), slice(512, 1024)]

    def two(fn):
        return [fn(sls[0]), fn(sls[1])]

    a, b = two(lambda sl: np.ascontiguousarray(Wqkv[:, 0:1024][:, sl]))
    add("l0a_", "wq", a, b)
    a, b = two(lambda sl: np.ascontiguousarray(
        np.hstack([Wqkv[:, 1024:2048][:, sl], Wqkv[:, 2048:3072][:, sl]])))
    add("l0a_", "wkv", a, b)
    a, b = two(lambda sl: np.ascontiguousarray(g("L0_attn_a_W_proj")[sl, :]))
    add("l0a_", "wp", a, b)
    a, b = two(lambda sl: np.ascontiguousarray(bqkv[0:1024][sl])[None])
    add("l0a_", "bq", a, b)
    a, b = two(lambda sl: np.ascontiguousarray(
        np.hstack([bqkv[1024:2048][sl], bqkv[2048:3072][sl]]))[None])
    add("l0a_", "bkv", a, b)
    add("l0a_", "gvec", _colvec(g("L0_attn_a_ln_gamma")))
    add("l0a_", "bvec", _colvec(g("L0_attn_a_ln_beta")))
    add("l0a_", "identf", identf)

    a, b = two(lambda sl: np.ascontiguousarray(g("L0_attn_b_Wq")[:, sl]))
    add("l0b_", "wq", a, b)
    a, b = two(lambda sl: np.ascontiguousarray(
        np.hstack([g("L0_attn_b_Wk")[:, sl], g("L0_attn_b_Wv")[:, sl]])))
    add("l0b_", "wkv", a, b)
    a, b = two(lambda sl: np.ascontiguousarray(g("L0_attn_b_Wo")[sl, :]))
    add("l0b_", "wp", a, b)
    a, b = two(lambda sl: np.ascontiguousarray(
        g("L0_attn_b_bq").astype(np.float32)[sl])[None])
    add("l0b_", "bq", a, b)
    a, b = two(lambda sl: np.ascontiguousarray(np.hstack(
        [g("L0_attn_b_bk").astype(np.float32)[sl],
         g("L0_attn_b_bv").astype(np.float32)[sl]]))[None])
    add("l0b_", "bkv", a, b)
    add("l0b_", "rcos", rcos)
    add("l0b_", "rsin", rsin)
    add("l0b_", "cmask", cmask)
    add("l0b_", "identf", identf)

    qkvT = np.ascontiguousarray(g("L1_attn_a_qkv_w").T)
    a, b = two(lambda sl: np.ascontiguousarray(qkvT[:, 0:1024][:, sl]))
    add("l1a_", "wq", a, b)
    a, b = two(lambda sl: np.ascontiguousarray(
        np.hstack([qkvT[:, 1024:2048][:, sl], qkvT[:, 2048:3072][:, sl]])))
    add("l1a_", "wkv", a, b)
    a, b = two(lambda sl: np.ascontiguousarray(g("L1_attn_a_o_w").T[sl, :]))
    add("l1a_", "wp", a, b)
    add("l1a_", "gvec", _colvec(g("L1_attn_a_norm1_w")))
    add("l1a_", "cmask", cmask)
    add("l1a_", "identf", identf)

    a, b = two(lambda sl: np.ascontiguousarray(g("L1_attn_b_Wq").T[:, sl]))
    add("l1b_", "wq", a, b)
    a, b = two(lambda sl: np.ascontiguousarray(
        np.hstack([g("L1_attn_b_Wk").T[:, sl], g("L1_attn_b_Wv").T[:, sl]])))
    add("l1b_", "wkv", a, b)
    a, b = two(lambda sl: np.ascontiguousarray(g("L1_attn_b_Wo").T[sl, :]))
    add("l1b_", "wp", a, b)
    add("l1b_", "cmask", cmask)
    add("l1b_", "identf", identf)

    wupT = g("L0_ffn_w_up").transpose(0, 2, 1)
    wdnT = g("L0_ffn_w_down").transpose(0, 2, 1)
    wu = np.ascontiguousarray(np.concatenate(
        [np.concatenate([wupT[e][:, h * 1024:(h + 1) * 1024],
                         wupT[e][:, 2048 + h * 1024:2048 + (h + 1) * 1024]],
                        axis=1)
         for e in range(8) for h in range(2)], axis=0))
    wd = np.ascontiguousarray(np.concatenate(
        [wdnT[e][h * 1024:(h + 1) * 1024, :]
         for e in range(8) for h in range(2)], axis=0))
    add("ffn0", "wrt", np.ascontiguousarray(g("L0_ffn_w_router").T))
    add("ffn0", "wup", wu)
    add("ffn0", "wdn", wd)
    add("ffn0", "gvec", _colvec(g("L0_ffn_norm_scale")))
    add("ffn0", "identf", identf)
    add("ffn0", "iota", iota)
    add("f0a_", "bcmb", np.ascontiguousarray(g("L0_attn_a_b_proj"))[None])
    add("f0b_", "bcmb", np.ascontiguousarray(
        g("L0_attn_b_bo").astype(np.float32))[None])

    ec, est, gmat = _scan_consts()
    add("scn_", "wzg", np.ascontiguousarray(
        np.hstack([g("L1_attn_a_in_w").T, g("L1_attn_a_gate_w").T])))
    add("scn_", "wdt", np.ascontiguousarray(
        np.hstack([g("L1_attn_a_dt_w").T, g("L1_attn_a_Bp_w").T,
                   g("L1_attn_a_Cp_w").T])))
    add("scn_", "dtb", np.ascontiguousarray(np.concatenate(
        [g("L1_attn_a_dt_b"), np.zeros(32, np.float32)]))[None])
    add("scn_", "alog", np.ascontiguousarray(
        g("L1_attn_a_A_log").reshape(-1).reshape(8, 128).T))
    add("scn_", "ec", ec)
    add("scn_", "est", est)
    add("scn_", "gmat", gmat)
    add("scn_", "wout", np.ascontiguousarray(g("L1_attn_a_out_w").T))
    add("scn_", "gvec", _colvec(g("L1_attn_a_norm2_w")))
    add("scn_", "identf", identf)

    rw1 = g("L1_ffn_rw1")
    rw2 = g("L1_ffn_rw2")
    wu = np.ascontiguousarray(np.concatenate(
        [rw1[e][:, h * 1024:(h + 1) * 1024]
         for e in range(8) for h in range(2)], axis=0))
    wd = np.ascontiguousarray(np.concatenate(
        [rw2[e][h * 1024:(h + 1) * 1024, :]
         for e in range(8) for h in range(2)], axis=0))
    sw1 = g("L1_ffn_sw1")
    add("ffn1", "wrt", np.ascontiguousarray(g("L1_ffn_router_w")))
    add("ffn1", "brt", np.ascontiguousarray(g("L1_ffn_router_b"))[None])
    add("ffn1", "wup", wu)
    add("ffn1", "wdn", wd)
    add("ffn1", "wsu", np.ascontiguousarray(np.concatenate(
        [sw1[:, 0:1024], sw1[:, 1024:2048]], axis=0)))
    add("ffn1", "wsd", np.ascontiguousarray(g("L1_ffn_sw2")))
    add("ffn1", "bs1", np.ascontiguousarray(g("L1_ffn_sb1").reshape(2, 1024)))
    add("ffn1", "bs2", np.ascontiguousarray(g("L1_ffn_sb2"))[None])
    add("ffn1", "rb1", np.ascontiguousarray(g("L1_ffn_rb1").reshape(16, 1024)))
    add("ffn1", "rb2", np.ascontiguousarray(g("L1_ffn_rb2")))
    add("ffn1", "identf", identf)
    add("ffn1", "iota", iota)
    return V


_FUSED_CACHE = {}


def kernel_fused(**inputs):
    import secrets
    x = np.asarray(inputs["L0_attn_a_x"]).astype(np.float32)
    state0 = np.ascontiguousarray(x.reshape(4, T, D))
    if "fused" not in _FUSED_CACHE:
        pb = build_fused()
        nc = pb.emit_and_finish()
        _FUSED_CACHE["fused"] = (nc, {dt: (w.tensor.name, spec,
                                           int(w.tensor.shape[1]))
                                      for dt, (w, spec, pos) in
                                      pb.walls.items()})
    nc, wallinfo = _FUSED_CACHE["fused"]
    V = _fused_values(inputs)
    np_dt = {F32: np.float32, FP16: np.float16}
    walls = {}
    for dt, (wname, spec, ndecl) in wallinfo.items():
        arrs = [np.zeros((1, ndecl), dtype=np_dt[dt]) for _ in range(2)]
        for (group, name, shape, off) in spec:
            v = V[(group, name)]
            n = int(np.prod(shape))
            for pi in range(2):
                arrs[pi][0, off:off + n] = np.asarray(v[pi], np_dt[dt]).ravel()
        walls[wname] = arrs
    nonce = np.array([[secrets.randbits(31) | 1, 0]], dtype=np.int32)
    in_maps = []
    for c in range(8):
        m = {wname: arrs[c % 2] for wname, arrs in walls.items()}
        m["xb"] = np.ascontiguousarray(state0[c // 2])
        m["nonce"] = nonce
        in_maps.append(m)
    res = run_bass_kernel_spmd(nc, in_maps, core_ids=list(range(8)))
    out = np.empty((4, T, D), dtype=np.float32)
    for c in range(8):
        b, half = c // 2, c % 2
        out[b][half * 2048:(half + 1) * 2048] = res.results[c]["y"]
    return out


FUSED = True


def kernel(**inputs):
    if FUSED:
        return kernel_fused(**inputs)
    return kernel_unfused(**inputs)


# version 11 (final)
# speedup vs baseline: 2.4813x; 1.2215x over previous
import math
import numpy as np
import ml_dtypes
import concourse.bass as bass
import concourse.mybir as mybir
from concourse.bass_utils import run_bass_kernel_spmd
from contextlib import ExitStack

F32 = mybir.dt.float32
BF16 = mybir.dt.bfloat16
FP16 = mybir.dt.float16
AF = mybir.ActivationFunctionType
ALU = mybir.AluOpType
AXX = mybir.AxisListType.X

D = 1024
T = 4096
NT_FULL = 32
NCH = 8
BF = ml_dtypes.bfloat16
EPS_LN = 1e-5
NEG = -30000.0

ENGS = ("sync", "scalar", "vector", "tensor", "gpsimd")


def own_qtiles(core):
    if core % 2 == 0:
        return list(range(0, 8)) + list(range(24, 32))
    return list(range(8, 24))


def own_row_slices(core):
    if core % 2 == 0:
        return [(0, 1024), (3072, 4096)]
    return [(1024, 3072)]


class Sem:
    __slots__ = ("h", "v", "name")

    def __init__(self, nc, name):
        self.h = nc.alloc_semaphore(name)
        self.v = 0
        self.name = name


class Lazy:
    __slots__ = ("fn",)

    def __init__(self, fn):
        self.fn = fn

    def __call__(self):
        return self.fn()


class EngQ:
    def __init__(self, auto_sem=None):
        self.ops = []
        self.auto_sem = auto_sem

    def call(self, method, *args, inc=None, incn=1, **kwargs):
        if inc is None and self.auto_sem is not None and method != "dma_start":
            inc = self.auto_sem
        if inc is not None:
            inc.v += incn
        self.ops.append((method, args, kwargs, inc, incn))
        if self.auto_sem is not None and inc is not None and method != "dma_start":
            self.ops.append(("wait_ge", (inc.h, inc.v), {}, None, 0))
        return inc.v if inc is not None else None

    def wait(self, sem, val=None):
        v = sem.v if val is None else val
        self.ops.append(("wait_ge", (sem.h, v), {}, None, 0))

    def dma(self, dst, src, sem, **kw):
        return self.call("dma_start", dst, src, inc=sem, incn=16, **kw)


class Prog:
    def __init__(self, name):
        self.nc = bass.Bass("TRN2", target_bir_lowering=False)
        self.ctx = ExitStack()
        self.name = name
        self.sbuf_bytes = 0
        self.prefix = ""
        self.q = {e: EngQ() for e in ENGS}
        self.q["vector"].auto_sem = Sem(self.nc, "VEauto")
        self.q["scalar"].auto_sem = Sem(self.nc, "SCauto")
        self.gsem = None

    def init_barrier_sems(self):
        self.gsem = {e: Sem(self.nc, f"G_{e}") for e in ENGS}

    def stage_begin(self, prefix):
        self.prefix = prefix + "_"
        self.alias_group = None
        self._stage_ctx = ExitStack()
        self._saved_bytes = self.sbuf_bytes
        self._saved_ctx, self.ctx = self.ctx, self._stage_ctx
        if self.gsem is not None:
            for e in ENGS:
                for e2 in ENGS:
                    self.q[e].wait(self.gsem[e2])

    def stage_end(self):
        if self.gsem is not None:
            for e in ENGS:
                self.q[e].call("nop", inc=self.gsem[e])
        self.ctx = self._saved_ctx
        self._stage_ctx.close()
        self.sbuf_bytes = self._saved_bytes
        self.prefix = ""

    def __getattr__(self, item):
        if item in ENGS:
            return self.q[item]
        raise AttributeError(item)

    def sem(self, name):
        if not hasattr(self, "_semcache"):
            self._semcache = {}
        if name not in self._semcache:
            self._semcache[name] = Sem(self.nc, name)
        return self._semcache[name]

    def sbuf(self, name, shape, dtype):
        self.sbuf_bytes += int(np.prod(shape[1:])) * mybir.dt.size(dtype)
        assert self.sbuf_bytes <= 206000, \
            f"{self.name}: SBUF over budget at {name}: {self.sbuf_bytes // 1024}KB/part"
        return self.ctx.enter_context(
            self.nc.sbuf_tensor(self.prefix + name, list(shape), dtype))

    def psum(self, name, shape, dtype):
        return self.ctx.enter_context(
            self.nc.psum_tensor(self.prefix + name, list(shape), dtype))

    def din(self, name, shape, dtype, no_alias=False):
        if getattr(self, "walls", None) is None:
            return self.nc.dram_tensor(self.prefix + name, list(shape), dtype,
                                       kind="ExternalInput").ap()
        group = (getattr(self, "alias_group", None) or self.prefix) \
            if not no_alias else self.prefix
        key = (group, name)
        if key in self.wall_cache:
            return self.wall_cache[key]
        wall, spec, pos = self.walls[dtype]
        n = int(np.prod(shape))
        off = pos[0]
        spec.append((group, name, tuple(shape), off))
        pos[0] = off + n
        ap = wall[0, off:off + n]
        if len(shape) == 2:
            ap = ap.rearrange("(r c) -> r c", c=shape[1])
        self.wall_cache[key] = ap
        return ap

    def dint(self, name, shape, dtype):
        if getattr(self, "walls", None) is None:
            return self.nc.dram_tensor(self.prefix + name, list(shape), dtype,
                                       kind="Internal").ap()
        wall, spec, pos = self.iwall[dtype]
        n = int(np.prod(shape))
        off = pos[0]
        pos[0] = off + n
        ap = wall[0, off:off + n]
        if len(shape) == 2:
            ap = ap.rearrange("(r c) -> r c", c=shape[1])
        return ap

    def init_walls(self, sizes):
        self.walls = {}
        self.iwall = {}
        self.wall_cache = {}
        self.alias_group = None
        for dt_, n in sizes.items():
            if isinstance(dt_, tuple):
                t = self.nc.dram_tensor(f"iwall_{mybir.dt.size(dt_[1])}",
                                        [1, n], dt_[1], kind="Internal").ap()
                self.iwall[dt_[1]] = (t, [], [0])
            else:
                t = self.nc.dram_tensor(f"wall_{mybir.dt.size(dt_)}{dt_}",
                                        [1, n], dt_, kind="ExternalInput").ap()
                self.walls[dt_] = (t, [], [0])

    def emit_and_finish(self):
        def replay(eng, ops):
            for method, args, kwargs, sem, incn in ops:
                if method == "__raw__":
                    args[0](eng)
                    continue
                args = [a() if isinstance(a, Lazy) else a for a in args]
                kwargs = {k: (v() if isinstance(v, Lazy) else v)
                          for k, v in kwargs.items()}
                ins = getattr(eng, method)(*args, **kwargs)
                if sem is not None:
                    ins.then_inc(sem.h, incn)

        with self.nc.Block() as blk:
            for e in ENGS:
                if self.q[e].ops:
                    getattr(blk, e)(lambda eng, ops=self.q[e].ops: replay(eng, ops))
        self.ctx.close()
        self.nc.finalize()
        return self.nc



def build_attention(cfg, p=None, io=None):
    H, dh = cfg["H"], cfg["dh"]
    Hm = H // 2
    P2 = 2 if dh == 64 else 1
    NP2 = Hm // P2
    VW = P2 * dh + (2 if P2 == 2 else 1)
    causal = cfg["causal"]
    rope = cfg.get("rope", False)
    eps2 = cfg.get("eps2")
    standalone = p is None
    if standalone:
        p = Prog(cfg["name"])
    nc = p.nc
    wdt = FP16 if cfg.get("w_f16") else F32
    scale = 1.0 / math.sqrt(dh)
    has_bias = cfg.get("beta_qkv") or cfg.get("bias_rows")
    DKV = D
    DQ = D // 2
    DP = D // 2

    xb = io["xb"] if io else p.din("xb", [T, D], F32)
    wkv_d = p.din("wkv", [D, DKV], wdt)
    wq_d = p.din("wq", [D, DQ], wdt)
    wp_d = p.din("wp", [DP, D], wdt)
    identf = p.din("identf", [128, 128], F32)
    gvec = p.din("gvec", [128, 8], F32) if cfg.get("gamma_fold") else None
    bvec = p.din("bvec", [128, 8], F32) if cfg.get("beta_qkv") else None
    bq_d = bkv_d = None
    if has_bias:
        bq_d = p.din("bq", [1, DQ], F32)
        bkv_d = p.din("bkv", [1, DKV], F32)
    cmask = p.din("cmask", [128, 128], F32) if causal else None
    if rope:
        rcos_d = p.din("rcos", [T, DQ // 2], F32)
        rsin_d = p.din("rsin", [T, DQ // 2], F32)
    if standalone:
        y_out = nc.dram_tensor("y", [T, D], F32, kind="ExternalOutput").ap()
        st_out = nc.dram_tensor("st_out", [128, NT_FULL * 6], F32,
                                kind="ExternalOutput").ap()
    kscr = p.dint("kscr", [128, NP2 * T], BF16)

    stats = (io.get("stats_buf") if io else None) or \
        p.sbuf("stats", [128, NT_FULL * 6], F32)
    xstage = p.sbuf("xstage", [128, 2 * D], F32)
    xnb = p.sbuf("xnb", [128, D], BF16)
    xnT = p.sbuf("xnT", [128, D], BF16)
    qkvtm = p.sbuf("qkvtm", [128, DKV + DQ], BF16)
    ktT = p.sbuf("ktT", [128, DP], BF16)
    Vb = p.sbuf("Vb", [128, NT_FULL * NP2 * VW], BF16)
    Qf = p.sbuf("Qf", [128, NP2 * T], BF16)
    kblk = p.sbuf("kblk", [128, T], BF16)
    Wres = p.sbuf("Wres", [128, NCH * (DKV + DQ)], BF16)
    Wpj = p.sbuf("Wpj", [128, (DP // 128) * D], BF16)
    Wstage = p.sbuf("Wstage", [128, 2 * 512], wdt)
    PTb = p.sbuf("PTb", [128, 4 * 512], BF16)
    ctmp = p.sbuf("ctmp", [128, 2 * 512], F32)
    ctxTM = p.sbuf("ctxTM", [128, 4 * (Hm * dh)], BF16)
    ctxFT = p.sbuf("ctxFT", [128, Hm * dh], BF16)
    ystage = p.sbuf("ystage", [128, D], F32)
    sid = p.sbuf("sid", [128, 128], F32)
    sidb = p.sbuf("sidb", [128, 128], BF16)
    onesf = p.sbuf("onesf", [1, 128], F32)
    biasq = p.sbuf("biasq", [1, DQ], F32)
    biaskv = p.sbuf("biaskv", [1, DKV], F32)
    gcol = p.sbuf("gcol", [128, 8], F32)
    bcolf = p.sbuf("bcolf", [128, 8], F32)
    bcol = p.sbuf("bcol", [128, 8], BF16)
    kn2 = p.sbuf("kn2", [128, Hm], F32)
    qn2 = p.sbuf("qn2", [128, Hm], F32)
    nrm = p.sbuf("nrm", [128, 16], F32)
    Mb = p.sbuf("Mb", [128, Hm], F32)
    mrow = p.sbuf("mrow", [1, 16], F32)
    smask = p.sbuf("smask", [128, 128], F32) if causal else None
    rc = p.sbuf("rc", [128, DQ // 2], F32) if rope else None
    rs = p.sbuf("rs", [128, DQ // 2], F32) if rope else None
    qkr = p.sbuf("qkr", [128, DKV + DQ], BF16) if rope else None

    ps_g = [p.psum(f"psg{i}", [128, 512], F32) for i in range(4)]
    ps_ctx = [p.psum(f"psc{i}", [128, 512], F32) for i in range(2)]
    ps_tr = p.psum("ps_tr", [128, 1024], BF16)
    ps_trb = p.psum("ps_trb", [128, 1024], BF16)

    LD = p.sem("LD")
    WL = p.sem("WL")
    WLb = p.sem("WLb")
    KSM = p.sem("KSM")
    GP = p.sem("GP")
    SC = p.sem("SC")
    VE = p.sem("VE")
    TT = p.sem("TT")
    STR = p.sem("STR")

    if eps2 is None:
        va, vb_ = 1.0, EPS_LN
    else:
        va, vb_ = 1.0 + eps2, EPS_LN * eps2

    p.sync.dma(sid[:], identf, LD)
    if causal:
        p.sync.dma(smask[:], cmask, LD)
    if cfg.get("gamma_fold"):
        p.sync.dma(gcol[:], gvec, LD)
    if has_bias:
        p.sync.dma(biasq[:, :], bq_d, LD)
        p.sync.dma(biaskv[:, :], bkv_d, LD)
    ld_consts = LD.v
    p.vector.wait(LD, ld_consts)
    p.vector.call("tensor_copy", sidb[:], sid[:], inc=VE)

    p.gpsimd.call("memset", onesf[:], 1.0)
    p.gpsimd.call("memset", Vb[:], 1.0)
    p.gpsimd.call("memset", kn2[:], 0.0)
    p.gpsimd.call("memset", qn2[:], 0.0, inc=GP)
    CB = p.sem("CB")
    if cfg.get("beta_qkv"):
        p.gpsimd.dma(bcolf[:], bvec, CB)
        p.vector.wait(CB, CB.v)
        p.vector.wait(LD, ld_consts)
        p.vector.call("reciprocal", nrm[:, 0:8], gcol[:], inc=VE)
        p.vector.wait(VE, VE.v)
        p.vector.call("tensor_mul", bcol[:], bcolf[:], nrm[:, 0:8], inc=VE)
    cb_consts = CB.v
    bcol_v = VE.v

    CW = DKV + DQ
    wl_cast_vals = []
    slot = [0]

    def stream_weights(wd, nchunks, dout, dst_buf, dst_cw, roff, fold):
        for c in range(nchunks):
            for s in range(dout // 512):
                sl0 = (slot[0] % 2) * 512
                wsem = WL if slot[0] % 2 == 0 else WLb
                if slot[0] >= 2:
                    p.sync.wait(VE, wl_cast_vals[slot[0] - 2])
                p.sync.dma(Wstage[:, sl0:sl0 + 512],
                           wd[c * 128:(c + 1) * 128, s * 512:(s + 1) * 512], wsem)
                p.vector.wait(wsem)
                dst = dst_buf[:, c * dst_cw + roff + s * 512:
                              c * dst_cw + roff + (s + 1) * 512]
                if fold:
                    p.vector.wait(LD, ld_consts)
                    p.vector.call("tensor_scalar", dst, Wstage[:, sl0:sl0 + 512],
                                  gcol[:, c:c + 1], None, ALU.mult, inc=VE)
                else:
                    p.vector.call("tensor_copy", dst, Wstage[:, sl0:sl0 + 512],
                                  inc=VE)
                wl_cast_vals.append(VE.v)
                slot[0] += 1

    fold_qkv = bool(cfg.get("gamma_fold"))
    stream_weights(wkv_d, NCH, DKV, Wres, CW, 0, fold_qkv)
    stream_weights(wq_d, NCH, DQ, Wres, CW, DKV, fold_qkv)
    stream_weights(wp_d, DP // 128, D, Wpj, D, 0, False)
    w_ready = VE.v

    if cfg.get("beta_qkv"):
        p.tensor.wait(VE, max(w_ready, bcol_v))
        p.tensor.wait(GP, GP.v)
        p.tensor.wait(CB, cb_consts)
        for (brow, roff, dout) in ((biaskv, 0, DKV), (biasq, DKV, DQ)):
            for s in range(dout // 512):
                p.tensor.wait(VE, VE.v)
                for c in range(NCH):
                    p.tensor.call("matmul", ps_ctx[0][0:1, 0:512], bcol[:, c:c + 1],
                                  Wres[:, c * CW + roff + s * 512:
                                       c * CW + roff + (s + 1) * 512],
                                  start=(c == 0), stop=(c == NCH - 1),
                                  inc=(TT if c == NCH - 1 else None))
                p.vector.wait(TT)
                p.vector.wait(LD, ld_consts)
                p.vector.call("tensor_add", brow[:, s * 512:(s + 1) * 512],
                              brow[:, s * 512:(s + 1) * 512],
                              ps_ctx[0][0:1, 0:512], inc=VE)
    bias_v = VE.v

    NS = (DKV + DQ) // 512
    tile_consumed = [0] * NT_FULL
    p.tensor.wait(VE, max(w_ready, bias_v))
    p.tensor.wait(GP, GP.v)
    for t in range(NT_FULL):
        gt = t
        if t >= 1:
            p.sync.wait(VE, tile_consumed[t - 1])
        x_ld = p.sync.dma(xstage[:, (t % 2) * D:(t % 2 + 1) * D],
                          xb[gt * 128:(gt + 1) * 128, :], LD)
        if rope:
            p.sync.dma(rc[:], rcos_d[gt * 128:(gt + 1) * 128, :], LD)
            p.sync.dma(rs[:], rsin_d[gt * 128:(gt + 1) * 128, :], LD)
            x_ld = LD.v
        st6 = stats[:, gt * 6:(gt + 1) * 6]
        mu, s_ap, r1 = st6[:, 0:1], st6[:, 1:2], st6[:, 2:3]
        w_ap, a_ap, b_ap = st6[:, 3:4], st6[:, 4:5], st6[:, 5:6]
        xt = xstage[:, (t % 2) * D:(t % 2 + 1) * D]
        p.scalar.wait(LD, x_ld)
        p.scalar.wait(TT, TT.v)
        p.scalar.call("activation", xnb[:], xt, AF.Square, accum_out=b_ap,
                      inc=SC)
        sq_v = SC.v
        p.vector.wait(LD, x_ld)
        p.vector.call("reduce_sum", a_ap, xt, axis=AXX, inc=VE)
        p.vector.wait(VE)
        p.vector.call("tensor_scalar", mu, a_ap, 1.0 / D, None, ALU.mult)
        p.vector.wait(SC, sq_v)
        p.vector.call("tensor_scalar", w_ap, b_ap, 1.0 / D, None, ALU.mult)
        p.vector.call("tensor_scalar", a_ap, mu, mu, None, ALU.mult)
        p.vector.call("tensor_sub", w_ap, w_ap, a_ap, inc=VE)
        p.vector.wait(VE)
        p.vector.call("tensor_scalar", b_ap, w_ap, va, vb_, ALU.mult, ALU.add,
                      inc=VE)
        p.scalar.wait(VE, VE.v)
        p.scalar.call("activation", a_ap, b_ap, AF.Ln)
        p.scalar.call("activation", s_ap, a_ap, AF.Exp, scale=-0.5, inc=SC)
        p.vector.wait(SC, SC.v)
        p.vector.call("tensor_scalar", b_ap, w_ap, 1.0, EPS_LN,
                      ALU.mult, ALU.add, inc=VE)
        p.scalar.wait(VE, VE.v)
        p.scalar.call("activation", a_ap, b_ap, AF.Ln)
        p.scalar.call("activation", r1, a_ap, AF.Exp, scale=-0.5, inc=SC)
        s_v = SC.v
        p.vector.wait(SC, s_v)
        p.vector.call("tensor_scalar", xnb[:], xt, mu, s_ap,
                      ALU.subtract, ALU.mult, inc=VE)
        xn_v = VE.v
        p.tensor.wait(VE, xn_v)
        p.tensor.wait(SC, SC.v)
        for c in range(NCH):
            p.tensor.call("transpose", ps_tr[:, c * 128:(c + 1) * 128],
                          xnb[:, c * 128:(c + 1) * 128], sidb[0:128, 0:128],
                          inc=(TT if c == NCH - 1 else None))
        p.scalar.wait(TT)
        p.scalar.call("copy", xnT[:], ps_tr[:], inc=SC)
        ev_v = SC.v
        p.tensor.wait(SC, ev_v)
        for s in range(NS):
            if has_bias:
                p.tensor.wait(LD, ld_consts)
                bsrc = biaskv[:, s * 512:(s + 1) * 512] if s < 2 else \
                    biasq[:, 0:512]
                p.tensor.call("matmul", ps_g[s][:], onesf[:, 0:128], bsrc,
                              start=True, stop=False)
            for c in range(NCH):
                p.tensor.call("matmul", ps_g[s][:],
                              xnT[:, c * 128:(c + 1) * 128],
                              Wres[:, c * CW + s * 512:c * CW + (s + 1) * 512],
                              start=(c == 0 and not has_bias),
                              stop=(c == NCH - 1),
                              inc=(TT if c == NCH - 1 else None))
        qkv_done = TT.v
        p.vector.wait(TT, qkv_done)
        p.vector.call("tensor_copy", qkvtm[:, 0:512], ps_g[0][:], inc=VE)
        p.vector.call("tensor_scalar", qkvtm[:, 1024:1536], ps_g[2][:],
                      scale, None, ALU.mult, inc=VE)
        k_ev = VE.v
        vview = Vb[:].rearrange("p (t n w) -> p t n w", t=NT_FULL, n=NP2)
        if P2 == 2:
            dst3 = Vb[:].rearrange("p (t n o w) -> p t n o w",
                                   t=NT_FULL, n=NP2, o=2)[:, gt, :, :, 0:dh]
            srcv = ps_g[1][:].rearrange("p (n o w) -> p n o w", n=NP2, o=2)
        else:
            dst3 = vview[:, gt, :, 0:dh]
            srcv = ps_g[1][:].rearrange("p (n w) -> p n w", n=NP2)
        p.scalar.wait(TT, qkv_done)
        p.scalar.call("copy", dst3, srcv, inc=SC)
        ksrc_off = 0
        qsrc_off = 1024
        src_buf = qkvtm
        if rope:
            p.vector.wait(VE, k_ev)
            for (off, dst_off) in ((0, 0), (1024, 1024)):
                k3 = qkvtm[:, off:off + 512].rearrange("p (h d) -> p h d", h=Hm)
                kd3 = qkr[:, dst_off:dst_off + 512].rearrange("p (h d) -> p h d", h=Hm)
                c3 = rc[:].rearrange("p (h d) -> p h d", h=Hm)
                s3 = rs[:].rearrange("p (h d) -> p h d", h=Hm)
                t1 = ctmp[:, 0:256].rearrange("p (h d) -> p h d", h=Hm)
                t2 = ctmp[:, 256:512].rearrange("p (h d) -> p h d", h=Hm)
                p.vector.call("tensor_mul", t1, c3, k3[:, :, 0:dh // 2])
                p.vector.call("tensor_mul", t2, s3, k3[:, :, dh // 2:dh])
                p.vector.call("tensor_sub", kd3[:, :, 0:dh // 2], t1, t2)
                p.vector.call("tensor_mul", t1, s3, k3[:, :, 0:dh // 2])
                p.vector.call("tensor_mul", t2, c3, k3[:, :, dh // 2:dh])
                p.vector.call("tensor_add", kd3[:, :, dh // 2:dh], t1, t2,
                              inc=VE)
            src_buf = qkr
        for (off, acc) in ((ksrc_off, kn2), (qsrc_off, qn2)):
            sl = src_buf[:, off:off + 512]
            p.scalar.wait(VE, VE.v)
            p.scalar.call("activation", ctmp[:, 0:512], sl, AF.Square, inc=SC)
            p.vector.wait(SC, SC.v)
            p.vector.call("reduce_sum", nrm[:, 0:Hm],
                          ctmp[:, 0:512].rearrange("p (a b) -> p a b", b=dh),
                          axis=AXX, inc=VE)
            p.vector.wait(VE)
            p.vector.call("tensor_max", acc[:, 0:Hm], acc[:, 0:Hm],
                          nrm[:, 0:Hm], inc=VE)
        p.tensor.wait(VE, VE.v)
        p.tensor.wait(KSM, KSM.v)
        for c in range(4):
            p.tensor.call("transpose", ps_trb[:, c * 128:(c + 1) * 128],
                          src_buf[:, ksrc_off + c * 128:ksrc_off + (c + 1) * 128],
                          sidb[0:128, 0:128],
                          inc=(TT if c == 3 else None))
        p.scalar.wait(TT)
        p.scalar.wait(KSM, KSM.v)
        p.scalar.call("copy", ktT[:], ps_trb[:, 0:512], inc=SC)
        kev = SC.v
        p.gpsimd.wait(SC, kev)
        p.gpsimd.dma(kscr.rearrange("p (c t) -> p c t", c=NP2)
                     [:, :, gt * 128:(gt + 1) * 128],
                     ktT[:].rearrange("p (c t) -> p c t", c=NP2), KSM)
        p.tensor.wait(SC, SC.v)
        for c in range(4):
            p.tensor.call("transpose", ps_trb[:, c * 128:(c + 1) * 128],
                          src_buf[:, qsrc_off + c * 128:qsrc_off + (c + 1) * 128],
                          sidb[0:128, 0:128],
                          inc=(TT if c == 3 else None))
        p.scalar.wait(TT)
        p.scalar.call("copy",
                      Qf[:].rearrange("p (c t) -> p c t", c=NP2)
                      [:, :, gt * 128:(gt + 1) * 128],
                      ps_trb[:, 0:512].rearrange("p (c t) -> p c t", c=NP2),
                      inc=SC)
        tile_consumed[t] = VE.v

    p.tensor.wait(VE, VE.v)
    p.tensor.call("transpose", ps_ctx[0][0:Hm, 0:128], kn2[:, 0:Hm],
                  sid[0:128, 0:128])
    p.tensor.call("transpose", ps_ctx[1][0:Hm, 0:128], qn2[:, 0:Hm],
                  sid[0:128, 0:128], inc=TT)
    p.vector.wait(TT)
    p.vector.call("reduce_max", nrm[0:Hm, 0:1], ps_ctx[0][0:Hm, 0:128],
                  axis=AXX)
    p.vector.call("reduce_max", nrm[0:Hm, 1:2], ps_ctx[1][0:Hm, 0:128],
                  axis=AXX, inc=VE)
    p.vector.wait(VE)
    p.vector.call("tensor_mul", nrm[0:Hm, 2:3], nrm[0:Hm, 0:1],
                  nrm[0:Hm, 1:2], inc=VE)
    p.scalar.wait(VE, VE.v)
    p.scalar.call("activation", nrm[0:Hm, 3:4], nrm[0:Hm, 2:3], AF.Sqrt,
                  inc=SC)
    p.vector.wait(SC, SC.v)
    p.vector.call("tensor_scalar", nrm[0:Hm, 4:5], nrm[0:Hm, 3:4], -1.0, None,
                  ALU.mult, inc=VE)
    p.tensor.wait(VE, VE.v)
    p.tensor.call("transpose", ps_ctx[0][0:1, 0:Hm], nrm[0:Hm, 4:5],
                  sid[0:Hm, 0:Hm], inc=TT)
    p.scalar.wait(TT)
    p.scalar.call("copy", mrow[:, 0:Hm], ps_ctx[0][0:1, 0:Hm], inc=SC)
    p.tensor.wait(SC, SC.v)
    p.tensor.call("matmul", ps_ctx[1][:, 0:Hm], onesf[:, 0:128], mrow[:, 0:Hm],
                  start=True, stop=True, inc=TT)
    p.scalar.wait(TT)
    p.scalar.call("copy", Mb[:, 0:Hm], ps_ctx[1][:, 0:Hm], inc=SC)
    bounds_done = SC.v
    if standalone:
        p.sync.wait(SC, SC.v)
        p.sync.wait(VE, VE.v)
        p.sync.dma(st_out, stats[:], STR)

    for qq in range(8):
        tiles = [qq * 4 + i for i in range(4)]
        ktmax = (max(tiles) + 1) if causal else NT_FULL
        for hp in range(NP2):
            p.gpsimd.wait(TT, TT.v)
            p.gpsimd.wait(KSM, KSM.v)
            p.gpsimd.dma(kblk[:], kscr[:, hp * T:(hp + 1) * T], KSM)
            kblk_v = KSM.v
            p.tensor.wait(KSM, kblk_v)
            p.tensor.wait(SC, SC.v)
            p.tensor.wait(VE, VE.v)
            exp_done = {}
            vv = Vb[:].rearrange("p (t n w) -> p t n w", t=NT_FULL, n=NP2)

            def emit_av(kt, h01, first, last):
                ppk = kt % 2
                pt_src = PTb[:, (ppk * 2 + h01) * 512:
                             (ppk * 2 + h01 + 1) * 512]
                p.tensor.wait(SC, exp_done[(kt, h01)])
                if P2 == 2:
                    p.tensor.call("matmul", ps_ctx[h01][0:65, :],
                                  vv[:, kt, hp, h01 * 65:(h01 + 1) * 65],
                                  pt_src, start=first, stop=last,
                                  inc=(TT if last and h01 == P2 - 1 else None))
                else:
                    p.tensor.call("matmul", ps_ctx[0][0:64, :],
                                  vv[:, kt, hp, 0:64], pt_src,
                                  start=first, stop=last)
                    p.tensor.call("matmul", ps_ctx[1][0:65, :],
                                  vv[:, kt, hp, 64:129], pt_src,
                                  start=first, stop=last,
                                  inc=(TT if last else None))

            for kt in range(ktmax):
                ppk = kt % 2
                st_vs = {}
                for h01 in range(P2):
                    stp = ps_g[ppk * 2 + h01]
                    if P2 == 2:
                        lhsT = kblk[h01 * 64:(h01 + 1) * 64,
                                    kt * 128:(kt + 1) * 128]
                        rhs = Qf[h01 * 64:(h01 + 1) * 64,
                                 hp * T + qq * 512:hp * T + (qq + 1) * 512]
                    else:
                        lhsT = kblk[:, kt * 128:(kt + 1) * 128]
                        rhs = Qf[:, hp * T + qq * 512:hp * T + (qq + 1) * 512]
                    if kt >= 2:
                        p.tensor.wait(SC, exp_done[(kt - 2, h01)])
                    p.tensor.call("matmul", stp[:], lhsT, rhs,
                                  start=True, stop=True, inc=TT)
                    st_vs[h01] = TT.v
                for h01 in range(P2):
                    h = hp * P2 + h01
                    stp = ps_g[ppk * 2 + h01]
                    st_v = st_vs[h01]
                    mk_v = None
                    if causal:
                        did = False
                        for i, g in enumerate(tiles):
                            if g == kt or g < kt:
                                if not did:
                                    p.vector.wait(TT, st_v)
                                if g == kt:
                                    p.vector.call("tensor_add",
                                                  stp[:, i * 128:(i + 1) * 128],
                                                  stp[:, i * 128:(i + 1) * 128],
                                                  smask[:], inc=VE)
                                else:
                                    p.vector.call("memset",
                                                  stp[:, i * 128:(i + 1) * 128],
                                                  NEG, inc=VE)
                                did = True
                        mk_v = VE.v if did else None
                    p.scalar.wait(TT, st_v)
                    if mk_v is not None:
                        p.scalar.wait(VE, mk_v)
                    pt_dst = PTb[:, (ppk * 2 + h01) * 512:(ppk * 2 + h01 + 1) * 512]
                    p.scalar.call("activation", pt_dst, stp[:], AF.Exp,
                                  bias=Mb[:, h:h + 1], inc=SC)
                    exp_done[(kt, h01)] = SC.v
                if kt >= 1:
                    for h01 in range(P2):
                        emit_av(kt - 1, h01, kt - 1 == 0, False)
            for h01 in range(P2):
                emit_av(ktmax - 1, h01, ktmax == 1, True)
            av_v = TT.v
            p.scalar.wait(TT, av_v)
            if P2 == 2:
                p.scalar.call("copy", ctmp[0:65, 0:512], ps_ctx[0][0:65, :])
                p.scalar.call("copy", ctmp[0:65, 512:1024], ps_ctx[1][0:65, :],
                              inc=SC)
            else:
                p.scalar.call("copy", ctmp[0:64, 0:512], ps_ctx[0][0:64, :])
                p.scalar.call("copy", ctmp[0:65, 512:1024], ps_ctx[1][0:65, :],
                              inc=SC)
            ce_v = SC.v
            p.tensor.wait(SC, ce_v)
            for h01 in range(2):
                nr = 65 if (P2 == 2 or h01 == 1) else 64
                for i in range(4):
                    p.tensor.call(
                        "transpose",
                        ps_ctx[h01][:, 0:288].rearrange("p (a b) -> p a b", a=4)
                        [:, i, 0:nr],
                        ctmp[0:nr, h01 * 512 + i * 128:h01 * 512 + (i + 1) * 128],
                        sid[0:nr, 0:nr],
                        inc=(TT if i == 3 else None))
            p.vector.wait(TT, TT.v)
            if P2 == 2:
                for h01 in range(2):
                    cc = ps_ctx[h01][:, 0:288].rearrange("p (a b) -> p a b", a=4)
                    for i in range(4):
                        p.vector.call("reciprocal", nrm[:, 8:9], cc[:, i, 64:65],
                                      inc=VE)
                        p.vector.wait(VE, VE.v)
                        dt_ = ctxTM[:, i * (Hm * dh) + (hp * 2 + h01) * dh:
                                    i * (Hm * dh) + (hp * 2 + h01) * dh + dh]
                        p.vector.call("tensor_scalar", dt_, cc[:, i, 0:dh],
                                      nrm[:, 8:9], None, ALU.mult, inc=VE)
            else:
                c0 = ps_ctx[0][:, 0:288].rearrange("p (a b) -> p a b", a=4)
                c1 = ps_ctx[1][:, 0:288].rearrange("p (a b) -> p a b", a=4)
                for i in range(4):
                    p.vector.call("reciprocal", nrm[:, 8:9], c1[:, i, 64:65],
                                  inc=VE)
                    p.vector.wait(VE, VE.v)
                    dt_ = ctxTM[:, i * (Hm * dh) + hp * dh:
                                i * (Hm * dh) + hp * dh + dh]
                    p.vector.call("tensor_scalar", dt_[:, 0:64], c0[:, i, 0:64],
                                  nrm[:, 8:9], None, ALU.mult)
                    p.vector.call("tensor_scalar", dt_[:, 64:128], c1[:, i, 0:64],
                                  nrm[:, 8:9], None, ALU.mult, inc=VE)
        ctx_done = VE.v

        NCP = DP // 128
        for i in range(4):
            gt = tiles[i]
            p.tensor.wait(VE, ctx_done)
            p.tensor.wait(SC, SC.v)
            for c in range(NCP):
                p.tensor.call("transpose", ps_tr[:, c * 128:(c + 1) * 128],
                              ctxTM[:, i * (Hm * dh) + c * 128:
                                    i * (Hm * dh) + (c + 1) * 128],
                              sidb[0:128, 0:128],
                              inc=(TT if c == NCP - 1 else None))
            p.scalar.wait(TT)
            p.scalar.call("copy", ctxFT[:], ps_tr[:, 0:DP], inc=SC)
            ev_v = SC.v
            p.tensor.wait(SC, ev_v)
            p.tensor.wait(VE, VE.v)
            for s in range(2):
                for c in range(NCP):
                    p.tensor.call("matmul", ps_g[s][:],
                                  ctxFT[:, c * 128:(c + 1) * 128],
                                  Wpj[:, c * D + s * 512:c * D + (s + 1) * 512],
                                  start=(c == 0), stop=(c == NCP - 1),
                                  inc=(TT if c == NCP - 1 else None))
            pj_v = TT.v
            p.vector.wait(TT, pj_v)
            p.vector.wait(STR, STR.v)
            p.vector.call("tensor_copy", ystage[:, 0:512], ps_g[0][:])
            p.vector.call("tensor_copy", ystage[:, 512:1024], ps_g[1][:],
                          inc=VE)
            y_v = VE.v
            p.sync.wait(VE, y_v)
            if io:
                io["ywrite"](gt, ystage, STR)
            else:
                p.sync.dma(y_out[gt * 128:(gt + 1) * 128, :], ystage[:], STR)
    p.sync.wait(STR, STR.v)
    p.sync.wait(KSM, KSM.v)
    return p



def build_ffn(cfg, p=None, io=None):
    standalone = p is None
    if standalone:
        p = Prog(cfg["name"])
    nc = p.nc
    NT = 16
    HH = 1024
    rms_eps = cfg.get("rms_eps")
    shared = cfg.get("shared", False)
    moe_bias = cfg.get("moe_bias", False)
    combine = cfg.get("combine")
    l0mode = cfg["router"] == "topk_softmax"

    xo_d = None if io else p.din("xo", [2048, D], F32)
    pA_d = pB_d = bcmb_d = None
    if combine and not io:
        pA_d = p.din("pA", [2048, D], F32)
        pB_d = p.din("pB", [2048, D], F32)
    if cfg.get("bcmb"):
        bcmb_d = p.din("bcmb", [1, D], F32, no_alias=True)
    wrt_d = p.din("wrt", [D, 8], F32)
    brt_d = p.din("brt", [1, 8], F32) if not l0mode else None
    gated = cfg.get("gated", True)
    upw = 2 * HH if gated else HH
    wup_d = p.din("wup", [16 * D, upw], F32)
    wdn_d = p.din("wdn", [16 * HH, D], F32)
    gvec = p.din("gvec", [128, 8], F32) if cfg.get("nfold") else None
    if shared:
        wsu_d = p.din("wsu", [2 * D, HH], F32)
        wsd_d = p.din("wsd", [2 * HH, D], F32)
        bs1_d = p.din("bs1", [2, HH], F32)
        bs2_d = p.din("bs2", [1, D], F32)
    if moe_bias:
        rb1_d = p.din("rb1", [16, HH], F32)
        rb2_d = p.din("rb2", [8, D], F32)
    identf = p.din("identf", [128, 128], F32)
    iota_d = p.din("iota", [1, 8], F32)
    if standalone:
        y_out = nc.dram_tensor("y", [2048, D], F32, kind="ExternalOutput").ap()
    ybscr = p.dint("ybscr", [2048, D], F32)

    stats = p.sbuf("stats", [128, NT * 6], F32)
    xstage = p.sbuf("xstage", [128, D], F32)
    pstage = p.sbuf("pstage", [128, D], F32) if combine else None
    xnb = p.sbuf("xnb", [128, D], F32)
    xnT = p.sbuf("xnT", [128, NT * D], BF16)
    hT = p.sbuf("hT", [128, NT * HH], BF16)
    oa = [p.sbuf("oa0", [128, NT * 512], F32),
          p.sbuf("oa1", [128, NT * 512], F32)]
    Wsl = p.sbuf("Wsl", [128, NCH * 2048], BF16)
    Wstage = p.sbuf("Wstage", [128, 8], F32)
    hstage = p.sbuf("hstage", [128, HH], BF16)
    gstage = p.sbuf("gstage", [128, HH], BF16)
    gsc = p.sbuf("gsc", [128, 1024], F32) if gated else None
    ub_tile = p.sbuf("ub_tile", [128, 1024], F32) if (
        cfg.get("shared") or cfg.get("moe_bias")) else None
    db_tile = p.sbuf("db_tile", [128, 1024], F32) if (
        cfg.get("shared") or cfg.get("moe_bias")) else None
    sid = p.sbuf("sid", [128, 128], F32)
    sidb = p.sbuf("sidb", [128, 128], BF16)
    onesf = p.sbuf("onesf", [1, 128], F32)
    iota_b = p.sbuf("iota_b", [128, 8], F32)
    rsc = p.sbuf("rsc", [128, 24], F32)
    wsel = p.sbuf("wsel", [128, NT * 8], F32)
    has_b = cfg.get("shared") or cfg.get("moe_bias")
    UBO = D if cfg.get("bcmb") else 0
    brow = p.sbuf("brow", [1, UBO + (HH if has_b else 0) or D], F32)
    b2row = p.sbuf("b2row", [1, D], F32) if has_b else None
    gcol = p.sbuf("gcol", [128, 8], F32)
    wrtb = p.sbuf("wrtb", [128, 8 * 8], BF16)
    ystage = p.sbuf("ystage", [128, D], F32)
    c16 = p.sbuf("c16", [128, D], FP16) if combine == "fp16" else None

    ps_g = [p.psum(f"psg{i}", [128, 512], F32) for i in range(4)]
    ps_r = p.psum("ps_r", [128, 512], F32)
    ps_tr = p.psum("ps_tr", [128, 1024], BF16)
    ps_trf = p.psum("ps_trf", [128, 1024], F32)

    LD = p.sem("LD")
    WL = p.sem("WL")
    WLb = p.sem("WLb")
    SC = p.sem("SC")
    VE = p.sem("VE")
    TT = p.sem("TT")
    STR = p.sem("STR")
    GP = p.sem("GP")
    CB = p.sem("CB")

    p.sync.dma(sid[:], identf, LD)
    p.sync.dma(iota_b[0:1, :], iota_d, LD)
    if cfg.get("nfold"):
        p.sync.dma(gcol[:], gvec, LD)
    if cfg.get("bcmb"):
        p.sync.dma(brow[:, 0:D], bcmb_d, LD)
    if not l0mode:
        p.sync.dma(rsc[0:1, 16:24], brt_d, LD)
    ld_consts = LD.v
    p.vector.wait(LD, ld_consts)
    p.vector.call("tensor_copy", sidb[:], sid[:], inc=VE)
    p.gpsimd.call("memset", onesf[:], 1.0)
    p.gpsimd.call("memset", oa[0][:], 0.0)
    p.gpsimd.call("memset", oa[1][:], 0.0, inc=GP)
    p.tensor.wait(LD, ld_consts)
    p.tensor.wait(GP, GP.v)
    p.tensor.call("matmul", ps_r[:, 0:8], onesf[:, 0:128], iota_b[0:1, 0:8],
                  start=True, stop=True, inc=TT)
    p.scalar.wait(TT)
    p.scalar.call("copy", iota_b[:, 0:8], ps_r[:, 0:8], inc=SC)
    if not l0mode:
        p.tensor.wait(SC, SC.v)
        p.tensor.call("matmul", ps_r[:, 8:16], onesf[:, 0:128], rsc[0:1, 16:24],
                      start=True, stop=True, inc=TT)
        p.scalar.wait(TT)
        p.scalar.call("copy", rsc[:, 16:24], ps_r[:, 8:16], inc=SC)
    iota_v = SC.v

    first_stream = [True]

    def stream_w(wd, r0, c0, nch, dout, dst, dcw, roff):
        if first_stream[0]:
            first_stream[0] = False
        else:
            p.gpsimd.wait(TT, TT.v)
        for c in range(nch):
            p.gpsimd.dma(dst[:, c * dcw + roff:c * dcw + roff + dout],
                         wd[r0 + c * 128:r0 + (c + 1) * 128, c0:c0 + dout],
                         CB)
        return CB.v

    for c in range(NCH):
        if c > 0:
            p.sync.wait(VE, VE.v)
        p.sync.dma(Wstage[:, 0:8], wrt_d[c * 128:(c + 1) * 128, :], WL)
        p.vector.wait(WL)
        p.vector.call("tensor_copy", wrtb[:, c * 8:(c + 1) * 8],
                      Wstage[:, 0:8], inc=VE)
    wrt_v = VE.v

    eps1 = EPS_LN
    if rms_eps is None:
        va, vb_ = 1.0, eps1
    else:
        va, vb_ = 1.0 + rms_eps, eps1 * rms_eps
    tile_done = [0] * NT
    for t in range(NT):
        if t >= 1:
            p.sync.wait(VE, tile_done[t - 1])
        if io:
            xld = io["load_xo"](xstage[:, 0:D], t, LD)
        else:
            xld = p.sync.dma(xstage[:, 0:D],
                             xo_d[t * 128:(t + 1) * 128, :], LD)
        xt = xstage[:, 0:D]
        if combine:
            p.sync.wait(STR, STR.v)
            if io:
                io["load_part"](pstage[:, 0:D], 0, t, LD)
                xld = io["load_part"](ystage[:, 0:D], 1, t, LD)
            else:
                p.sync.dma(pstage[:, 0:D], pA_d[t * 128:(t + 1) * 128, :], LD)
                xld = p.sync.dma(ystage[:, 0:D],
                                 pB_d[t * 128:(t + 1) * 128, :], LD)
            p.vector.wait(LD, xld)
            p.vector.call("tensor_add", pstage[:, 0:D], pstage[:, 0:D],
                          ystage[:, 0:D], inc=VE)
            if cfg.get("bcmb"):
                p.tensor.wait(VE, VE.v)
                p.tensor.wait(SC, SC.v)
                p.tensor.call("matmul", ps_g[3][:], onesf[:, 0:128],
                              brow[:, 0:512], start=True, stop=True)
                p.tensor.call("matmul", ps_g[2][:], onesf[:, 0:128],
                              brow[:, 512:1024], start=True, stop=True, inc=TT)
                p.vector.wait(TT, TT.v)
                p.vector.call("tensor_add", pstage[:, 0:512], pstage[:, 0:512],
                              ps_g[3][:])
                p.vector.call("tensor_add", pstage[:, 512:1024],
                              pstage[:, 512:1024], ps_g[2][:], inc=VE)
            if combine == "fp16":
                p.vector.call("tensor_copy", c16[:], pstage[:, 0:D], inc=VE)
                p.vector.call("tensor_add", xt, xt, c16[:], inc=VE)
            else:
                p.vector.call("tensor_add", xt, xt, pstage[:, 0:D], inc=VE)
        st6 = stats[:, t * 6:(t + 1) * 6]
        mu, s_ap, r1 = st6[:, 0:1], st6[:, 1:2], st6[:, 2:3]
        w_ap, a_ap, b_ap = st6[:, 3:4], st6[:, 4:5], st6[:, 5:6]
        p.scalar.wait(LD, xld)
        p.scalar.wait(VE, VE.v)
        p.scalar.wait(TT, TT.v)
        p.scalar.call("activation", xnb[:], xt, AF.Square, accum_out=b_ap,
                      inc=SC)
        p.vector.wait(SC, SC.v)
        p.vector.call("reduce_sum", a_ap, xt, axis=AXX, inc=VE)
        p.vector.wait(VE)
        p.vector.call("tensor_scalar", mu, a_ap, 1.0 / D, None, ALU.mult)
        p.vector.call("tensor_scalar", w_ap, b_ap, 1.0 / D, None, ALU.mult)
        p.vector.call("tensor_scalar", a_ap, mu, mu, None, ALU.mult)
        p.vector.call("tensor_sub", w_ap, w_ap, a_ap, inc=VE)
        p.vector.call("tensor_scalar", b_ap, w_ap, va, vb_, ALU.mult, ALU.add,
                      inc=VE)
        p.scalar.wait(VE, VE.v)
        p.scalar.call("activation", a_ap, b_ap, AF.Ln)
        p.scalar.call("activation", s_ap, a_ap, AF.Exp, scale=-0.5, inc=SC)
        p.vector.wait(SC, SC.v)
        p.vector.call("tensor_scalar", b_ap, w_ap, 1.0, eps1,
                      ALU.mult, ALU.add, inc=VE)
        p.scalar.wait(VE, VE.v)
        p.scalar.call("activation", a_ap, b_ap, AF.Ln)
        p.scalar.call("activation", r1, a_ap, AF.Exp, scale=-0.5, inc=SC)
        p.vector.wait(SC, SC.v)
        p.vector.call("tensor_scalar", xnb[:], xt, mu, s_ap,
                      ALU.subtract, ALU.mult, inc=VE)
        xn_v = VE.v
        p.vector.wait(STR, STR.v)
        if cfg.get("skip_ln"):
            p.vector.call("tensor_scalar", ystage[:], xt, mu, r1,
                          ALU.subtract, ALU.mult, inc=VE)
            p.vector.call("tensor_add", ystage[:], ystage[:], xt, inc=VE)
        else:
            p.vector.call("tensor_copy", ystage[:], xt, inc=VE)
        yb_v = VE.v
        p.sync.wait(VE, yb_v)
        p.sync.dma(ybscr[t * 128:(t + 1) * 128, :], ystage[:], STR)
        p.tensor.wait(VE, xn_v)
        p.tensor.wait(SC, SC.v)
        for c in range(NCH):
            p.tensor.call("transpose", ps_trf[:, c * 128:(c + 1) * 128],
                          xnb[:, c * 128:(c + 1) * 128], sid[0:128, 0:128],
                          inc=(TT if c == NCH - 1 else None))
        p.scalar.wait(TT)
        if cfg.get("nfold"):
            for c in range(NCH):
                p.scalar.call("activation",
                              xnT[:, t * D + c * 128:t * D + (c + 1) * 128],
                              ps_trf[:, c * 128:(c + 1) * 128], AF.Copy,
                              scale=gcol[:, c:c + 1],
                              inc=(SC if c == NCH - 1 else None))
        else:
            p.scalar.call("copy", xnT[:, t * D:(t + 1) * D], ps_trf[:], inc=SC)
        ev_v = SC.v
        p.tensor.wait(SC, max(ev_v, iota_v))
        p.tensor.wait(VE, wrt_v)
        for c in range(NCH):
            p.tensor.call("matmul", ps_r[:, 0:8],
                          xnT[:, t * D + c * 128:t * D + (c + 1) * 128],
                          wrtb[:, c * 8:(c + 1) * 8],
                          start=(c == 0), stop=(c == NCH - 1),
                          inc=(TT if c == NCH - 1 else None))
        rt_v = TT.v
        p.vector.wait(TT, rt_v)
        sc0 = rsc[:, 0:8]
        p.vector.call("tensor_copy", sc0, ps_r[:, 0:8], inc=VE)
        if not l0mode:
            p.vector.call("tensor_add", sc0, sc0, rsc[:, 16:24], inc=VE)
        m1 = stats[:, t * 6 + 3:t * 6 + 4]
        m2c = stats[:, t * 6 + 4:t * 6 + 5]
        i1 = stats[:, t * 6 + 5:t * 6 + 6]
        tmp8 = wsel[:, t * 8:(t + 1) * 8]
        d_ap, e_ap, z_ap = rsc[:, 8:9], rsc[:, 9:10], rsc[:, 10:11]
        w1, w2, i2 = rsc[:, 11:12], rsc[:, 12:13], rsc[:, 13:14]
        negm1 = rsc[:, 14:15]
        p.vector.call("reduce_max", m1, sc0, axis=AXX, inc=VE)
        p.vector.wait(VE)
        if not l0mode:
            p.vector.call("tensor_scalar", negm1, m1, -1.0, None, ALU.mult,
                          inc=VE)
            p.scalar.wait(VE, VE.v)
            p.scalar.call("activation", tmp8, sc0, AF.Exp, bias=negm1,
                          accum_out=z_ap, inc=SC)
            p.vector.wait(SC, SC.v)
        p.vector.call("tensor_scalar", tmp8, sc0, m1, None, ALU.is_ge, inc=VE)
        p.vector.call("tensor_mul", tmp8, tmp8, iota_b[:, 0:8], inc=VE)
        p.vector.call("reduce_max", i1, tmp8, axis=AXX, inc=VE)
        p.vector.wait(VE)
        p.vector.call("tensor_scalar", tmp8, iota_b[:, 0:8], i1, None,
                      ALU.is_equal, inc=VE)
        p.vector.call("scalar_tensor_tensor", tmp8, tmp8, -1e9, sc0,
                      ALU.mult, ALU.add, inc=VE)
        p.vector.call("reduce_max", m2c, tmp8, axis=AXX, inc=VE)
        p.vector.wait(VE)
        p.vector.call("tensor_scalar", tmp8, tmp8, m2c, None, ALU.is_ge,
                      inc=VE)
        p.vector.call("tensor_mul", tmp8, tmp8, iota_b[:, 0:8], inc=VE)
        p.vector.call("reduce_max", i2, tmp8, axis=AXX, inc=VE)
        p.vector.wait(VE)
        if l0mode:
            p.vector.call("tensor_sub", d_ap, m2c, m1, inc=VE)
            p.scalar.wait(VE, VE.v)
            p.scalar.call("activation", e_ap, d_ap, AF.Exp, inc=SC)
            p.vector.wait(SC, SC.v)
            p.vector.call("tensor_scalar", z_ap, e_ap, 1.0, 1.0, ALU.mult,
                          ALU.add, inc=VE)
            p.vector.call("reciprocal", w1, z_ap, inc=VE)
            p.vector.wait(VE)
            p.vector.call("tensor_scalar", w2, w1, -1.0, 1.0, ALU.mult,
                          ALU.add, inc=VE)
        else:
            p.vector.call("tensor_sub", d_ap, m2c, m1, inc=VE)
            p.scalar.wait(VE, VE.v)
            p.scalar.call("activation", e_ap, d_ap, AF.Exp, inc=SC)
            p.vector.wait(SC, SC.v)
            p.vector.call("reciprocal", w1, z_ap, inc=VE)
            p.vector.wait(VE)
            p.vector.call("tensor_mul", w2, e_ap, w1, inc=VE)
        p.vector.call("tensor_scalar", tmp8, iota_b[:, 0:8], i1, None,
                      ALU.is_equal, inc=VE)
        p.vector.call("tensor_scalar", tmp8, tmp8, w1, None, ALU.mult, inc=VE)
        p.vector.call("tensor_scalar", rsc[:, 0:8], iota_b[:, 0:8], i2, None,
                      ALU.is_equal, inc=VE)
        p.vector.call("tensor_scalar", rsc[:, 0:8], rsc[:, 0:8], w2, None,
                      ALU.mult, inc=VE)
        p.vector.call("tensor_add", tmp8, tmp8, rsc[:, 0:8], inc=VE)
        tile_done[t] = VE.v

    experts = [(e, h) for e in range(8) for h in range(2)]
    if shared:
        experts += [("sh", 0), ("sh", 1)]
    for (e, hf) in experts:
        is_sh = (e == "sh")
        pe = (e * 2 + hf) if not is_sh else hf
        upd = HH if (is_sh or not gated) else 2 * HH
        if is_sh:
            up_v = stream_w(wsu_d, hf * D, 0, NCH, HH, Wsl, 2048, 0)
        else:
            up_v = stream_w(wup_d, pe * D, 0, NCH, upd, Wsl, 2048, 0)
        has_up_bias = is_sh or moe_bias
        if has_up_bias or ((is_sh or moe_bias) and hf == 0):
            p.sync.wait(TT, TT.v)
        if has_up_bias:
            bld = p.sync.dma(brow[:, UBO:UBO + HH],
                             bs1_d[hf:hf + 1, :] if is_sh else rb1_d[pe:pe + 1, :], LD)
        has_dn_bias = (is_sh or moe_bias) and hf == 0
        if has_dn_bias:
            bld = p.sync.dma(b2row[:, 0:D],
                             bs2_d if is_sh else rb2_d[e:e + 1, :], LD)
        p.tensor.wait(CB, up_v)
        if has_up_bias or has_dn_bias:
            p.tensor.wait(LD, LD.v)
        p.tensor.wait(VE, VE.v)
        if has_up_bias or has_dn_bias:
            mm_specs = []
            if has_up_bias:
                mm_specs += [(ps_g[0], brow[:, UBO:UBO + 512], ub_tile, 0),
                             (ps_g[1], brow[:, UBO + 512:UBO + 1024],
                              ub_tile, 512)]
            if has_dn_bias:
                mm_specs += [(ps_g[2], b2row[:, 0:512], db_tile, 0),
                             (ps_g[3], b2row[:, 512:1024], db_tile, 512)]
            for i, (bank, rowap, _, _) in enumerate(mm_specs):
                p.tensor.call("matmul", bank[:], onesf[:, 0:128], rowap,
                              start=True, stop=True,
                              inc=(TT if i == len(mm_specs) - 1 else None))
            p.scalar.wait(TT)
            for i, (bank, _, dstt, off) in enumerate(mm_specs):
                p.scalar.call("copy", dstt[:, off:off + 512], bank[:],
                              inc=(SC if i == len(mm_specs) - 1 else None))
            p.tensor.wait(SC, SC.v)
        hst = [hstage, gstage]
        g_sub = gated and not is_sh
        evac_sc = {}
        evac_ve = {}
        tr_cp = {}
        h_rdy = {}

        def emit_tr(tt):
            sem, val = h_rdy[tt]
            p.tensor.wait(sem, val)
            if tt >= 1:
                p.tensor.wait(SC, tr_cp[tt - 1])
            for c in range(NCH):
                p.tensor.call("transpose", ps_tr[:, c * 128:(c + 1) * 128],
                              hst[tt % 2][:, c * 128:(c + 1) * 128],
                              sidb[0:128, 0:128],
                              inc=(TT if c == NCH - 1 else None))
            p.scalar.wait(TT)
            p.scalar.call("copy", hT[:, tt * HH:(tt + 1) * HH], ps_tr[:],
                          inc=SC)
            tr_cp[tt] = SC.v

        for t in range(NT):
            if g_sub:
                for s in range(2):
                    k = t * 2 + s
                    if k >= 2:
                        p.tensor.wait(SC, evac_sc[k - 2])
                        p.tensor.wait(VE, evac_ve[k - 2])
                    for c in range(NCH):
                        p.tensor.call("matmul", ps_g[s][:],
                                      xnT[:, t * D + c * 128:t * D + (c + 1) * 128],
                                      Wsl[:, c * 2048 + s * 512:c * 2048 + (s + 1) * 512],
                                      start=(c == 0), stop=(c == NCH - 1))
                    for c in range(NCH):
                        p.tensor.call("matmul", ps_g[2 + s][:],
                                      xnT[:, t * D + c * 128:t * D + (c + 1) * 128],
                                      Wsl[:, c * 2048 + (2 + s) * 512:
                                          c * 2048 + (3 + s) * 512],
                                      start=(c == 0), stop=(c == NCH - 1),
                                      inc=(TT if c == NCH - 1 else None))
                    p.scalar.wait(TT)
                    gh = gsc[:, (k % 2) * 512:(k % 2 + 1) * 512]
                    p.scalar.call("activation", gh, ps_g[2 + s][:], AF.Silu,
                                  inc=SC)
                    evac_sc[k] = SC.v
                    p.vector.wait(SC, evac_sc[k])
                    p.vector.call("tensor_mul",
                                  hst[t % 2][:, s * 512:(s + 1) * 512],
                                  ps_g[s][:], gh, inc=VE)
                    evac_ve[k] = VE.v
                h_rdy[t] = (VE, evac_ve[t * 2 + 1])
            else:
                pg = (ps_g[0], ps_g[1]) if t % 2 == 0 else (ps_g[2], ps_g[3])
                if t >= 2:
                    p.tensor.wait(SC, evac_sc[t - 2])
                for s in range(2):
                    for c in range(NCH):
                        p.tensor.call("matmul", pg[s][:],
                                      xnT[:, t * D + c * 128:t * D + (c + 1) * 128],
                                      Wsl[:, c * 2048 + s * 512:c * 2048 + (s + 1) * 512],
                                      start=(c == 0),
                                      stop=(c == NCH - 1),
                                      inc=(TT if c == NCH - 1 else None))
                if has_up_bias:
                    p.vector.wait(TT, TT.v)
                    p.vector.call("tensor_add", pg[0][:], pg[0][:],
                                  ub_tile[:, 0:512])
                    p.vector.call("tensor_add", pg[1][:], pg[1][:],
                                  ub_tile[:, 512:1024], inc=VE)
                    p.scalar.wait(VE, VE.v)
                else:
                    p.scalar.wait(TT)
                p.scalar.call("activation", hst[t % 2][:, 0:512], pg[0][:],
                              AF.Gelu)
                p.scalar.call("activation", hst[t % 2][:, 512:1024], pg[1][:],
                              AF.Gelu, inc=SC)
                evac_sc[t] = SC.v
                h_rdy[t] = (SC, evac_sc[t])
            if t >= 1:
                emit_tr(t - 1)
        emit_tr(NT - 1)
        hT_done = SC.v
        if is_sh:
            dn_v = stream_w(wsd_d, hf * HH, 0, NCH, D, Wsl, 2048, 0)
        else:
            dn_v = stream_w(wdn_d, pe * HH, 0, NCH, D, Wsl, 2048, 0)
        p.tensor.wait(CB, dn_v)
        p.tensor.wait(SC, hT_done)
        p.tensor.wait(VE, VE.v)
        oa_ve = {}
        for t in range(NT):
            pg = (ps_g[0], ps_g[1]) if t % 2 == 0 else (ps_g[2], ps_g[3])
            if t >= 2:
                p.tensor.wait(VE, oa_ve[t - 2])
            for s in range(2):
                for c in range(NCH):
                    p.tensor.call("matmul", pg[s][:],
                                  hT[:, t * HH + c * 128:t * HH + (c + 1) * 128],
                                  Wsl[:, c * 2048 + s * 512:c * 2048 + (s + 1) * 512],
                                  start=(c == 0),
                                  stop=(c == NCH - 1),
                                  inc=(TT if c == NCH - 1 else None))
            dnmm_v = TT.v
            p.vector.wait(TT, dnmm_v)
            if has_dn_bias:
                p.vector.call("tensor_add", pg[0][:], pg[0][:],
                              db_tile[:, 0:512])
                p.vector.call("tensor_add", pg[1][:], pg[1][:],
                              db_tile[:, 512:1024], inc=VE)
            if is_sh:
                p.vector.call("tensor_add", oa[0][:, t * 512:(t + 1) * 512],
                              oa[0][:, t * 512:(t + 1) * 512], pg[0][:])
                p.vector.call("tensor_add", oa[1][:, t * 512:(t + 1) * 512],
                              oa[1][:, t * 512:(t + 1) * 512], pg[1][:],
                              inc=VE)
            else:
                we = rsc[:, 15:16] if t % 2 == 0 else rsc[:, 16:17]
                p.vector.call("tensor_scalar", we, wsel[:, t * 8 + e:t * 8 + e + 1],
                              1.0, None, ALU.mult, inc=VE)
                p.vector.wait(VE)
                p.vector.call("scalar_tensor_tensor",
                              oa[0][:, t * 512:(t + 1) * 512],
                              pg[0][:], we, oa[0][:, t * 512:(t + 1) * 512],
                              ALU.mult, ALU.add)
                p.vector.call("scalar_tensor_tensor",
                              oa[1][:, t * 512:(t + 1) * 512],
                              pg[1][:], we, oa[1][:, t * 512:(t + 1) * 512],
                              ALU.mult, ALU.add, inc=VE)
            oa_ve[t] = VE.v

    for t in range(NT):
        p.sync.wait(VE, VE.v)
        p.sync.wait(STR, STR.v)
        xld = p.sync.dma(xstage[:, 0:D], ybscr[t * 128:(t + 1) * 128, :], LD)
        p.vector.wait(LD, xld)
        p.vector.call("tensor_add", ystage[:, 0:512], xstage[:, 0:512],
                      oa[0][:, t * 512:(t + 1) * 512])
        p.vector.call("tensor_add", ystage[:, 512:1024], xstage[:, 512:1024],
                      oa[1][:, t * 512:(t + 1) * 512], inc=VE)
        p.sync.wait(VE, VE.v)
        if io:
            io["ywrite"](t, ystage, STR)
        else:
            p.sync.dma(y_out[t * 128:(t + 1) * 128, :], ystage[:], STR)
    p.sync.wait(STR, STR.v)
    return p



def build_scan(cfg, p=None, io=None):
    standalone = p is None
    if standalone:
        p = Prog(cfg["name"])
    nc = p.nc
    SCD, STD = 64, 16
    EPSR = 1.1920929e-07

    x_d = io["xb"] if io else p.din("xb", [T, D], F32)
    pA_d = io["pA"] if io else p.din("pA", [T, D], F32)
    pB_d = io["pB"] if io else p.din("pB", [T, D], F32)
    st_d = None if io else p.din("st_in", [128, NT_FULL * 6], F32)
    wzg_d = p.din("wzg", [D, 128], F32)
    wdt_d = p.din("wdt", [SCD, 96], F32)
    dtb_d = p.din("dtb", [1, 96], F32)
    alog_d = p.din("alog", [128, 8], F32)
    ec_d = p.din("ec", [SCD, 8 * 128], F32)
    est_d = p.din("est", [STD, 128], F32)
    g_d = p.din("gmat", [128, 8 * SCD], F32)
    wout_d = p.din("wout", [SCD, D], F32)
    gvec_d = p.din("gvec", [128, 8], F32)
    identf = p.din("identf", [128, 128], F32)
    if standalone:
        y_out = nc.dram_tensor("y", [T, D], F32, kind="ExternalOutput").ap()
    x1scr = p.dint("x1scr", [T, D], F32)

    rmsc = p.sbuf("rmsc", [128, NT_FULL * 2], F32)
    stin = (io.get("stats_buf") if io else None) or \
        p.sbuf("stin", [128, NT_FULL * 6], F32)
    xstage = p.sbuf("xstage", [128, 2 * D], F32)
    pstage = p.sbuf("pstage", [128, 2 * D], F32)
    x1b = p.sbuf("x1b", [128, D], F32)
    xnb = p.sbuf("xnb", [128, D], BF16)
    xnT = p.sbuf("xnT", [128, D], BF16)
    wzg = p.sbuf("wzg_s", [128, NCH * 128], BF16)
    wdt = p.sbuf("wdt_s", [SCD, 96], BF16)
    zF = p.sbuf("zF", [SCD, T], BF16)
    gF = p.sbuf("gF", [SCD, T], BF16)
    ztm = p.sbuf("ztm", [128, SCD], F32)
    dtm = p.sbuf("dtm", [128, 2 * SCD], F32)
    bici = p.sbuf("bici", [128, 96], F32)
    dtF = p.sbuf("dtF", [SCD, T], F32)
    dtzF = p.sbuf("dtzF", [SCD, T], F32)
    BiF = p.sbuf("BiF", [STD, T], F32)
    CiF = p.sbuf("CiF", [STD, T], F32)
    ec = p.sbuf("ec_s", [SCD, 8 * 128], F32)
    est = p.sbuf("est_s", [STD, 128], F32)
    gmat = p.sbuf("gmat_s", [128, 8 * SCD], F32)
    woutb = p.sbuf("woutb", [SCD, D], BF16)
    acol = p.sbuf("acol_s", [128, 8], F32)
    onec = p.sbuf("onec", [128, 1], F32)
    onesr = p.sbuf("onesr", [1, 128], F32)
    a_sb = p.sbuf("a_sb", [128, 512], F32)
    u0_sb = p.sbuf("u0_sb", [128, 512], F32)
    u_sb = p.sbuf("u_sb", [128, 512], F32)
    states = p.sbuf("states", [128, 512], F32)
    prod = p.sbuf("prod", [128, 512], F32)
    carry = p.sbuf("carry", [128, 8], F32)
    ygb = p.sbuf("ygb", [SCD, 512], BF16)
    ystage = p.sbuf("ystage", [128, D], F32)
    x1r = p.sbuf("x1r", [128, D], F32)
    sid = p.sbuf("sid", [128, 128], F32)
    sidb = p.sbuf("sidb", [128, 128], BF16)
    sp1 = p.sbuf("sp1", [128, SCD], F32)
    sp2 = p.sbuf("sp2", [128, SCD], F32)
    Wstage = p.sbuf("Wstage", [128, 512], F32)
    dtbrow = p.sbuf("dtbrow", [1, 96], F32)
    gcol = p.sbuf("gcol_s", [128, 8], F32)

    ps_a = p.psum("ps_a", [128, 512], F32)
    ps_b = p.psum("ps_b", [128, 512], F32)
    ps_c = p.psum("ps_c", [128, 512], F32)
    ps_y = p.psum("ps_y", [128, 512], F32)
    ps_p = [p.psum(f"ps_p{i}", [128, 512], F32) for i in range(2)]
    ps_tr = p.psum("ps_tr", [128, 1024], BF16)
    ps_trf = p.psum("ps_trf", [128, 512], F32)

    LD = p.sem("LD")
    WL = p.sem("WL")
    WLb = p.sem("WLb")
    SC = p.sem("SC")
    VE = p.sem("VE")
    TT = p.sem("TT")
    STR = p.sem("STR")
    GP = p.sem("GP")
    X1 = p.sem("X1")

    p.sync.dma(sid[:], identf, LD)
    if not io:
        p.sync.dma(stin[:], st_d, LD)
    p.sync.dma(ec[:], ec_d, LD)
    p.sync.dma(est[:], est_d, LD)
    p.sync.dma(gmat[:], g_d, LD)
    p.sync.dma(acol[:], alog_d, LD)
    p.sync.dma(dtbrow[:, :], dtb_d, LD)
    p.sync.dma(gcol[:], gvec_d, LD)
    p.sync.dma(Wstage[0:SCD, 0:96], wdt_d, LD)
    ld_consts = LD.v
    p.vector.wait(LD, ld_consts)
    p.vector.call("tensor_copy", sidb[:], sid[:], inc=VE)
    p.vector.call("tensor_copy", wdt[:], Wstage[0:SCD, 0:96], inc=VE)
    CB = p.sem("CB")
    p.gpsimd.call("memset", onec[:], 1.0)
    p.gpsimd.call("memset", onesr[:], 1.0, inc=GP)
    p.gpsimd.dma(woutb[:], wout_d, CB)
    gp_v = GP.v
    cb_v = CB.v
    p.scalar.wait(LD, ld_consts)
    p.scalar.call("activation", acol[:], acol[:], AF.Exp, inc=SC)
    p.vector.wait(SC, SC.v)
    p.vector.call("tensor_scalar", acol[:], acol[:], -1.0, None, ALU.mult,
                  inc=VE)
    acol_v = VE.v
    for c in range(NCH):
        p.sync.wait(VE, VE.v)
        p.sync.dma(Wstage[:, 0:128], wzg_d[c * 128:(c + 1) * 128, :], WL)
        p.vector.wait(WL)
        p.vector.call("tensor_scalar", wzg[:, c * 128:(c + 1) * 128],
                      Wstage[:, 0:128], gcol[:, c:c + 1], None, ALU.mult,
                      inc=VE)
    wzg_v = VE.v

    tile_done = [0] * NT_FULL
    for t in range(cfg.get("maxt", NT_FULL)):
        if t >= 1:
            p.sync.wait(VE, tile_done[t - 1])
        p.sync.dma(xstage[:, (t % 2) * D:(t % 2 + 1) * D],
                   x_d[t * 128:(t + 1) * 128, :], LD)
        p.sync.dma(pstage[:, 0:D], pA_d[t * 128:(t + 1) * 128, :], LD)
        xld = p.sync.dma(pstage[:, D:2 * D], pB_d[t * 128:(t + 1) * 128, :], LD)
        xt = xstage[:, (t % 2) * D:(t % 2 + 1) * D]
        st6 = stin[:, t * 6:(t + 1) * 6]
        p.vector.wait(LD, xld)
        p.vector.wait(X1, X1.v)
        p.vector.call("tensor_scalar", x1b[:], xt, st6[:, 0:1], st6[:, 2:3],
                      ALU.subtract, ALU.mult, inc=VE)
        p.vector.call("tensor_add", x1b[:], x1b[:], pstage[:, 0:D], inc=VE)
        p.vector.call("tensor_add", x1b[:], x1b[:], pstage[:, D:2 * D], inc=VE)
        x1_v = VE.v
        p.sync.wait(VE, x1_v)
        p.sync.dma(x1scr[t * 128:(t + 1) * 128, :], x1b[:], X1)
        if cfg.get("steps", 99) < 1:
            tile_done[t] = VE.v
            continue
        p.scalar.wait(VE, x1_v)
        p.scalar.wait(TT, TT.v)
        p.scalar.call("activation", xnb[:], x1b[:], AF.Square,
                      accum_out=rmsc[:, 2 * t:2 * t + 1], inc=SC)
        p.vector.wait(SC, SC.v)
        p.vector.call("tensor_scalar", rmsc[:, 2 * t:2 * t + 1],
                      rmsc[:, 2 * t:2 * t + 1], 1.0 / D, EPSR,
                      ALU.mult, ALU.add, inc=VE)
        p.scalar.wait(VE, VE.v)
        p.scalar.call("activation", rmsc[:, 2 * t + 1:2 * t + 2],
                      rmsc[:, 2 * t:2 * t + 1], AF.Ln)
        p.scalar.call("activation", rmsc[:, 2 * t:2 * t + 1],
                      rmsc[:, 2 * t + 1:2 * t + 2], AF.Exp, scale=-0.5, inc=SC)
        p.vector.wait(SC, SC.v)
        p.vector.call("tensor_scalar", xnb[:], x1b[:],
                      rmsc[:, 2 * t:2 * t + 1], None, ALU.mult, inc=VE)
        xn_v = VE.v
        if cfg.get("steps", 99) < 2:
            tile_done[t] = VE.v
            continue
        p.tensor.wait(VE, xn_v)
        p.tensor.wait(SC, SC.v)
        for c in range(NCH):
            p.tensor.call("transpose", ps_tr[:, c * 128:(c + 1) * 128],
                          xnb[:, c * 128:(c + 1) * 128], sidb[0:128, 0:128],
                          inc=(TT if c == NCH - 1 else None))
        p.scalar.wait(TT)
        p.scalar.call("copy", xnT[:], ps_tr[:], inc=SC)
        ev_v = SC.v
        if cfg.get("steps", 99) < 3:
            tile_done[t] = VE.v
            continue
        p.tensor.wait(SC, ev_v)
        p.tensor.wait(VE, wzg_v)
        for c in range(NCH):
            p.tensor.call("matmul", ps_a[:, 0:128],
                          xnT[:, c * 128:(c + 1) * 128],
                          wzg[:, c * 128:(c + 1) * 128],
                          start=(c == 0), stop=(c == NCH - 1),
                          inc=(TT if c == NCH - 1 else None))
        zg_v = TT.v
        if cfg.get("steps", 99) < 4:
            tile_done[t] = VE.v
            continue
        p.vector.wait(TT, zg_v)
        p.vector.call("tensor_copy", ztm[:], ps_a[:, 0:SCD], inc=VE)
        p.vector.call("tensor_copy", xnb[:, SCD:2 * SCD], ztm[:], inc=VE)
        p.scalar.wait(TT, zg_v)
        p.scalar.call("activation", xnb[:, 0:SCD], ps_a[:, SCD:128], AF.Silu,
                      inc=SC)
        p.tensor.wait(VE, VE.v)
        p.tensor.wait(SC, SC.v)
        p.tensor.call("transpose", ps_tr[:, 0:128][0:SCD, 0:128],
                      xnb[:, SCD:2 * SCD], sidb[0:128, 0:128])
        p.tensor.call("transpose", ps_tr[:, 128:256][0:SCD, 0:128],
                      xnb[:, 0:SCD], sidb[0:128, 0:128], inc=TT)
        p.scalar.wait(TT)
        p.scalar.call("copy", zF[:, t * 128:(t + 1) * 128],
                      ps_tr[:, 0:128][0:SCD, :])
        p.scalar.call("copy", gF[:, t * 128:(t + 1) * 128],
                      ps_tr[:, 128:256][0:SCD, :], inc=SC)
        zf_v = SC.v
        if cfg.get("steps", 99) < 5:
            tile_done[t] = VE.v
            continue
        ss = cfg.get("substep", 99)
        p.tensor.wait(SC, zf_v)
        p.tensor.wait(LD, ld_consts)
        p.tensor.wait(GP, gp_v)
        p.tensor.wait(CB, cb_v)
        p.tensor.wait(VE, VE.v)
        if ss >= 1:
            p.tensor.call("matmul", ps_b[:, 0:96], onesr[:, 0:128], dtbrow[:, :],
                          start=True, stop=False)
            p.tensor.call("matmul", ps_b[:, 0:96],
                          zF[:, t * 128:(t + 1) * 128], wdt[:],
                          start=False, stop=True, inc=TT)
        dt_v = TT.v
        if ss >= 2:
            p.vector.wait(TT, dt_v)
            p.vector.call("tensor_scalar", dtm[:, 0:SCD], ps_b[:, 0:SCD], 0.0,
                          None, ALU.max, inc=VE)
            p.vector.call("tensor_copy", bici[:], ps_b[:, 0:96], inc=VE)
        if ss >= 3:
            p.scalar.wait(VE, VE.v)
            p.scalar.call("activation", sp1[:], bici[:, 0:SCD], AF.Abs)
            p.scalar.call("activation", sp1[:], sp1[:], AF.Exp, scale=-1.0)
            p.scalar.call("activation", sp2[:], sp1[:], AF.Ln,
                          bias=onec[:], inc=SC)
        if ss >= 4:
            p.vector.wait(SC, SC.v)
            p.vector.call("tensor_add", dtm[:, 0:SCD], dtm[:, 0:SCD], sp2[:],
                          inc=VE)
            p.vector.call("tensor_mul", dtm[:, SCD:2 * SCD], dtm[:, 0:SCD],
                          ztm[:], inc=VE)
        dtz_v = VE.v
        if cfg.get("steps", 99) < 6:
            tile_done[t] = VE.v
            continue
        p.tensor.wait(VE, dtz_v)
        p.tensor.wait(SC, SC.v)
        p.tensor.call("transpose", ps_trf[:, 0:128][0:SCD, :],
                      dtm[:, 0:SCD], sid[0:128, 0:128])
        p.tensor.call("transpose", ps_trf[:, 128:256][0:SCD, :],
                      dtm[:, SCD:2 * SCD], sid[0:128, 0:128])
        p.tensor.call("transpose", ps_trf[:, 256:384][0:STD, :],
                      bici[:, 64:64 + STD], sid[0:128, 0:128])
        p.tensor.call("transpose", ps_trf[:, 384:512][0:STD, :],
                      bici[:, 64 + STD:96], sid[0:128, 0:128], inc=TT)
        p.scalar.wait(TT)
        p.scalar.call("copy", dtF[:, t * 128:(t + 1) * 128],
                      ps_trf[:, 0:128][0:SCD, :])
        p.scalar.call("copy", dtzF[:, t * 128:(t + 1) * 128],
                      ps_trf[:, 128:256][0:SCD, :])
        p.scalar.call("copy", BiF[:, t * 128:(t + 1) * 128],
                      ps_trf[:, 256:384][0:STD, :])
        p.scalar.call("copy", CiF[:, t * 128:(t + 1) * 128],
                      ps_trf[:, 384:512][0:STD, :], inc=SC)
        tile_done[t] = VE.v

    phaseA = (SC.v, VE.v, TT.v)
    if cfg.get("nophaseb"):
        p.sync.wait(STR, STR.v)
        p.sync.wait(X1, X1.v)
        return p
    p.tensor.wait(SC, SC.v)
    p.tensor.wait(VE, VE.v)
    p.tensor.wait(LD, ld_consts)
    for ch in range(8):
        cs = slice(ch * 512, (ch + 1) * 512)
        for j in range(8):
            p.tensor.wait(VE, VE.v)
            p.tensor.call("matmul", ps_a[:], ec[:, j * 128:(j + 1) * 128],
                          dtF[:, cs], start=True, stop=True)
            p.tensor.call("matmul", ps_c[:], ec[:, j * 128:(j + 1) * 128],
                          dtzF[:, cs], start=True, stop=True)
            p.tensor.call("matmul", ps_b[:], est[:], BiF[:, cs],
                          start=True, stop=True, inc=TT)
            rep_v = TT.v
            p.vector.wait(TT, rep_v)
            p.vector.call("tensor_scalar", a_sb[:], ps_a[:],
                          acol[:, j:j + 1], 1.0, ALU.mult, ALU.add, inc=VE)
            p.vector.call("tensor_copy", u0_sb[:], ps_c[:], inc=VE)
            p.vector.call("tensor_mul", u_sb[:], u0_sb[:], ps_b[:], inc=VE)
            if cfg.get("noscan"):
                p.vector.call("tensor_copy", states[:], u_sb[:], inc=VE)
            elif ch == 0:
                p.vector.call("tensor_tensor_scan", states[:], a_sb[:],
                              u_sb[:], 0.0, ALU.mult, ALU.add, inc=VE)
            else:
                p.vector.call("tensor_tensor_scan", states[:], a_sb[:],
                              u_sb[:], carry[:, j:j + 1], ALU.mult, ALU.add,
                              inc=VE)
            p.vector.call("tensor_copy", carry[:, j:j + 1],
                          states[:, 511:512], inc=VE)
            p.tensor.wait(VE, VE.v)
            p.tensor.call("matmul", ps_b[:], est[:], CiF[:, cs],
                          start=True, stop=True, inc=TT)
            p.vector.wait(TT, TT.v)
            p.vector.call("tensor_mul", prod[:], states[:], ps_b[:], inc=VE)
            prod_v = VE.v
            p.tensor.wait(VE, prod_v)
            p.tensor.call("matmul", ps_y[0:SCD, :],
                          gmat[:, j * SCD:(j + 1) * SCD], prod[:],
                          start=(j == 0), stop=(j == 7),
                          inc=(TT if j == 7 else None))
        y_v = TT.v
        p.vector.wait(TT, y_v)
        p.vector.call("tensor_mul", ygb[:], ps_y[0:SCD, :], gF[0:SCD, cs],
                      inc=VE)
        yg_v = VE.v
        for i in range(4):
            gt = ch * 4 + i
            p.tensor.wait(VE, yg_v if i == 0 else VE.v)
            for s in range(2):
                p.tensor.call("matmul", ps_p[s][:],
                              ygb[:, i * 128:(i + 1) * 128],
                              woutb[:, s * 512:(s + 1) * 512],
                              start=True, stop=True,
                              inc=(TT if s == 1 else None))
            pj_v = TT.v
            p.sync.wait(VE, VE.v)
            p.sync.wait(STR, STR.v)
            p.sync.dma(xstage[:, 0:D], x_d[gt * 128:(gt + 1) * 128, :], LD)
            p.sync.wait(X1, X1.v)
            xld = p.sync.dma(x1r[:], x1scr[gt * 128:(gt + 1) * 128, :], LD)
            p.vector.wait(TT, pj_v)
            p.vector.wait(LD, xld)
            p.vector.call("tensor_add", ystage[:], xstage[:, 0:D], x1r[:],
                          inc=VE)
            p.vector.call("tensor_add", ystage[:, 0:512], ystage[:, 0:512],
                          ps_p[0][:])
            p.vector.call("tensor_add", ystage[:, 512:1024],
                          ystage[:, 512:1024], ps_p[1][:], inc=VE)
            fy_v = VE.v
            p.sync.wait(VE, fy_v)
            if io:
                io["ywrite"](gt, ystage, STR)
            else:
                p.sync.dma(y_out[gt * 128:(gt + 1) * 128, :], ystage[:], STR)
    p.sync.wait(STR, STR.v)
    p.sync.wait(X1, X1.v)
    return p



_PROG_CACHE = {}

CFG_L0A = dict(name="l0a", H=16, dh=64, causal=False, eps2=EPS_LN,
               gamma_fold=True, beta_qkv=True)
CFG_L0B = dict(name="l0b", H=16, dh=64, causal=True, rope=True, eps2=None,
               bias_rows=True, w_f16=True)
CFG_L1A = dict(name="l1a", H=16, dh=64, causal=True, eps2=1.1920929e-07,
               gamma_fold=True)
CFG_L1B = dict(name="l1b", H=8, dh=128, causal=True, eps2=None)
CFG_FFN0A = dict(name="ffn0a", router="topk_softmax", rms_eps=1e-6, nfold=True,
                 skip_ln=True, combine="plain", bcmb=True, gated=True)
CFG_FFN0B = dict(name="ffn0b", router="topk_softmax", rms_eps=1e-6, nfold=True,
                 skip_ln=True, combine="fp16", bcmb=True, gated=True)
CFG_FFN1A = dict(name="ffn1a", router="softmax_topk", rms_eps=None, nfold=False,
                 skip_ln=False, combine=None, shared=True, moe_bias=True,
                 gated=False)
CFG_FFN1B = dict(name="ffn1b", router="softmax_topk", rms_eps=None, nfold=False,
                 skip_ln=False, combine="plain", shared=True, moe_bias=True,
                 gated=False)


def _prog(key, builder, cfg):
    if key not in _PROG_CACHE:
        _PROG_CACHE[key] = builder(cfg).emit_and_finish()
    return _PROG_CACHE[key]


def _colvec(v):
    return np.ascontiguousarray(np.asarray(v, np.float32).reshape(8, 128).T)


def _identf():
    return np.eye(128, dtype=np.float32)


def _cmask():
    k = np.arange(128)
    return np.ascontiguousarray(
        np.where(k[:, None] > k[None, :], np.float32(NEG), np.float32(0.0)))


def _iota8():
    return np.arange(8, dtype=np.float32)[None]


def _rope_tables():
    dh = 64
    pos = np.arange(T, dtype=np.float32)
    inv = np.exp(np.arange(0, dh, 2, dtype=np.float32) * (-np.log(10000.0) / dh))
    ang = pos[:, None] * inv[None, :]
    sin = np.sin(ang).astype(np.float16).astype(np.float32)
    cos = np.cos(ang).astype(np.float16).astype(np.float32)
    return (np.ascontiguousarray(np.tile(cos, (1, 8))),
            np.ascontiguousarray(np.tile(sin, (1, 8))))


def _scan_consts():
    ec = np.zeros((64, 8 * 128), np.float32)
    for j in range(8):
        for pp in range(128):
            ec[8 * j + pp // 16, j * 128 + pp] = 1.0
    est = np.zeros((16, 128), np.float32)
    for pp in range(128):
        est[pp % 16, pp] = 1.0
    g = np.zeros((128, 8 * 64), np.float32)
    for j in range(8):
        for pp in range(128):
            g[pp, j * 64 + 8 * j + pp // 16] = 1.0
    return ec, est, g


def _run(nc, in_maps):
    r = run_bass_kernel_spmd(nc, in_maps, core_ids=list(range(8)))
    return r.results


def _attn_launch(state, cfg, wmaps):
    nc = _prog(cfg["name"], build_attention, cfg)
    in_maps = []
    for c in range(8):
        m = dict(wmaps[c % 2])
        m["xb"] = np.ascontiguousarray(state[c // 2])
        in_maps.append(m)
    res = _run(nc, in_maps)
    return ([res[c]["y"] for c in range(8)],
            [res[c]["st_out"] for c in range(8)])


def _ffn_launch(state, cfg, wshared, partials=None, key=None):
    nc = _prog(key or cfg["name"], build_ffn, cfg)
    in_maps = []
    for c in range(8):
        b = c // 2
        half = c % 2
        rows = slice(half * 2048, (half + 1) * 2048)
        m = dict(wshared)
        m["xo"] = np.ascontiguousarray(state[b][rows])
        if partials is not None:
            m["pA"] = np.ascontiguousarray(partials[2 * b][rows])
            m["pB"] = np.ascontiguousarray(partials[2 * b + 1][rows])
        in_maps.append(m)
    res = _run(nc, in_maps)
    new_state = state.copy()
    for c in range(8):
        b, half = c // 2, c % 2
        new_state[b][half * 2048:(half + 1) * 2048] = res[c]["y"]
    return new_state


def kernel_unfused(**inputs):
    g = lambda k: np.asarray(inputs[k])
    x = g("L0_attn_a_x").astype(np.float32)
    state = np.ascontiguousarray(x.reshape(4, T, D))
    identf = _identf()
    cmask = _cmask()
    iota = _iota8()
    rcos, rsin = _rope_tables()

    Wqkv = g("L0_attn_a_W_qkv")
    bqkv = g("L0_attn_a_b_qkv")
    wmaps = []
    for pi in range(2):
        sl = slice(pi * 512, (pi + 1) * 512)
        wmaps.append(dict(
            wq=np.ascontiguousarray(Wqkv[:, 0:1024][:, sl]),
            wkv=np.ascontiguousarray(np.hstack([Wqkv[:, 1024:2048][:, sl],
                                                Wqkv[:, 2048:3072][:, sl]])),
            wp=np.ascontiguousarray(g("L0_attn_a_W_proj")[sl, :]),
            bq=np.ascontiguousarray(bqkv[0:1024][sl])[None],
            bkv=np.ascontiguousarray(np.hstack([bqkv[1024:2048][sl],
                                                bqkv[2048:3072][sl]]))[None],
            gvec=_colvec(g("L0_attn_a_ln_gamma")),
            bvec=_colvec(g("L0_attn_a_ln_beta")),
            identf=identf))
    parts, _ = _attn_launch(state, CFG_L0A, wmaps)

    wupT = g("L0_ffn_w_up").transpose(0, 2, 1)
    wdnT = g("L0_ffn_w_down").transpose(0, 2, 1)
    wu = np.ascontiguousarray(np.concatenate(
        [np.concatenate([wupT[e][:, h * 1024:(h + 1) * 1024],
                         wupT[e][:, 2048 + h * 1024:2048 + (h + 1) * 1024]],
                        axis=1)
         for e in range(8) for h in range(2)], axis=0))
    wd = np.ascontiguousarray(np.concatenate(
        [wdnT[e][h * 1024:(h + 1) * 1024, :]
         for e in range(8) for h in range(2)], axis=0))
    ffn0_w = dict(wrt=np.ascontiguousarray(g("L0_ffn_w_router").T),
                  wup=wu, wdn=wd,
                  gvec=_colvec(g("L0_ffn_norm_scale")),
                  identf=identf, iota=iota)
    w0a = dict(ffn0_w)
    w0a["bcmb"] = np.ascontiguousarray(g("L0_attn_a_b_proj"))[None]
    state = _ffn_launch(state, CFG_FFN0A, w0a, parts)

    wmaps = []
    for pi in range(2):
        sl = slice(pi * 512, (pi + 1) * 512)
        wmaps.append(dict(
            wq=np.ascontiguousarray(g("L0_attn_b_Wq")[:, sl]),
            wkv=np.ascontiguousarray(np.hstack([g("L0_attn_b_Wk")[:, sl],
                                                g("L0_attn_b_Wv")[:, sl]])),
            wp=np.ascontiguousarray(g("L0_attn_b_Wo")[sl, :]),
            bq=np.ascontiguousarray(
                g("L0_attn_b_bq").astype(np.float32)[sl])[None],
            bkv=np.ascontiguousarray(np.hstack(
                [g("L0_attn_b_bk").astype(np.float32)[sl],
                 g("L0_attn_b_bv").astype(np.float32)[sl]]))[None],
            rcos=rcos, rsin=rsin, cmask=cmask, identf=identf))
    parts, _ = _attn_launch(state, CFG_L0B, wmaps)

    w0b = dict(ffn0_w)
    w0b["bcmb"] = np.ascontiguousarray(
        g("L0_attn_b_bo").astype(np.float32))[None]
    state = _ffn_launch(state, CFG_FFN0B, w0b, parts)

    qkvT = np.ascontiguousarray(g("L1_attn_a_qkv_w").T)
    wmaps = []
    for pi in range(2):
        sl = slice(pi * 512, (pi + 1) * 512)
        wmaps.append(dict(
            wq=np.ascontiguousarray(qkvT[:, 0:1024][:, sl]),
            wkv=np.ascontiguousarray(np.hstack([qkvT[:, 1024:2048][:, sl],
                                                qkvT[:, 2048:3072][:, sl]])),
            wp=np.ascontiguousarray(g("L1_attn_a_o_w").T[sl, :]),
            gvec=_colvec(g("L1_attn_a_norm1_w")),
            cmask=cmask, identf=identf))
    parts, stats = _attn_launch(state, CFG_L1A, wmaps)

    ec, est, gmat = _scan_consts()
    nc = _prog("scan", build_scan, dict(name="scan"))
    in_maps = []
    for c in range(8):
        b = c // 2
        m = dict(
            xb=np.ascontiguousarray(state[b]),
            pA=np.ascontiguousarray(parts[2 * b]),
            pB=np.ascontiguousarray(parts[2 * b + 1]),
            st_in=stats[2 * b],
            wzg=np.ascontiguousarray(np.hstack([g("L1_attn_a_in_w").T,
                                                g("L1_attn_a_gate_w").T])),
            wdt=np.ascontiguousarray(np.hstack([g("L1_attn_a_dt_w").T,
                                                g("L1_attn_a_Bp_w").T,
                                                g("L1_attn_a_Cp_w").T])),
            dtb=np.ascontiguousarray(np.concatenate(
                [g("L1_attn_a_dt_b"), np.zeros(32, np.float32)]))[None],
            alog=np.ascontiguousarray(
                g("L1_attn_a_A_log").reshape(-1).reshape(8, 128).T),
            ec=ec, est=est, gmat=gmat,
            wout=np.ascontiguousarray(g("L1_attn_a_out_w").T),
            gvec=_colvec(g("L1_attn_a_norm2_w")),
            identf=identf)
        in_maps.append(m)
    res = _run(nc, in_maps)
    for b in range(4):
        state[b] = res[2 * b]["y"]

    rw1 = g("L1_ffn_rw1")
    rw2 = g("L1_ffn_rw2")
    wu = np.ascontiguousarray(np.concatenate(
        [rw1[e][:, h * 1024:(h + 1) * 1024]
         for e in range(8) for h in range(2)], axis=0))
    wd = np.ascontiguousarray(np.concatenate(
        [rw2[e][h * 1024:(h + 1) * 1024, :]
         for e in range(8) for h in range(2)], axis=0))
    sw1 = g("L1_ffn_sw1")
    ffn1_w = dict(
        wrt=np.ascontiguousarray(g("L1_ffn_router_w")),
        brt=np.ascontiguousarray(g("L1_ffn_router_b"))[None],
        wup=wu, wdn=wd,
        wsu=np.ascontiguousarray(np.concatenate(
            [sw1[:, 0:1024], sw1[:, 1024:2048]], axis=0)),
        wsd=np.ascontiguousarray(g("L1_ffn_sw2")),
        bs1=np.ascontiguousarray(g("L1_ffn_sb1").reshape(2, 1024)),
        bs2=np.ascontiguousarray(g("L1_ffn_sb2"))[None],
        rb1=np.ascontiguousarray(g("L1_ffn_rb1").reshape(16, 1024)),
        rb2=np.ascontiguousarray(g("L1_ffn_rb2")),
        identf=identf, iota=iota)
    state = _ffn_launch(state, CFG_FFN1A, ffn1_w)

    wmaps = []
    for pi in range(2):
        sl = slice(pi * 512, (pi + 1) * 512)
        wmaps.append(dict(
            wq=np.ascontiguousarray(g("L1_attn_b_Wq").T[:, sl]),
            wkv=np.ascontiguousarray(np.hstack([g("L1_attn_b_Wk").T[:, sl],
                                                g("L1_attn_b_Wv").T[:, sl]])),
            wp=np.ascontiguousarray(g("L1_attn_b_Wo").T[sl, :]),
            cmask=cmask, identf=identf))
    parts, _ = _attn_launch(state, CFG_L1B, wmaps)

    state = _ffn_launch(state, CFG_FFN1B, ffn1_w, parts)
    return np.ascontiguousarray(state.reshape(4, T, D))


I32 = mybir.dt.int32


def build_fused():
    p = Prog("fused")
    nc = p.nc
    p.init_barrier_sems()
    p.init_walls({F32: 98_000_000, FP16: 2_200_000,
                  ("internal", BF16): 12_000_000,
                  ("internal", F32): 24_000_000})
    state = nc.dram_tensor("state", [T, D], F32, kind="Internal",
                           addr_space="Shared").ap()
    parts = nc.dram_tensor("parts", [2 * T, D], F32, kind="Internal",
                           addr_space="Shared").ap()
    flags = nc.dram_tensor("flags", [1, 32], I32, kind="Internal",
                           addr_space="Shared").ap()
    xb_in = nc.dram_tensor("xb", [T, D], F32, kind="ExternalInput").ap()
    nonce_in = nc.dram_tensor("nonce", [1, 2], I32, kind="ExternalInput").ap()
    y_final = nc.dram_tensor("y", [2048, D], F32, kind="ExternalOutput").ap()
    partA = parts[0:T, :]
    partB = parts[T:2 * T, :]

    stats_l1a = p.sbuf("statsP", [128, NT_FULL * 6], F32)
    snon = p.sbuf("snon", [1, 2], I32)
    fpoll = p.sbuf("fpoll", [1, 1], I32)
    FL = p.sem("FLG")
    NLD = p.sem("NLD")

    H = {}

    def init_vals(eng):
        pid = eng.partition_id()
        par = eng.compute_val(pid % 2)
        rp = eng.alloc_register("rpar")
        eng.reg_mov(rp, par)
        H["rp"] = rp

    p.sync.dma(snon[:], nonce_in, NLD)
    p.sync.wait(NLD, NLD.v)
    p.q["sync"].ops.append(("__raw__", [init_vals], {}, None, 0))

    def pair_dma(dstA, dstB, srcA, srcB, S):
        def go(eng, dA=dstA, dB=dstB, sA=srcA, sB=srcB, sh=S.h):
            with eng.If_eq(H["rp"], 0):
                eng.dma_start(dA, sA).then_inc(sh, 16)
            with eng.Else():
                eng.dma_start(dB, sB).then_inc(sh, 16)
        p.q["sync"].ops.append(("__raw__", [go], {}, None, 0))
        S.v += 16
        return S.v

    def publish(stage):
        def go(eng, se=2 * stage, so=2 * stage + 1, sh=FL.h):
            with eng.If_eq(H["rp"], 0):
                eng.dma_start(flags[0:1, se:se + 1], snon[:, 0:1]).then_inc(sh, 16)
            with eng.Else():
                eng.dma_start(flags[0:1, so:so + 1], snon[:, 0:1]).then_inc(sh, 16)
        p.q["sync"].ops.append(("__raw__", [go], {}, None, 0))
        FL.v += 16
        p.sync.wait(FL, FL.v)

    def poll(stage):
        PL = p.sem(f"PL{stage}")
        slot_e = 2 * stage + 1
        slot_o = 2 * stage

        def go(eng):
            with eng.register(f"pv{stage}") as pv, \
                 eng.register(f"ex{stage}") as ex, \
                 eng.register(f"rn{stage}") as rn:
                nv = eng.value_load(snon[:, 0:1])
                eng.reg_mov(rn, nv)
                eng.reg_mov(pv, 0)
                eng.reg_mov(ex, 16)

                def cond():
                    return (eng.snap(pv, min_val=-(1 << 31),
                                     max_val=(1 << 31) - 1)
                            != eng.snap(rn, min_val=-(1 << 31),
                                        max_val=(1 << 31) - 1))
                with eng.While(cond):
                    with eng.If_eq(H["rp"], 0):
                        eng.dma_start(fpoll[:, 0:1],
                                      flags[0:1, slot_e:slot_e + 1]).then_inc(PL.h, 16)
                    with eng.Else():
                        eng.dma_start(fpoll[:, 0:1],
                                      flags[0:1, slot_o:slot_o + 1]).then_inc(PL.h, 16)
                    eng.wait_ge(PL.h, ex)
                    eng.reg_add(ex, ex, 16)
                    v = eng.value_load(fpoll[:, 0:1])
                    eng.reg_mov(pv, v)
        p.q["sync"].ops.append(("__raw__", [go], {}, None, 0))

    def attn_io(xb_ap, stats_buf=None):
        def ywrite(gt, ystage, STR):
            pair_dma(partA[gt * 128:(gt + 1) * 128, :],
                     partB[gt * 128:(gt + 1) * 128, :],
                     ystage[:], ystage[:], STR)
        io = {"xb": xb_ap, "ywrite": ywrite}
        if stats_buf is not None:
            io["stats_buf"] = stats_buf
        return io

    def ffn_io(x_src, final=False):
        def load_xo(dst, t, LD):
            return pair_dma(dst, dst,
                            x_src[t * 128:(t + 1) * 128, :],
                            x_src[2048 + t * 128:2048 + (t + 1) * 128, :], LD)

        def load_part(dst, which, t, LD):
            src = partA if which == 0 else partB
            return pair_dma(dst, dst,
                            src[t * 128:(t + 1) * 128, :],
                            src[2048 + t * 128:2048 + (t + 1) * 128, :], LD)

        def ywrite(t, ystage, STR):
            if final:
                p.sync.dma(y_final[t * 128:(t + 1) * 128, :], ystage[:], STR)
            else:
                pair_dma(state[t * 128:(t + 1) * 128, :],
                         state[2048 + t * 128:2048 + (t + 1) * 128, :],
                         ystage[:], ystage[:], STR)
        return {"load_xo": load_xo, "load_part": load_part, "ywrite": ywrite}

    def scan_io():
        def ywrite(gt, ystage, STR):
            def go(eng, gt=gt, ap=ystage[:], sh=STR.h):
                with eng.If_eq(H["rp"], 0 if gt < 16 else 1):
                    eng.dma_start(state[gt * 128:(gt + 1) * 128, :],
                                  ap).then_inc(sh, 16)
                with eng.Else():
                    eng.dma_start(fpoll[:, 0:1], flags[0:1, 30:31]).then_inc(sh, 16)
            p.q["sync"].ops.append(("__raw__", [go], {}, None, 0))
            STR.v += 16
        return {"xb": state, "pA": partA, "pB": partB,
                "stats_buf": stats_l1a, "ywrite": ywrite}

    p.stage_begin("l0a")
    build_attention(CFG_L0A, p=p, io=attn_io(xb_in))
    publish(1)
    p.stage_end()

    p.stage_begin("f0a")
    p.alias_group = "ffn0"
    poll(1)
    build_ffn(CFG_FFN0A, p=p, io=ffn_io(xb_in))
    publish(2)
    p.stage_end()

    p.stage_begin("l0b")
    poll(2)
    build_attention(CFG_L0B, p=p, io=attn_io(state))
    publish(3)
    p.stage_end()

    p.stage_begin("f0b")
    p.alias_group = "ffn0"
    poll(3)
    build_ffn(CFG_FFN0B, p=p, io=ffn_io(state))
    publish(4)
    p.stage_end()

    p.stage_begin("l1a")
    poll(4)
    build_attention(CFG_L1A, p=p, io=attn_io(state, stats_buf=stats_l1a))
    publish(5)
    p.stage_end()

    p.stage_begin("scn")
    poll(5)
    build_scan(dict(name="scan"), p=p, io=scan_io())
    p.stage_end()

    p.stage_begin("f1a")
    p.alias_group = "ffn1"
    build_ffn(CFG_FFN1A, p=p, io=ffn_io(state))
    publish(7)
    p.stage_end()

    p.stage_begin("l1b")
    poll(7)
    build_attention(CFG_L1B, p=p, io=attn_io(state))
    publish(8)
    p.stage_end()

    p.stage_begin("f1b")
    p.alias_group = "ffn1"
    poll(8)
    build_ffn(CFG_FFN1B, p=p, io=ffn_io(state, final=True))
    p.stage_end()
    return p


def _fused_values(inputs):
    g = lambda k: np.asarray(inputs[k])
    identf = _identf()
    cmask = _cmask()
    iota = _iota8()
    rcos, rsin = _rope_tables()
    V = {}

    def add(group, name, v0, v1=None):
        V[(group, name)] = [v0, v0 if v1 is None else v1]

    Wqkv = g("L0_attn_a_W_qkv")
    bqkv = g("L0_attn_a_b_qkv")
    sls = [slice(0, 512), slice(512, 1024)]

    def two(fn):
        return [fn(sls[0]), fn(sls[1])]

    a, b = two(lambda sl: np.ascontiguousarray(Wqkv[:, 0:1024][:, sl]))
    add("l0a_", "wq", a, b)
    a, b = two(lambda sl: np.ascontiguousarray(
        np.hstack([Wqkv[:, 1024:2048][:, sl], Wqkv[:, 2048:3072][:, sl]])))
    add("l0a_", "wkv", a, b)
    a, b = two(lambda sl: np.ascontiguousarray(g("L0_attn_a_W_proj")[sl, :]))
    add("l0a_", "wp", a, b)
    a, b = two(lambda sl: np.ascontiguousarray(bqkv[0:1024][sl])[None])
    add("l0a_", "bq", a, b)
    a, b = two(lambda sl: np.ascontiguousarray(
        np.hstack([bqkv[1024:2048][sl], bqkv[2048:3072][sl]]))[None])
    add("l0a_", "bkv", a, b)
    add("l0a_", "gvec", _colvec(g("L0_attn_a_ln_gamma")))
    add("l0a_", "bvec", _colvec(g("L0_attn_a_ln_beta")))
    add("l0a_", "identf", identf)

    a, b = two(lambda sl: np.ascontiguousarray(g("L0_attn_b_Wq")[:, sl]))
    add("l0b_", "wq", a, b)
    a, b = two(lambda sl: np.ascontiguousarray(
        np.hstack([g("L0_attn_b_Wk")[:, sl], g("L0_attn_b_Wv")[:, sl]])))
    add("l0b_", "wkv", a, b)
    a, b = two(lambda sl: np.ascontiguousarray(g("L0_attn_b_Wo")[sl, :]))
    add("l0b_", "wp", a, b)
    a, b = two(lambda sl: np.ascontiguousarray(
        g("L0_attn_b_bq").astype(np.float32)[sl])[None])
    add("l0b_", "bq", a, b)
    a, b = two(lambda sl: np.ascontiguousarray(np.hstack(
        [g("L0_attn_b_bk").astype(np.float32)[sl],
         g("L0_attn_b_bv").astype(np.float32)[sl]]))[None])
    add("l0b_", "bkv", a, b)
    add("l0b_", "rcos", rcos)
    add("l0b_", "rsin", rsin)
    add("l0b_", "cmask", cmask)
    add("l0b_", "identf", identf)

    qkvT = np.ascontiguousarray(g("L1_attn_a_qkv_w").T)
    a, b = two(lambda sl: np.ascontiguousarray(qkvT[:, 0:1024][:, sl]))
    add("l1a_", "wq", a, b)
    a, b = two(lambda sl: np.ascontiguousarray(
        np.hstack([qkvT[:, 1024:2048][:, sl], qkvT[:, 2048:3072][:, sl]])))
    add("l1a_", "wkv", a, b)
    a, b = two(lambda sl: np.ascontiguousarray(g("L1_attn_a_o_w").T[sl, :]))
    add("l1a_", "wp", a, b)
    add("l1a_", "gvec", _colvec(g("L1_attn_a_norm1_w")))
    add("l1a_", "cmask", cmask)
    add("l1a_", "identf", identf)

    a, b = two(lambda sl: np.ascontiguousarray(g("L1_attn_b_Wq").T[:, sl]))
    add("l1b_", "wq", a, b)
    a, b = two(lambda sl: np.ascontiguousarray(
        np.hstack([g("L1_attn_b_Wk").T[:, sl], g("L1_attn_b_Wv").T[:, sl]])))
    add("l1b_", "wkv", a, b)
    a, b = two(lambda sl: np.ascontiguousarray(g("L1_attn_b_Wo").T[sl, :]))
    add("l1b_", "wp", a, b)
    add("l1b_", "cmask", cmask)
    add("l1b_", "identf", identf)

    wupT = g("L0_ffn_w_up").transpose(0, 2, 1)
    wdnT = g("L0_ffn_w_down").transpose(0, 2, 1)
    wu = np.ascontiguousarray(np.concatenate(
        [np.concatenate([wupT[e][:, h * 1024:(h + 1) * 1024],
                         wupT[e][:, 2048 + h * 1024:2048 + (h + 1) * 1024]],
                        axis=1)
         for e in range(8) for h in range(2)], axis=0))
    wd = np.ascontiguousarray(np.concatenate(
        [wdnT[e][h * 1024:(h + 1) * 1024, :]
         for e in range(8) for h in range(2)], axis=0))
    add("ffn0", "wrt", np.ascontiguousarray(g("L0_ffn_w_router").T))
    add("ffn0", "wup", wu)
    add("ffn0", "wdn", wd)
    add("ffn0", "gvec", _colvec(g("L0_ffn_norm_scale")))
    add("ffn0", "identf", identf)
    add("ffn0", "iota", iota)
    add("f0a_", "bcmb", np.ascontiguousarray(g("L0_attn_a_b_proj"))[None])
    add("f0b_", "bcmb", np.ascontiguousarray(
        g("L0_attn_b_bo").astype(np.float32))[None])

    ec, est, gmat = _scan_consts()
    add("scn_", "wzg", np.ascontiguousarray(
        np.hstack([g("L1_attn_a_in_w").T, g("L1_attn_a_gate_w").T])))
    add("scn_", "wdt", np.ascontiguousarray(
        np.hstack([g("L1_attn_a_dt_w").T, g("L1_attn_a_Bp_w").T,
                   g("L1_attn_a_Cp_w").T])))
    add("scn_", "dtb", np.ascontiguousarray(np.concatenate(
        [g("L1_attn_a_dt_b"), np.zeros(32, np.float32)]))[None])
    add("scn_", "alog", np.ascontiguousarray(
        g("L1_attn_a_A_log").reshape(-1).reshape(8, 128).T))
    add("scn_", "ec", ec)
    add("scn_", "est", est)
    add("scn_", "gmat", gmat)
    add("scn_", "wout", np.ascontiguousarray(g("L1_attn_a_out_w").T))
    add("scn_", "gvec", _colvec(g("L1_attn_a_norm2_w")))
    add("scn_", "identf", identf)

    rw1 = g("L1_ffn_rw1")
    rw2 = g("L1_ffn_rw2")
    wu = np.ascontiguousarray(np.concatenate(
        [rw1[e][:, h * 1024:(h + 1) * 1024]
         for e in range(8) for h in range(2)], axis=0))
    wd = np.ascontiguousarray(np.concatenate(
        [rw2[e][h * 1024:(h + 1) * 1024, :]
         for e in range(8) for h in range(2)], axis=0))
    sw1 = g("L1_ffn_sw1")
    add("ffn1", "wrt", np.ascontiguousarray(g("L1_ffn_router_w")))
    add("ffn1", "brt", np.ascontiguousarray(g("L1_ffn_router_b"))[None])
    add("ffn1", "wup", wu)
    add("ffn1", "wdn", wd)
    add("ffn1", "wsu", np.ascontiguousarray(np.concatenate(
        [sw1[:, 0:1024], sw1[:, 1024:2048]], axis=0)))
    add("ffn1", "wsd", np.ascontiguousarray(g("L1_ffn_sw2")))
    add("ffn1", "bs1", np.ascontiguousarray(g("L1_ffn_sb1").reshape(2, 1024)))
    add("ffn1", "bs2", np.ascontiguousarray(g("L1_ffn_sb2"))[None])
    add("ffn1", "rb1", np.ascontiguousarray(g("L1_ffn_rb1").reshape(16, 1024)))
    add("ffn1", "rb2", np.ascontiguousarray(g("L1_ffn_rb2")))
    add("ffn1", "identf", identf)
    add("ffn1", "iota", iota)
    return V


_FUSED_CACHE = {}


def kernel_fused(**inputs):
    import secrets
    x = np.asarray(inputs["L0_attn_a_x"]).astype(np.float32)
    state0 = np.ascontiguousarray(x.reshape(4, T, D))
    if "fused" not in _FUSED_CACHE:
        pb = build_fused()
        nc = pb.emit_and_finish()
        _FUSED_CACHE["fused"] = (nc, {dt: (w.tensor.name, spec,
                                           int(w.tensor.shape[1]))
                                      for dt, (w, spec, pos) in
                                      pb.walls.items()})
    nc, wallinfo = _FUSED_CACHE["fused"]
    V = _fused_values(inputs)
    np_dt = {F32: np.float32, FP16: np.float16}
    walls = {}
    for dt, (wname, spec, ndecl) in wallinfo.items():
        arrs = [np.zeros((1, ndecl), dtype=np_dt[dt]) for _ in range(2)]
        for (group, name, shape, off) in spec:
            v = V[(group, name)]
            n = int(np.prod(shape))
            for pi in range(2):
                arrs[pi][0, off:off + n] = np.asarray(v[pi], np_dt[dt]).ravel()
        walls[wname] = arrs
    nonce = np.array([[secrets.randbits(31) | 1, 0]], dtype=np.int32)
    in_maps = []
    for c in range(8):
        m = {wname: arrs[c % 2] for wname, arrs in walls.items()}
        m["xb"] = np.ascontiguousarray(state0[c // 2])
        m["nonce"] = nonce
        in_maps.append(m)
    res = run_bass_kernel_spmd(nc, in_maps, core_ids=list(range(8)))
    out = np.empty((4, T, D), dtype=np.float32)
    for c in range(8):
        b, half = c // 2, c % 2
        out[b][half * 2048:(half + 1) * 2048] = res.results[c]["y"]
    return out


FUSED = True


def kernel(**inputs):
    if FUSED:
        return kernel_fused(**inputs)
    return kernel_unfused(**inputs)
